# version 18
# baseline (speedup 1.0000x reference)
import numpy as np
import ml_dtypes
import concourse.bass as bass
import concourse.mybir as mybir
from concourse.bass_utils import run_bass_kernel_spmd

F32 = mybir.dt.float32
BF16 = mybir.dt.bfloat16
AF = mybir.ActivationFunctionType
ALU = mybir.AluOpType
AX = mybir.AxisListType

T = 4096
D = 1024
NC = 8
NST = 8
SW = 512

def host_consts():
    eye = np.eye(128)
    tri = (np.arange(128)[:, None] <= np.arange(128)[None, :]).astype(np.float32)
    blkq = (np.arange(128)[:, None] // 16 == np.arange(8)[None, :]).astype(np.float32)
    blkv = (np.arange(128)[:, None] // 64 == np.arange(2)[None, :]).astype(np.float32)
    blk2 = (np.arange(128)[None, :] // 64 == np.arange(2)[:, None]).astype(np.float32)
    IndE = np.zeros((32, 32 * 128), dtype=np.float32)
    for e in range(32):
        IndE[e, e * 128:(e + 1) * 128] = 1.0
    b = ml_dtypes.bfloat16
    return {
        "identf": eye.astype(np.float32),
        "identb": eye.astype(b),
        "onesb": np.ones((128, 128), dtype=b),
        "onesf": np.ones((1, 128), dtype=np.float32),
        "tri": tri.astype(b),
        "blkq": blkq.astype(b),
        "blkv": blkv.astype(b),
        "blk2": blk2.astype(b),
        "blkqT": np.ascontiguousarray(blkq.T).astype(b),
        "blkvT": np.ascontiguousarray(blkv.T).astype(b),
        "blkP": np.concatenate([np.pad(blkq, ((0, 0), (0, 120))),
                                np.pad(blkv, ((0, 0), (0, 126)))], axis=1).astype(b),
        "blkTP": np.concatenate([np.pad(blkq.T, ((0, 120), (0, 0))),
                                 np.pad(blkv.T, ((0, 126), (0, 0)))], axis=1).astype(b),
        "IndE": IndE.astype(b),
    }

CONST_SPECS = {
    "identf": ([128, 128], F32), "identb": ([128, 128], BF16),
    "onesb": ([128, 128], BF16), "onesf": ([1, 128], F32),
    "tri": ([128, 128], BF16), "blkq": ([128, 8], BF16),
    "blkv": ([128, 2], BF16), "blk2": ([2, 128], BF16),
    "blkqT": ([8, 128], BF16), "blkvT": ([2, 128], BF16),
    "blkP": ([128, 256], BF16), "blkTP": ([128, 256], BF16),
    "IndE": ([32, 4096], BF16),
}

WEIGHT_SPECS = {
    "Wq": ([1024, 256], F32), "Wk": ([1024, 256], F32), "Wv": ([1024, 1024], F32),
    "Wout": ([1024, 1024], F32),
    "WrT": ([1024, 1024], F32), "WkT": ([1024, 1024], F32), "WvT": ([1024, 1024], F32),
    "muR": ([128, 8], F32), "muK": ([128, 8], F32),
    "Aqk": ([1024, 128], F32), "Av": ([1024, 64], F32), "Ao": ([64, 1024], F32),
    "SelT": ([1024, 32], F32),
    "Keys": ([32, 1024, 512], F32), "Values": ([32, 512, 1024], F32),
}

STREAM_SPECS = {
    "s0": ([1024, 4096], F32), "s1": ([1024, 4096], F32), "t0": ([1024, 4096], F32),
    "qs": ([256, 4096], BF16), "ks": ([256, 4096], BF16), "vs": ([1024, 4096], BF16),
    "ssqT": ([16, 4096], F32), "u0": ([4096, 1024], F32), "fbT": ([2, 32768], BF16),
}

def host_weights(inputs):
    c = np.ascontiguousarray
    Wkv = inputs["L0_attn_Wkv"]
    Wk = c(Wkv.reshape(1024, 16, 80)[:, :, :16].reshape(1024, 256))
    Wv = c(Wkv.reshape(1024, 16, 80)[:, :, 16:].reshape(1024, 1024))
    return {
        "Wq": c(inputs["L0_attn_Wq"]), "Wk": Wk, "Wv": Wv,
        "Wout": c(inputs["L0_attn_Wout"]),
        "WrT": c(inputs["L0_ffn_W_r"].T), "WkT": c(inputs["L0_ffn_W_k"].T),
        "WvT": c(inputs["L0_ffn_W_v"].T),
        "muR": c(inputs["L0_ffn_mu_r"].reshape(8, 128).T),
        "muK": c(inputs["L0_ffn_mu_k"].reshape(8, 128).T),
        "Aqk": c(np.concatenate([inputs["L1_attn_W_q"], inputs["L1_attn_W_k"]], axis=1)),
        "Av": c(inputs["L1_attn_W_v"]), "Ao": c(inputs["L1_attn_W_o"]),
        "SelT": c(inputs["L1_ffn_expert_sel"].T),
        "Keys": c(inputs["L1_ffn_keys"]), "Values": c(inputs["L1_ffn_values"]),
    }



class Env:
    def __init__(self, nc, stack, dram):
        self.nc = nc
        self.stack = stack
        self.d = dram
        self._n = 0
        self._cur = []

    def sem(self, name, kind="eng"):
        self._n += 1
        h = self.nc.alloc_semaphore(name=f"{name}{self._n}")
        self._cur.append(h)
        return h

    def end_phase(self):
        if not self._cur:
            return
        self.nc.clear_and_free_semaphores(self._cur)
        self.nc.all_engine_barrier()
        self._cur = []

    def sbuf(self, name, shape, dt):
        self._n += 1
        return self.nc.sbuf_tensor(f"{name}_u{self._n}", shape, dt)

    def psum(self, name, shape, dt=F32):
        self._n += 1
        return self.nc.psum_tensor(f"{name}_u{self._n}", shape, dt)


def _dsem(E, name, n=2, kind="hw"):
    return [E.sem(f"{name}_{i}", kind) for i in range(n)]


def _transpose_pass(E, src, dst, tm_to_fm):
    nc = E.nc
    if tm_to_fm:
        tm = E.d[src]
        fmr = E.d[dst].rearrange("(c p) t -> p c t", p=128)
    else:
        fmr = E.d[src].rearrange("(c p) t -> p c t", p=128)
        tm = E.d[dst]
    lc = E.sem("tlc", "hw")
    ldp, stp = _dsem(E, "tld"), _dsem(E, "tst")
    mmq, evq = E.sem("tmm"), E.sem("tev")
    with (
        E.sbuf("t_xin", [128, 2 * 1024], F32) as xin,
        E.sbuf("t_idf", [128, 128], F32) as idf,
        E.sbuf("t_g", [128, 2 * 1024], F32) as sg,
        E.psum("t_pt0", [128, 128], F32) as pt0,
        E.psum("t_pt1", [128, 128], F32) as pt1,
        nc.Block() as blk,
    ):
        def in_slice(tt):
            p = (tt % 2) * 1024
            if tm_to_fm:
                return xin[:, p:p + 1024], tm[tt * 128:(tt + 1) * 128, :]
            return (xin[:, p:p + 1024].rearrange("p (c t) -> p c t", t=128),
                    fmr[:, :, tt * 128:(tt + 1) * 128])

        def out_slice(tt):
            p = (tt % 2) * 1024
            if tm_to_fm:
                return (fmr[:, :, tt * 128:(tt + 1) * 128],
                        sg[:, p:p + 1024].rearrange("p (c t) -> p c t", t=128))
            return tm[tt * 128:(tt + 1) * 128, :], sg[:, p:p + 1024]

        @blk.sync
        def _(sy):
            sy.dma_start(idf[:, :], E.d["identf"][:, :]).then_inc(lc, 16)
            for tt in range(32):
                if tt >= 2:
                    sy.wait_ge(mmq, 8 * (tt - 1))
                d, sr = in_slice(tt)
                sy.dma_start(d, sr).then_inc(ldp[tt % 2], 16)
                if tt >= 1:
                    sy.wait_ge(evq, 8 * tt)
                    d, sr = out_slice(tt - 1)
                    sy.dma_start(d, sr).then_inc(stp[(tt - 1) % 2], 16)
            sy.wait_ge(evq, 8 * 32)
            d, sr = out_slice(31)
            sy.dma_start(d, sr).then_inc(stp[31 % 2], 16)
            sy.wait_ge(stp[0], 16 * 16)
            sy.wait_ge(stp[1], 16 * 16)

        @blk.tensor
        def _(t):
            t.wait_ge(lc, 16)
            for tt in range(32):
                t.wait_ge(ldp[tt % 2], 16 * (tt // 2 + 1))
                for c in range(8):
                    k = tt * 8 + c
                    if k >= 2:
                        t.wait_ge(evq, k - 1)
                    t.transpose((pt0 if k % 2 == 0 else pt1)[:, :],
                                xin[:, (tt % 2) * 1024 + c * 128:(tt % 2) * 1024 + (c + 1) * 128],
                                idf[:, :]).then_inc(mmq, 1)

        @blk.scalar
        def _(s):
            for tt in range(32):
                for c in range(8):
                    k = tt * 8 + c
                    s.wait_ge(mmq, k + 1)
                    if c == 0 and tt >= 2:
                        s.wait_ge(stp[tt % 2], 16 * (tt // 2))
                    s.copy(sg[:, (tt % 2) * 1024 + c * 128:(tt % 2) * 1024 + (c + 1) * 128],
                           (pt0 if k % 2 == 0 else pt1)[:, :]).then_inc(evq, 1)
    return


def emit_entry(E, src="x0", dst="s0"):
    _transpose_pass(E, src, dst, tm_to_fm=True)


def emit_exit(E, src, dst="out"):
    _transpose_pass(E, src, dst, tm_to_fm=False)


def emit_ln(E, src, dst, residual=None):
    nc = E.nc
    N = SW
    srcr = E.d[src].rearrange("(c p) t -> p c t", p=128)
    resr = E.d[residual].rearrange("(c p) t -> p c t", p=128) if residual else None
    dstr = E.d[dst].rearrange("(c p) t -> p c t", p=128)
    lcp = _dsem(E, "lc")
    lbp, lfp, lrp, stp = _dsem(E, "lb", kind="sw"), _dsem(E, "lf"), _dsem(E, "lr"), _dsem(E, "ls")
    smm, sev, svv = E.sem("lm"), E.sem("le"), E.sem("lv")
    V_PER = (16 if residual else 0) + 3 + 2 + 16
    S_PER = 8 + 2 + 2
    R16 = 16 if residual else 0
    with (
        E.sbuf("ln_xs", [128, 2 * 4096], F32) as xs,
        (E.sbuf("ln_rs", [128, 2 * 4096], F32) if residual else E.sbuf("ln_rs", [1, 1], F32)) as rs,
        E.sbuf("ln_xb", [128, 2 * 4096], BF16) as xb,
        E.sbuf("ln_sq", [128, 4096], BF16) as sqb,
        E.sbuf("ln_onb", [128, 128], BF16) as onb,
        E.sbuf("ln_onf", [1, 128], F32) as onf,
        E.sbuf("ln_rows", [1, 6 * 512], F32) as rows,
        E.sbuf("ln_od", [128, 2 * 4096], F32) as od,
        E.psum("ln_pa", [1, 512], F32) as psA,
        E.psum("ln_pb", [1, 512], F32) as psB,
        E.psum("ln_pbc", [128, 2 * 512], F32) as psBC,
        nc.Block() as blk,
    ):
        nst = NST

        @blk.gpsimd
        def _(g):
            if not residual:
                for s in range(nst):
                    if s >= 2:
                        g.wait_ge(sev, S_PER * (s - 1))
                    g.dma_start(xb[:, (s % 2) * 4096:(s % 2 + 1) * 4096].rearrange("p (c t) -> p c t", t=N),
                                srcr[:, :, s * N:(s + 1) * N]).then_inc(lbp[s % 2], 16)

        @blk.sync
        def _(sy):
            sy.dma_start(onb[:, :], E.d["onesb"][:, :]).then_inc(lcp[0], 16)
            sy.dma_start(onf[:, :], E.d["onesf"][:, :]).then_inc(lcp[1], 16)
            for s in range(nst):
                if s >= 2:
                    sy.wait_ge(svv, V_PER * (s - 1))
                sy.dma_start(xs[:, (s % 2) * 4096:(s % 2 + 1) * 4096].rearrange("p (c t) -> p c t", t=N),
                             srcr[:, :, s * N:(s + 1) * N]).then_inc(lfp[s % 2], 16)
                if residual:
                    sy.dma_start(rs[:, (s % 2) * 4096:(s % 2 + 1) * 4096].rearrange("p (c t) -> p c t", t=N),
                                 resr[:, :, s * N:(s + 1) * N]).then_inc(lrp[s % 2], 16)
                if s >= 1:
                    sy.wait_ge(svv, V_PER * s)
                    sy.dma_start(dstr[:, :, (s - 1) * N:s * N],
                                 od[:, ((s - 1) % 2) * 4096:((s - 1) % 2 + 1) * 4096].rearrange(
                                     "p (c t) -> p c t", t=N)).then_inc(stp[(s - 1) % 2], 16)
            sy.wait_ge(svv, V_PER * nst)
            sy.dma_start(dstr[:, :, (nst - 1) * N:nst * N],
                         od[:, ((nst - 1) % 2) * 4096:((nst - 1) % 2 + 1) * 4096].rearrange(
                             "p (c t) -> p c t", t=N)).then_inc(stp[(nst - 1) % 2], 16)
            sy.wait_ge(stp[0], 16 * (nst // 2))
            sy.wait_ge(stp[1], 16 * (nst // 2))

        @blk.vector
        def _(v):
            for s in range(nst):
                base = (s % 2) * 4096
                myv = V_PER * s
                v.wait_ge(lfp[s % 2], 16 * (s // 2 + 1))
                if residual:
                    v.wait_ge(lrp[s % 2], 16 * (s // 2 + 1))
                    if s >= 2:
                        v.wait_ge(smm, 4 * (s - 1))
                    for c in range(8):
                        v.tensor_tensor(xs[:, base + c * N:base + (c + 1) * N],
                                        xs[:, base + c * N:base + (c + 1) * N],
                                        rs[:, base + c * N:base + (c + 1) * N], ALU.add).then_inc(svv, 1)
                        myv += 1
                    v.wait_ge(svv, myv)
                    for c in range(8):
                        v.tensor_copy(xb[:, base + c * N:base + (c + 1) * N],
                                      xs[:, base + c * N:base + (c + 1) * N]).then_inc(svv, 1)
                        myv += 1
                v.wait_ge(sev, S_PER * s + 8 + 2)
                v.tensor_tensor(rows[:, 2 * 512:3 * 512], rows[:, 0:512], rows[:, 0:512],
                                ALU.mult).then_inc(svv, 1)
                myv += 1
                v.wait_ge(svv, myv)
                v.tensor_tensor(rows[:, 3 * 512:4 * 512], rows[:, 512:1024], rows[:, 2 * 512:3 * 512],
                                ALU.subtract).then_inc(svv, 1)
                myv += 1
                v.wait_ge(svv, myv)
                v.tensor_scalar(rows[:, 3 * 512:4 * 512], rows[:, 3 * 512:4 * 512], 1e-5, None,
                                ALU.add).then_inc(svv, 1)
                myv += 1
                v.wait_ge(sev, S_PER * s + 8 + 2 + 2)
                v.tensor_tensor(rows[:, 5 * 512:6 * 512], rows[:, 0:512], rows[:, 4 * 512:5 * 512],
                                ALU.mult).then_inc(svv, 1)
                myv += 1
                v.wait_ge(svv, myv)
                v.tensor_scalar(rows[:, 5 * 512:6 * 512], rows[:, 5 * 512:6 * 512], -1.0, None,
                                ALU.mult).then_inc(svv, 1)
                myv += 1
                v.wait_ge(smm, 4 * s + 4)
                if s >= 2:
                    v.wait_ge(stp[s % 2], 16 * (s // 2))
                for c in range(8):
                    v.tensor_tensor(od[:, base + c * N:base + (c + 1) * N],
                                    xs[:, base + c * N:base + (c + 1) * N],
                                    psBC[:, 0:512], ALU.mult).then_inc(svv, 1)
                    myv += 1
                v.wait_ge(svv, myv)
                for c in range(8):
                    v.tensor_tensor(od[:, base + c * N:base + (c + 1) * N],
                                    od[:, base + c * N:base + (c + 1) * N],
                                    psBC[:, 512:1024], ALU.add).then_inc(svv, 1)
                    myv += 1
                assert myv == V_PER * (s + 1), (myv, s)

        @blk.scalar
        def _(s_):
            for s in range(nst):
                base = (s % 2) * 4096
                mys = S_PER * s
                if residual:
                    s_.wait_ge(svv, V_PER * s + 16)
                else:
                    s_.wait_ge(lbp[s % 2], 16 * (s // 2 + 1))
                if s >= 1:
                    s_.wait_ge(smm, 4 * (s - 1) + 2)
                for c in range(8):
                    s_.activation(sqb[:, c * N:(c + 1) * N], xb[:, base + c * N:base + (c + 1) * N],
                                  AF.Square).then_inc(sev, 1)
                    mys += 1
                s_.wait_ge(smm, 4 * s + 2)
                if s >= 1:
                    s_.wait_ge(svv, V_PER * (s - 1) + R16 + 4)
                s_.mul(rows[:, 0:512], psA[:, :], 1.0 / 1024.0).then_inc(sev, 1)
                s_.mul(rows[:, 512:1024], psB[:, :], 1.0 / 1024.0).then_inc(sev, 1)
                mys += 2
                s_.wait_ge(svv, V_PER * s + R16 + 3)
                s_.activation(rows[:, 4 * 512:5 * 512], rows[:, 3 * 512:4 * 512], AF.Ln).then_inc(sev, 1)
                mys += 1
                s_.wait_ge(sev, mys)
                s_.activation(rows[:, 4 * 512:5 * 512], rows[:, 4 * 512:5 * 512], AF.Exp,
                              scale=-0.5).then_inc(sev, 1)
                mys += 1
                assert mys == S_PER * (s + 1)

        @blk.tensor
        def _(t):
            t.wait_ge(lcp[0], 16)
            t.wait_ge(lcp[1], 16)
            for s in range(nst):
                base = (s % 2) * 4096
                if residual:
                    t.wait_ge(svv, V_PER * s + 16)
                else:
                    t.wait_ge(lbp[s % 2], 16 * (s // 2 + 1))
                if s >= 1:
                    t.wait_ge(sev, S_PER * (s - 1) + 10)
                for c in range(8):
                    mm = t.matmul(psA[:, :], onb[:, 0:1], xb[:, base + c * N:base + (c + 1) * N],
                                  start=(c == 0), stop=(c == 7))
                    if c == 7:
                        mm.then_inc(smm, 1)
                t.wait_ge(sev, S_PER * s + 8)
                for c in range(8):
                    mm = t.matmul(psB[:, :], onb[:, 0:1], sqb[:, c * N:(c + 1) * N],
                                  start=(c == 0), stop=(c == 7))
                    if c == 7:
                        mm.then_inc(smm, 1)
                t.wait_ge(svv, V_PER * s + R16 + 5)
                t.matmul(psBC[:, 0:512], onf[:, :], rows[:, 4 * 512:5 * 512], start=True,
                         stop=True).then_inc(smm, 1)
                t.matmul(psBC[:, 512:1024], onf[:, :], rows[:, 5 * 512:6 * 512], start=True,
                         stop=True).then_inc(smm, 1)
    return


ALL_SPECS = {}
ALL_SPECS.update(CONST_SPECS)
ALL_SPECS.update(WEIGHT_SPECS)
ALL_SPECS.update(STREAM_SPECS)
ALL_SPECS["x0"] = ([4096, 1024], F32)
ALL_SPECS["out"] = ([4096, 1024], F32)


class LazyDram:
    def __init__(self, nc, ext_in, ext_out):
        self.nc = nc
        self.ext_in = set(ext_in)
        self.ext_out = set(ext_out)
        self.aps = {}
        self.used_in = set()
        self.used_out = set()

    def __getitem__(self, name):
        if name not in self.aps:
            shape, dt = ALL_SPECS[name]
            if name in self.ext_in:
                kind = "ExternalInput"
                self.used_in.add(name)
            elif name in self.ext_out:
                kind = "ExternalOutput"
                self.used_out.add(name)
            else:
                kind = "Internal"
            self.aps[name] = self.nc.dram_tensor(name, shape, dt, kind=kind).ap()
        return self.aps[name]


def build_program(phases, ext_in, ext_out):
    from contextlib import ExitStack
    nc = bass.Bass("TRN2", target_bir_lowering=False)
    dram = LazyDram(nc, ext_in, ext_out)
    with ExitStack() as stack:
        E = Env(nc, stack, dram)
        import os
        recycle = os.environ.get("MK_NO_RECYCLE", "") != "1"
        for i, ph in enumerate(phases):
            ph(E)
            if recycle and i + 1 < len(phases):
                E.end_phase()
    return nc, dram.used_in, dram.used_out

def emit_B(E, src="s1", dst="t0"):
    nc = E.nc
    N = SW
    srcr = E.d[src].rearrange("(c p) t -> p c t", p=128)
    dstr = E.d[dst].rearrange("(c p) t -> p c t", p=128)
    lw = _dsem(E, "bw", 3, kind="sw") + _dsem(E, "bwh", 2)
    lx = _dsem(E, "bx")
    stp = _dsem(E, "bs")
    smm, sev, svv = E.sem("bm"), E.sem("be"), E.sem("bv")
    msz = E.sem("bz")
    V_PER = 24 + 8 + 16
    S_PER = 16
    M_PER = 24
    with (
        E.sbuf("b_wr", [128, 8 * 1024], BF16) as WrB,
        E.sbuf("b_wk", [128, 8 * 1024], BF16) as WkB,
        E.sbuf("b_wv", [128, 8 * 1024], BF16) as WvB,
        E.sbuf("b_mu", [128, 16], F32) as muB,
        E.sbuf("b_xs", [128, 2 * 8 * 513], F32) as xs,
        E.sbuf("b_d", [128, 4096], F32) as db,
        E.sbuf("b_xr", [128, 2 * 4096], BF16) as xrb,
        E.sbuf("b_xk", [128, 2 * 4096], BF16) as xkb,
        E.sbuf("b_kb", [128, 4096], BF16) as kb,
        E.sbuf("b_k2", [128, 4096], BF16) as k2b,
        E.sbuf("b_rb", [128, 4096], BF16) as rb,
        E.sbuf("b_od", [128, 2 * 4096], F32) as od,
        E.psum("b_pk0", [128, 512], F32) as pk0,
        E.psum("b_pk1", [128, 512], F32) as pk1,
        E.psum("b_pr0", [128, 512], F32) as pr0,
        E.psum("b_pr1", [128, 512], F32) as pr1,
        E.psum("b_pv0", [128, 512], F32) as pv0,
        E.psum("b_pv1", [128, 512], F32) as pv1,
        nc.Block() as blk,
    ):
        nst = NST
        PK = [pk0, pk1]; PR = [pr0, pr1]; PV = [pv0, pv1]

        @blk.gpsimd
        def _(g):
            g.dma_start(WrB[:, :].rearrange("p (c n) -> p c n", n=1024),
                        E.d["WrT"].rearrange("(c p) n -> p c n", p=128)).then_inc(lw[0], 16)
            g.dma_start(WkB[:, :].rearrange("p (c n) -> p c n", n=1024),
                        E.d["WkT"].rearrange("(c p) n -> p c n", p=128)).then_inc(lw[1], 16)
            g.dma_start(WvB[:, :].rearrange("p (c n) -> p c n", n=1024),
                        E.d["WvT"].rearrange("(c p) n -> p c n", p=128)).then_inc(lw[2], 16)

        @blk.sync
        def _(sy):
            sy.dma_start(muB[:, 0:8], E.d["muR"][:, :]).then_inc(lw[3], 16)
            sy.dma_start(muB[:, 8:16], E.d["muK"][:, :]).then_inc(lw[4], 16)
            for s in range(nst):
                base = (s % 2) * (8 * 513)
                if s >= 2:
                    sy.wait_ge(svv, V_PER * (s - 1))
                if s == 0:
                    sy.dma_start(
                        xs[:, 0:8 * 513].rearrange("p (c t) -> p c t", t=513)[:, :, 1:513],
                        srcr[:, :, 0:512]).then_inc(lx[0], 16)
                else:
                    sy.dma_start(
                        xs[:, base:base + 8 * 513].rearrange("p (c t) -> p c t", t=513),
                        srcr[:, :, s * N - 1:(s + 1) * N]).then_inc(lx[s % 2], 16)
                if s >= 1:
                    sy.wait_ge(svv, V_PER * s)
                    sy.dma_start(dstr[:, :, (s - 1) * N:s * N],
                                 od[:, ((s - 1) % 2) * 4096:((s - 1) % 2 + 1) * 4096].rearrange(
                                     "p (c t) -> p c t", t=N)).then_inc(stp[(s - 1) % 2], 16)
            sy.wait_ge(svv, V_PER * nst)
            sy.dma_start(dstr[:, :, (nst - 1) * N:nst * N],
                         od[:, ((nst - 1) % 2) * 4096:((nst - 1) % 2 + 1) * 4096].rearrange(
                             "p (c t) -> p c t", t=N)).then_inc(stp[(nst - 1) % 2], 16)
            sy.wait_ge(stp[0], 16 * (nst // 2))
            sy.wait_ge(stp[1], 16 * (nst // 2))

        @blk.vector
        def _(v):
            v.wait_ge(lw[3], 16)
            v.wait_ge(lw[4], 16)
            for s in range(nst):
                bx = (s % 2) * (8 * 513)
                ba = (s % 2) * 4096
                myv = V_PER * s
                v.wait_ge(lx[s % 2], 16 * (s // 2 + 1))
                if s == 0:
                    v.memset(xs[:, 0:8 * 513].rearrange("p (c t) -> p c t", t=513)[:, :, 0:1],
                             0.0).then_inc(msz, 1)
                    v.wait_ge(msz, 1)
                if s >= 2:
                    v.wait_ge(smm, M_PER * (s - 1))
                for c in range(8):
                    v.tensor_tensor(db[:, c * N:(c + 1) * N],
                                    xs[:, bx + c * 513 + 1:bx + c * 513 + 513],
                                    xs[:, bx + c * 513:bx + c * 513 + 512], ALU.subtract).then_inc(svv, 1)
                    myv += 1
                v.wait_ge(svv, myv)
                for c in range(8):
                    v.scalar_tensor_tensor(xrb[:, ba + c * N:ba + (c + 1) * N],
                                           db[:, c * N:(c + 1) * N], muB[:, c:c + 1],
                                           xs[:, bx + c * 513:bx + c * 513 + 512],
                                           ALU.mult, ALU.add).then_inc(svv, 1)
                    myv += 1
                for c in range(8):
                    v.scalar_tensor_tensor(xkb[:, ba + c * N:ba + (c + 1) * N],
                                           db[:, c * N:(c + 1) * N], muB[:, 8 + c:9 + c],
                                           xs[:, bx + c * 513:bx + c * 513 + 512],
                                           ALU.mult, ALU.add).then_inc(svv, 1)
                    myv += 1
                for m in range(8):
                    v.wait_ge(sev, S_PER * s + m + 1)
                    v.tensor_tensor(k2b[:, m * N:(m + 1) * N], kb[:, m * N:(m + 1) * N],
                                    kb[:, m * N:(m + 1) * N], ALU.mult).then_inc(svv, 1)
                    myv += 1
                if s >= 2:
                    v.wait_ge(stp[s % 2], 16 * (s // 2))
                for m in range(8):
                    v.wait_ge(smm, M_PER * s + 16 + m + 1)
                    v.wait_ge(sev, S_PER * s + 8 + m + 1)
                    v.tensor_tensor(od[:, ba + m * N:ba + (m + 1) * N],
                                    PV[m % 2][:, :], rb[:, m * N:(m + 1) * N], ALU.mult).then_inc(svv, 1)
                    myv += 1
                    v.wait_ge(svv, myv)
                    v.tensor_tensor(od[:, ba + m * N:ba + (m + 1) * N],
                                    od[:, ba + m * N:ba + (m + 1) * N],
                                    xs[:, bx + m * 513 + 1:bx + m * 513 + 513], ALU.add).then_inc(svv, 1)
                    myv += 1
                assert myv == V_PER * (s + 1)

        @blk.scalar
        def _(s_):
            for s in range(nst):
                mys = S_PER * s
                for m in range(8):
                    s_.wait_ge(smm, M_PER * s + m + 1)
                    s_.activation(kb[:, m * N:(m + 1) * N], PK[m % 2][:, :], AF.Relu).then_inc(sev, 1)
                    mys += 1
                for m in range(8):
                    s_.wait_ge(smm, M_PER * s + 8 + m + 1)
                    s_.activation(rb[:, m * N:(m + 1) * N], PR[m % 2][:, :], AF.Sigmoid).then_inc(sev, 1)
                    mys += 1
                assert mys == S_PER * (s + 1)

        @blk.tensor
        def _(t):
            t.wait_ge(lw[0], 16)
            t.wait_ge(lw[1], 16)
            t.wait_ge(lw[2], 16)
            for s in range(nst):
                ba = (s % 2) * 4096
                t.wait_ge(svv, V_PER * s + 24)
                for m in range(8):
                    if m >= 2:
                        t.wait_ge(sev, S_PER * s + (m - 1))
                    elif s >= 1:
                        t.wait_ge(sev, S_PER * (s - 1) + 8 + 8)
                    for c in range(8):
                        mm = t.matmul(PK[m % 2][:, :],
                                      WkB[:, c * 1024 + m * 128:c * 1024 + (m + 1) * 128],
                                      xkb[:, ba + c * N:ba + (c + 1) * N],
                                      start=(c == 0), stop=(c == 7))
                    mm.then_inc(smm, 1)
                for m in range(8):
                    if m >= 2:
                        t.wait_ge(sev, S_PER * s + 8 + (m - 1))
                    for c in range(8):
                        mm = t.matmul(PR[m % 2][:, :],
                                      WrB[:, c * 1024 + m * 128:c * 1024 + (m + 1) * 128],
                                      xrb[:, ba + c * N:ba + (c + 1) * N],
                                      start=(c == 0), stop=(c == 7))
                    mm.then_inc(smm, 1)
                t.wait_ge(svv, V_PER * s + 32)
                for m in range(8):
                    if m >= 2:
                        t.wait_ge(svv, V_PER * s + 32 + 2 * (m - 1) + 1)
                    for c in range(8):
                        mm = t.matmul(PV[m % 2][:, :],
                                      WvB[:, c * 1024 + m * 128:c * 1024 + (m + 1) * 128],
                                      k2b[:, c * N:(c + 1) * N],
                                      start=(c == 0), stop=(c == 7))
                    mm.then_inc(smm, 1)
    return

def emit_C(E, src="s0", dst="t0"):
    nc = E.nc
    N = SW
    srcr = E.d[src].rearrange("(c p) t -> p c t", p=128)
    dstr = E.d[dst].rearrange("(c p) t -> p c t", p=128)
    lw = _dsem(E, "cw", 3, kind="sw") + _dsem(E, "cwh", 1)
    lbc = _dsem(E, "cb", kind="sw")
    lxs = _dsem(E, "cx")
    stp = _dsem(E, "cs")
    qm, ce1, cv1 = E.sem("cqm"), E.sem("ce1"), E.sem("cv1")
    tmq, tve = E.sem("ctm"), E.sem("ctv")
    cm, ce, cv, mz = E.sem("ccm"), E.sem("cce"), E.sem("ccv"), E.sem("cmz")
    wm = E.sem("cwm")
    nst = NST
    from contextlib import ExitStack
    with ExitStack() as stk:
        AqkB = stk.enter_context(E.sbuf("c_aqk", [128, 8 * 128], BF16))
        AvB = stk.enter_context(E.sbuf("c_av", [128, 8 * 64], BF16))
        AoB = stk.enter_context(E.sbuf("c_ao", [64, 1024], BF16))
        idb = stk.enter_context(E.sbuf("c_idb", [128, 128], BF16))
        trib = stk.enter_context(E.sbuf("c_tri", [128, 128], BF16))
        onb = stk.enter_context(E.sbuf("c_onb", [128, 128], BF16))
        xb = stk.enter_context(E.sbuf("c_xb", [128, 2 * 4096], BF16))
        ph1 = stk.enter_context(E.sbuf("c_ph1", [128, 512], F32))
        ph2 = stk.enter_context(E.sbuf("c_ph2", [128, 512], F32))
        ph3 = stk.enter_context(E.sbuf("c_ph3", [128, 512], F32))
        qb = stk.enter_context(E.sbuf("c_qb", [64, 4096], BF16))
        kb = stk.enter_context(E.sbuf("c_kb", [64, 4096], BF16))
        vb = stk.enter_context(E.sbuf("c_vb", [64, 4096], BF16))
        ktm = stk.enter_context(E.sbuf("c_ktm", [128, 32 * 64], BF16))
        vtm = stk.enter_context(E.sbuf("c_vtm", [128, 32 * 64], BF16))
        STb = stk.enter_context(E.sbuf("c_ST", [64, 64], BF16))
        STf = stk.enter_context(E.sbuf("c_STf", [64, 64], F32))
        zf = stk.enter_context(E.sbuf("c_zf", [64, 1], F32))
        zi = stk.enter_context(E.sbuf("c_zi", [64, 1], F32))
        zb = stk.enter_context(E.sbuf("c_zb", [64, 1], BF16))
        gs = stk.enter_context(E.sbuf("c_gs", [128, 128], BF16))
        gks = stk.enter_context(E.sbuf("c_gks", [128, 128], BF16))
        gm = stk.enter_context(E.sbuf("c_gm", [128, 128], BF16))
        gkm = stk.enter_context(E.sbuf("c_gkm", [128, 128], BF16))
        dr = stk.enter_context(E.sbuf("c_dr", [1, 128], F32))
        dl = stk.enter_context(E.sbuf("c_dl", [1, 128], F32))
        drb = stk.enter_context(E.sbuf("c_drb", [1, 128], BF16))
        ob = stk.enter_context(E.sbuf("c_ob", [64, 128], BF16))
        obuf = stk.enter_context(E.sbuf("c_obuf", [64, 4096], BF16))
        xs = stk.enter_context(E.sbuf("c_xs", [128, 2 * 4096], F32))
        od = stk.enter_context(E.sbuf("c_od", [128, 2 * 4096], F32))
        with (
            E.psum("c_pqk0", [128, 512], F32) as pqk0,
            E.psum("c_pqk1", [128, 512], F32) as pqk1,
            E.psum("c_pv0", [64, 512], F32) as pv0,
            E.psum("c_pv1", [64, 512], F32) as pv1,
            nc.Block() as blk,
        ):
            PQK = [pqk0, pqk1]; PV = [pv0, pv1]

            @blk.gpsimd
            def _(g):
                g.dma_start(AqkB[:, :].rearrange("p (c n) -> p c n", n=128),
                            E.d["Aqk"].rearrange("(c p) n -> p c n", p=128)).then_inc(lw[0], 16)
                g.dma_start(AvB[:, :].rearrange("p (c n) -> p c n", n=64),
                            E.d["Av"].rearrange("(c p) n -> p c n", p=128)).then_inc(lw[1], 16)
                g.dma_start(AoB[:, :], E.d["Ao"][:, :]).then_inc(lw[2], 16)
                for s in range(nst):
                    if s >= 2:
                        g.wait_ge(qm, 2 * (s - 1))
                    g.dma_start(xb[:, (s % 2) * 4096:(s % 2 + 1) * 4096].rearrange("p (c t) -> p c t", t=N),
                                srcr[:, :, s * N:(s + 1) * N]).then_inc(lbc[s % 2], 16)

            @blk.sync
            def _(sy):
                sy.dma_start(idb[:, :], E.d["identb"][:, :]).then_inc(lw[3], 16)
                sy.dma_start(trib[:, :], E.d["tri"][:, :]).then_inc(lw[3], 16)
                sy.dma_start(onb[:, :], E.d["onesb"][:, :]).then_inc(lw[3], 16)

            @blk.tensor
            def _(t):
                t.wait_ge(lw[0], 16)
                t.wait_ge(lw[1], 16)
                for s in range(nst):
                    ba = (s % 2) * 4096
                    t.wait_ge(lbc[s % 2], 16 * (s // 2 + 1))
                    if s >= 2:
                        t.wait_ge(ce1, 4 * (s - 2) + 3)
                    for c in range(8):
                        mm = t.matmul(PQK[s % 2][:, :], AqkB[:, c * 128:(c + 1) * 128],
                                      xb[:, ba + c * N:ba + (c + 1) * N],
                                      start=(c == 0), stop=(c == 7))
                    mm.then_inc(qm, 1)
                    if s >= 2:
                        t.wait_ge(ce1, 4 * (s - 2) + 4)
                    for c in range(8):
                        mm = t.matmul(PV[s % 2][:, :], AvB[:, c * 64:(c + 1) * 64],
                                      xb[:, ba + c * N:ba + (c + 1) * N],
                                      start=(c == 0), stop=(c == 7))
                    mm.then_inc(qm, 1)

            @blk.scalar
            def _(s_):
                for s in range(nst):
                    s_.wait_ge(qm, 2 * s + 1)
                    if s >= 1:
                        s_.wait_ge(cv1, 2 * s)
                    s_.activation(ph1[:, :], PQK[s % 2][:, :], AF.Relu, scale=-1.0).then_inc(ce1, 1)
                    s_.wait_ge(ce1, 4 * s + 1)
                    s_.activation(ph2[:, :], ph1[:, :], AF.Exp, scale=-1.0).then_inc(ce1, 1)
                    s_.activation(ph3[:, :], PQK[s % 2][:, :], AF.Relu).then_inc(ce1, 1)
                    s_.wait_ge(qm, 2 * s + 2)
                    s_.copy(vb[:, s * N:(s + 1) * N], PV[s % 2][:, :]).then_inc(ce1, 1)

            @blk.vector
            def _(v):
                for s in range(nst):
                    v.wait_ge(ce1, 4 * s + 3)
                    v.tensor_tensor(qb[:, s * N:(s + 1) * N], ph2[0:64, :], ph3[0:64, :],
                                    ALU.add).then_inc(cv1, 1)
                    v.tensor_tensor(kb[:, s * N:(s + 1) * N], ph2[64:128, :], ph3[64:128, :],
                                    ALU.add).then_inc(cv1, 1)

        with (
            E.psum("c_pt0", [128, 64], BF16) as pt0,
            E.psum("c_pt1", [128, 64], BF16) as pt1,
            nc.Block() as blk2,
        ):
            PT = [pt0, pt1]

            @blk2.tensor
            def _(t):
                t.wait_ge(lw[3], 48)
                for k in range(64):
                    c = k // 2
                    srcb = kb if k % 2 == 0 else vb
                    if k >= 2:
                        t.wait_ge(tve, k - 1)
                    t.transpose(PT[k % 2][:, :], srcb[:, c * 128:(c + 1) * 128],
                                idb[0:64, 0:64]).then_inc(tmq, 1)

            @blk2.vector
            def _(v):
                for k in range(64):
                    c = k // 2
                    dstb = ktm if k % 2 == 0 else vtm
                    v.wait_ge(tmq, k + 1)
                    v.tensor_copy(dstb[:, c * 64:(c + 1) * 64], PT[k % 2][:, :]).then_inc(tve, 1)

        with (
            E.psum("c_pg", [128, 128], F32) as pg,
            E.psum("c_pgk", [128, 128], F32) as pgk,
            E.psum("c_pO", [64, 128], F32) as pO,
            E.psum("c_pd", [1, 128], F32) as pd,
            E.psum("c_pS", [64, 64], F32) as pS,
            E.psum("c_pbc", [64, 128], F32) as pbc,
            nc.Block() as blk3,
        ):
            @blk3.vector
            def _(v):
                v.memset(STf[:, :], 0.0).then_inc(mz, 1)
                v.memset(STb[:, :], 0.0).then_inc(mz, 1)
                v.memset(zf[:, :], 0.0).then_inc(mz, 1)
                v.memset(zb[:, :], 0.0).then_inc(mz, 1)
                v.wait_ge(mz, 4)
                for c in range(32):
                    v.wait_ge(ce, 5 * c + 1)
                    v.tensor_tensor(gm[:, :], gs[:, :], trib[:, :], ALU.mult).then_inc(cv, 1)
                    v.wait_ge(ce, 5 * c + 2)
                    v.tensor_tensor(gkm[:, :], gks[:, :], trib[:, :], ALU.mult).then_inc(cv, 1)
                    v.wait_ge(cm, 8 * c + 7)
                    v.tensor_tensor(STf[:, :], STf[:, :], pS[:, :], ALU.add).then_inc(cv, 1)
                    v.wait_ge(cv, 10 * c + 3)
                    v.tensor_copy(STb[:, :], STf[:, :]).then_inc(cv, 1)
                    v.reduce_sum(zi[:, :], kb[:, c * 128:(c + 1) * 128], axis=AX.X).then_inc(cv, 1)
                    v.wait_ge(cv, 10 * c + 5)
                    v.tensor_tensor(zf[:, :], zf[:, :], zi[:, :], ALU.add).then_inc(cv, 1)
                    v.wait_ge(cv, 10 * c + 6)
                    v.wait_ge(cm, 8 * c + 5)
                    v.tensor_copy(zb[:, :], zf[:, :]).then_inc(cv, 1)
                    v.wait_ge(cm, 8 * c + 6)
                    v.tensor_scalar(dr[:, :], pd[:, :], 1e-6, None, ALU.max).then_inc(cv, 1)
                    v.wait_ge(ce, 5 * c + 4)
                    v.tensor_copy(drb[:, :], dl[:, :]).then_inc(cv, 1)
                    v.wait_ge(ce, 5 * c + 5)
                    v.wait_ge(cm, 8 * c + 8)
                    v.tensor_tensor(obuf[:, c * 128:(c + 1) * 128], ob[:, :], pbc[:, :],
                                    ALU.mult).then_inc(cv, 1)

            @blk3.scalar
            def _(s_):
                for c in range(32):
                    s_.wait_ge(cm, 8 * c + 1)
                    if c >= 1:
                        s_.wait_ge(cv, 10 * (c - 1) + 1)
                    s_.copy(gs[:, :], pg[:, :]).then_inc(ce, 1)
                    s_.wait_ge(cm, 8 * c + 2)
                    if c >= 1:
                        s_.wait_ge(cv, 10 * (c - 1) + 2)
                    s_.copy(gks[:, :], pgk[:, :]).then_inc(ce, 1)
                    s_.wait_ge(cv, 10 * c + 8)
                    s_.activation(dl[:, :], dr[:, :], AF.Ln).then_inc(ce, 1)
                    s_.wait_ge(ce, 5 * c + 3)
                    s_.activation(dl[:, :], dl[:, :], AF.Exp, scale=-1.0).then_inc(ce, 1)
                    s_.wait_ge(cm, 8 * c + 4)
                    if c >= 1:
                        s_.wait_ge(cv, 10 * (c - 1) + 10)
                    s_.copy(ob[:, :], pO[:, :]).then_inc(ce, 1)

            @blk3.tensor
            def _(t):
                t.wait_ge(mz, 4)
                for c in range(32):
                    if c >= 1:
                        t.wait_ge(ce, 5 * (c - 1) + 1)
                    t.matmul(pg[:, :], vb[:, c * 128:(c + 1) * 128], qb[:, c * 128:(c + 1) * 128],
                             start=True, stop=True).then_inc(cm, 1)
                    if c >= 1:
                        t.wait_ge(ce, 5 * (c - 1) + 2)
                    t.matmul(pgk[:, :], kb[:, c * 128:(c + 1) * 128], qb[:, c * 128:(c + 1) * 128],
                             start=True, stop=True).then_inc(cm, 1)
                    if c >= 1:
                        t.wait_ge(cv, 10 * (c - 1) + 4)
                        t.wait_ge(ce, 5 * (c - 1) + 5)
                    t.matmul(pO[:, :], STb[:, :], qb[:, c * 128:(c + 1) * 128],
                             start=True, stop=False).then_inc(cm, 1)
                    t.wait_ge(cv, 10 * c + 1)
                    t.matmul(pO[:, :], ktm[:, c * 64:(c + 1) * 64], gm[:, :],
                             start=False, stop=True).then_inc(cm, 1)
                    if c >= 1:
                        t.wait_ge(cv, 10 * (c - 1) + 7)
                        t.wait_ge(cv, 10 * (c - 1) + 8)
                    t.matmul(pd[:, :], zb[:, :], qb[:, c * 128:(c + 1) * 128],
                             start=True, stop=False).then_inc(cm, 1)
                    t.wait_ge(cv, 10 * c + 2)
                    t.matmul(pd[:, :], onb[:, 0:1], gkm[:, :],
                             start=False, stop=True).then_inc(cm, 1)
                    if c >= 1:
                        t.wait_ge(cv, 10 * (c - 1) + 3)
                    t.matmul(pS[:, :], vtm[:, c * 64:(c + 1) * 64], ktm[:, c * 64:(c + 1) * 64],
                             start=True, stop=True).then_inc(cm, 1)
                    t.wait_ge(cv, 10 * c + 9)
                    t.matmul(pbc[:, :], onb[0:1, 0:64], drb[:, :], start=True,
                             stop=True).then_inc(cm, 1)

        with (
            E.psum("c_pw0", [128, 512], F32) as pw0,
            E.psum("c_pw1", [128, 512], F32) as pw1,
            nc.Block() as blk4,
        ):
            PW = [pw0, pw1]
            wv = E.sem("cwv")

            @blk4.sync
            def _(sy):
                for s in range(nst):
                    if s >= 2:
                        sy.wait_ge(wv, 8 * (s - 1))
                    sy.dma_start(xs[:, (s % 2) * 4096:(s % 2 + 1) * 4096].rearrange("p (c t) -> p c t", t=N),
                                 srcr[:, :, s * N:(s + 1) * N]).then_inc(lxs[s % 2], 16)
                    if s >= 1:
                        sy.wait_ge(wv, 8 * s)
                        sy.dma_start(dstr[:, :, (s - 1) * N:s * N],
                                     od[:, ((s - 1) % 2) * 4096:((s - 1) % 2 + 1) * 4096].rearrange(
                                         "p (c t) -> p c t", t=N)).then_inc(stp[(s - 1) % 2], 16)
                sy.wait_ge(wv, 8 * nst)
                sy.dma_start(dstr[:, :, (nst - 1) * N:nst * N],
                             od[:, ((nst - 1) % 2) * 4096:((nst - 1) % 2 + 1) * 4096].rearrange(
                                 "p (c t) -> p c t", t=N)).then_inc(stp[(nst - 1) % 2], 16)
                sy.wait_ge(stp[0], 16 * (nst // 2))
                sy.wait_ge(stp[1], 16 * (nst // 2))

            @blk4.tensor
            def _(t):
                t.wait_ge(lw[2], 16)
                for s in range(nst):
                    for m in range(8):
                        k = 8 * s + m
                        if k >= 2:
                            t.wait_ge(wv, k - 1)
                        t.matmul(PW[k % 2][:, :], AoB[:, m * 128:(m + 1) * 128],
                                 obuf[:, s * N:(s + 1) * N], start=True, stop=True).then_inc(wm, 1)

            @blk4.vector
            def _(v):
                for s in range(nst):
                    ba = (s % 2) * 4096
                    v.wait_ge(lxs[s % 2], 16 * (s // 2 + 1))
                    if s >= 2:
                        v.wait_ge(stp[s % 2], 16 * (s // 2))
                    for m in range(8):
                        k = 8 * s + m
                        v.wait_ge(wm, k + 1)
                        v.tensor_tensor(od[:, ba + m * N:ba + (m + 1) * N], PW[k % 2][:, :],
                                        xs[:, ba + m * N:ba + (m + 1) * N], ALU.add).then_inc(wv, 1)
    return

def emit_D(E, src="s1", dst="u0"):
    nc = E.nc
    srcr = E.d[src].rearrange("(c p) t -> p c t", p=128)
    u0 = E.d[dst]
    from contextlib import ExitStack
    with ExitStack() as stk:
        gates = stk.enter_context(E.sbuf("d_gates", [32, 4096], BF16))
        IndB = stk.enter_context(E.sbuf("d_ind", [32, 4096], BF16))
        lxs = _dsem(E, "dx")
        lcw = _dsem(E, "dc", 2)
        gm1, ge1, gv1 = E.sem("dgm"), E.sem("dge"), E.sem("dgv")
        with ExitStack() as stk1:
            xs = stk1.enter_context(E.sbuf("d_xs", [128, 2 * 4096], F32))
            SelB = stk1.enter_context(E.sbuf("d_sel", [128, 8 * 32], F32))
            gbuf = stk1.enter_context(E.sbuf("d_g", [128, 2 * 32], F32))
            lbuf = stk1.enter_context(E.sbuf("d_l", [128, 2 * 32], F32))
            gw = stk1.enter_context(E.sbuf("d_gw", [128, 2 * 32], F32))
            mcol = stk1.enter_context(E.sbuf("d_m", [128, 2], F32))
            keep = stk1.enter_context(E.sbuf("d_keep", [128, 2 * 32], F32))
            gfb = stk1.enter_context(E.sbuf("d_gfb", [128, 2 * 32], BF16))
            pl0 = stk1.enter_context(E.psum("d_pl0", [128, 32], F32))
            pl1 = stk1.enter_context(E.psum("d_pl1", [128, 32], F32))
            idb = stk1.enter_context(E.sbuf("d_idb", [128, 128], BF16))
            pt0 = stk1.enter_context(E.psum("d_pt0", [32, 128], BF16))
            pt1 = stk1.enter_context(E.psum("d_pt1", [32, 128], BF16))
            with nc.Block() as blk:
                PL = [pl0, pl1]
                PT = [pt0, pt1]

                @blk.sync
                def _(sy):
                    sy.dma_start(SelB[:, :].rearrange("p (c n) -> p c n", n=32),
                                 E.d["SelT"].rearrange("(c p) n -> p c n", p=128)).then_inc(lcw[0], 16)
                    sy.dma_start(IndB[:, :], E.d["IndE"][:, :]).then_inc(lcw[1], 16)
                    sy.dma_start(idb[:, :], E.d["identb"][:, :]).then_inc(lcw[1], 16)
                    for s in range(NST):
                        if s >= 2:
                            sy.wait_ge(gm1, 8 * (s - 1))
                        sy.dma_start(xs[:, (s % 2) * 4096:(s % 2 + 1) * 4096].rearrange("p (c t) -> p c t", t=512),
                                     srcr[:, :, s * 512:(s + 1) * 512]).then_inc(lxs[s % 2], 16)

                @blk.tensor
                def _(t):
                    t.wait_ge(lcw[0], 16)
                    t.wait_ge(lcw[1], 32)
                    for s in range(NST):
                        for tt in range(4):
                            K = s * 4 + tt
                            t.wait_ge(lxs[s % 2], 16 * (s // 2 + 1))
                            if K >= 2:
                                t.wait_ge(ge1, 3 * (K - 2) + 1 if K >= 3 else 2)
                            for c in range(8):
                                mm = t.matmul(PL[K % 2][:, :], xs[:, (s % 2) * 4096 + c * 512 + tt * 128:
                                                                  (s % 2) * 4096 + c * 512 + (tt + 1) * 128],
                                              SelB[:, c * 32:(c + 1) * 32],
                                              start=(c == 0), stop=(c == 7))
                            mm.then_inc(gm1, 1)
                            t.wait_ge(gv1, 14 * K)
                            if K >= 1:
                                KK = K - 1
                                if KK >= 2:
                                    t.wait_ge(ge1, 3 * KK - 1)
                                t.transpose(PT[KK % 2][:, :], gfb[:, (KK % 2) * 32:(KK % 2 + 1) * 32],
                                            idb[:, :]).then_inc(gm1, 1)
                    t.wait_ge(gv1, 14 * 32)
                    KK = 31
                    t.wait_ge(ge1, 3 * KK - 1)
                    t.transpose(PT[KK % 2][:, :], gfb[:, (KK % 2) * 32:(KK % 2 + 1) * 32],
                                idb[:, :]).then_inc(gm1, 1)

                @blk.scalar
                def _(s_):
                    for K in range(32):
                        s_.wait_ge(gm1, 2 * K + 1)
                        if K >= 2:
                            s_.wait_ge(gv1, 14 * (K - 2) + 14)
                        s_.activation(gbuf[:, (K % 2) * 32:(K % 2 + 1) * 32], PL[K % 2][:, :],
                                      AF.Sigmoid).then_inc(ge1, 1)
                        s_.copy(lbuf[:, (K % 2) * 32:(K % 2 + 1) * 32], PL[K % 2][:, :]).then_inc(ge1, 1)
                        if K >= 1:
                            KK = K - 1
                            s_.copy(gates[:, KK * 128:(KK + 1) * 128], PT[KK % 2][:, :]).then_inc(ge1, 1)
                    s_.wait_ge(gm1, 64)
                    s_.copy(gates[:, 31 * 128:32 * 128], PT[31 % 2][:, :]).then_inc(ge1, 1)

                @blk.vector
                def _(v):
                    for K in range(32):
                        b = (K % 2) * 32
                        myv = 14 * K
                        v.wait_ge(ge1, 3 * K + 2)
                        v.tensor_copy(gw[:, b:b + 32], lbuf[:, b:b + 32]).then_inc(gv1, 1)
                        myv += 1
                        for it in range(4):
                            v.wait_ge(gv1, myv)
                            v.reduce_max(mcol[:, K % 2:K % 2 + 1], gw[:, b:b + 32],
                                         axis=AX.X).then_inc(gv1, 1)
                            myv += 1
                            if it < 3:
                                v.wait_ge(gv1, myv)
                                v.tensor_scalar(keep[:, b:b + 32], gw[:, b:b + 32],
                                                mcol[:, K % 2:K % 2 + 1], None, ALU.is_ge).then_inc(gv1, 1)
                                myv += 1
                                v.wait_ge(gv1, myv)
                                v.scalar_tensor_tensor(gw[:, b:b + 32], keep[:, b:b + 32], -1e30,
                                                       gw[:, b:b + 32], ALU.mult, ALU.add).then_inc(gv1, 1)
                                myv += 1
                        v.wait_ge(gv1, myv)
                        v.tensor_scalar(keep[:, b:b + 32], lbuf[:, b:b + 32],
                                        mcol[:, K % 2:K % 2 + 1], None, ALU.is_ge).then_inc(gv1, 1)
                        myv += 1
                        v.wait_ge(gv1, myv)
                        v.tensor_tensor(keep[:, b:b + 32], gbuf[:, b:b + 32], keep[:, b:b + 32],
                                        ALU.mult).then_inc(gv1, 1)
                        myv += 1
                        v.wait_ge(gv1, myv)
                        v.tensor_copy(gfb[:, b:b + 32], keep[:, b:b + 32]).then_inc(gv1, 1)
                        myv += 1
                        assert myv == 14 * (K + 1), (myv, K)

        NG, GE, NSP, SWD = 4, 8, 16, 256
        wk = _dsem(E, "dwk", 8, kind="sw")
        wv = _dsem(E, "dwv", 8, kind="sw")
        lxb = _dsem(E, "dxb", kind="sw")
        du = _dsem(E, "du", NSP, kind="sw")
        km, bm, vm = E.sem("dkm"), E.sem("dbm"), E.sem("dvm")
        sev, vv, vo = E.sem("dse"), E.sem("dvv"), E.sem("dvo")
        with ExitStack() as stk2:
            KeB = stk2.enter_context(E.sbuf("d_ke", [128, 8 * 4096], BF16))
            VeB = stk2.enter_context(E.sbuf("d_ve", [128, 8 * 4096], BF16))
            xb = stk2.enter_context(E.sbuf("d_xb", [128, 2 * 2048], BF16))
            scb = stk2.enter_context(E.sbuf("d_scb", [128, 2 * 1024], BF16))
            scg = stk2.enter_context(E.sbuf("d_scg", [128, 2 * 1024], BF16))
            odb = stk2.enter_context(E.sbuf("d_od", [128, 2 * 2048], F32))
            po = [stk2.enter_context(E.psum(f"d_po{j}", [128, 512], F32)) for j in range(4)]
            psc = [stk2.enter_context(E.psum(f"d_psc{j}", [128, 256], F32)) for j in range(3)]
            pgb = stk2.enter_context(E.psum("d_pgb", [128, 256], F32))
            with nc.Block() as blk2:
                Keys3 = E.d["Keys"]
                Vals3 = E.d["Values"]

                @blk2.gpsimd
                def _(g):
                    for gi in range(NG):
                        for ee in range(GE):
                            e = gi * GE + ee
                            if gi >= 1:
                                g.wait_ge(km, 32 * 16 * gi)
                                g.wait_ge(vm, 8 * 16 * gi)
                            g.dma_start(KeB[:, ee * 4096:(ee + 1) * 4096].rearrange("p (c f) -> p c f", f=512),
                                        Keys3[e].rearrange("(c p) f -> p c f", p=128)).then_inc(wk[ee], 16)
                            g.dma_start(VeB[:, ee * 4096:(ee + 1) * 4096].rearrange("p (f v) -> p f v", v=1024),
                                        Vals3[e].rearrange("(f p) v -> p f v", p=128)).then_inc(wv[ee], 16)
                        for s in range(NSP):
                            P = gi * NSP + s
                            if P >= 2:
                                g.wait_ge(km, 32 * (P - 1))
                            g.dma_start(xb[:, (P % 2) * 2048:(P % 2 + 1) * 2048].rearrange("p (c t) -> p c t", t=SWD),
                                        srcr[:, :, s * SWD:(s + 1) * SWD]).then_inc(lxb[P % 2], 16)
                            if P >= 1:
                                PP = P - 1
                                gp, sp = divmod(PP, NSP)
                                g.wait_ge(vo, 4 * (PP + 1))
                                if gp >= 1:
                                    g.wait_ge(du[sp], 64 * gp)
                                for j in range(4):
                                    tsub, vh = divmod(j, 2)
                                    d = u0[sp * SWD + tsub * 128: sp * SWD + (tsub + 1) * 128,
                                           vh * 512:(vh + 1) * 512]
                                    srcap = odb[:, (PP % 2) * 2048 + j * 512:(PP % 2) * 2048 + (j + 1) * 512]
                                    if gp == 0:
                                        g.dma_start(d, srcap).then_inc(du[sp], 16)
                                    else:
                                        g.dma_start(d, srcap, accum_op=ALU.add).then_inc(du[sp], 16)
                    PP = NG * NSP - 1
                    gp, sp = divmod(PP, NSP)
                    g.wait_ge(vo, 4 * (PP + 1))
                    g.wait_ge(du[sp], 64 * gp)
                    for j in range(4):
                        tsub, vh = divmod(j, 2)
                        d = u0[sp * SWD + tsub * 128: sp * SWD + (tsub + 1) * 128,
                               vh * 512:(vh + 1) * 512]
                        g.dma_start(d, odb[:, (PP % 2) * 2048 + j * 512:(PP % 2) * 2048 + (j + 1) * 512],
                                    accum_op=ALU.add).then_inc(du[sp], 16)
                    for sp in range(NSP):
                        g.wait_ge(du[sp], 64 * NG)

                @blk2.tensor
                def _(t):
                    for gi in range(NG):
                        for s in range(NSP):
                            P = gi * NSP + s
                            t.wait_ge(lxb[P % 2], 16 * (P // 2 + 1))
                            for step in range(GE + 1):
                                if step < GE:
                                    ee = step
                                    EIX = P * 8 + ee
                                    t.wait_ge(wk[ee], 16 * (gi + 1))
                                    for fc in range(4):
                                        KIX = EIX * 4 + fc
                                        if KIX >= 3:
                                            t.wait_ge(sev, KIX - 2)
                                        for c in range(8):
                                            mm = t.matmul(psc[KIX % 3][:, :],
                                                          KeB[:, ee * 4096 + c * 512 + fc * 128:
                                                              ee * 4096 + c * 512 + (fc + 1) * 128],
                                                          xb[:, (P % 2) * 2048 + c * SWD:(P % 2) * 2048 + (c + 1) * SWD],
                                                          start=(c == 0), stop=(c == 7))
                                        mm.then_inc(km, 1)
                                    if EIX >= 1:
                                        t.wait_ge(vv, 4 * EIX)
                                    t.matmul(pgb[:, :], IndB[:, (gi * GE + ee) * 128:(gi * GE + ee + 1) * 128],
                                             gates[:, s * SWD:(s + 1) * SWD], start=True,
                                             stop=True).then_inc(bm, 1)
                                if step >= 1:
                                    ee = step - 1
                                    EIX = P * 8 + ee
                                    t.wait_ge(wv[ee], 16 * (gi + 1))
                                    t.wait_ge(vv, 4 * (EIX + 1))
                                    if ee == 0:
                                        t.wait_ge(vo, 4 * P)
                                    for fc in range(4):
                                        for j in range(4):
                                            tsub, vh = divmod(j, 2)
                                            mm = t.matmul(po[j][:, :],
                                                     scg[:, (EIX % 2) * 1024 + fc * 256 + tsub * 128:
                                                         (EIX % 2) * 1024 + fc * 256 + (tsub + 1) * 128],
                                                     VeB[:, ee * 4096 + fc * 1024 + vh * 512:
                                                         ee * 4096 + fc * 1024 + (vh + 1) * 512],
                                                     start=(ee == 0 and fc == 0), stop=(ee == 7 and fc == 3))
                                    mm.then_inc(vm, 1)

                @blk2.scalar
                def _(s_):
                    for P in range(NG * NSP):
                        for ee in range(GE):
                            EIX = P * 8 + ee
                            for fc in range(4):
                                KIX = EIX * 4 + fc
                                s_.wait_ge(km, KIX + 1)
                                if EIX >= 2:
                                    s_.wait_ge(vv, 4 * (EIX - 1))
                                s_.activation(scb[:, (EIX % 2) * 1024 + fc * 256:(EIX % 2) * 1024 + (fc + 1) * 256],
                                              psc[KIX % 3][:, :], AF.Relu).then_inc(sev, 1)

                @blk2.vector
                def _(v):
                    for P in range(NG * NSP):
                        for ee in range(GE):
                            EIX = P * 8 + ee
                            v.wait_ge(bm, EIX + 1)
                            for fc in range(4):
                                KIX = EIX * 4 + fc
                                v.wait_ge(sev, KIX + 1)
                                if EIX >= 2:
                                    v.wait_ge(vm, EIX - 1)
                                v.tensor_tensor(scg[:, (EIX % 2) * 1024 + fc * 256:(EIX % 2) * 1024 + (fc + 1) * 256],
                                                scb[:, (EIX % 2) * 1024 + fc * 256:(EIX % 2) * 1024 + (fc + 1) * 256],
                                                pgb[:, :], ALU.mult).then_inc(vv, 1)
                        v.wait_ge(vm, 8 * (P + 1))
                        if P >= 2:
                            PP = P - 2
                            gp, sp = divmod(PP, NSP)
                            v.wait_ge(du[sp], 64 * gp + 64)
                        for j in range(4):
                            v.tensor_copy(odb[:, (P % 2) * 2048 + j * 512:(P % 2) * 2048 + (j + 1) * 512],
                                          po[j][:, :]).then_inc(vo, 1)
    return


def emit_final(E, usrc="u0", xsrc="s1", dst="out"):
    nc = E.nc
    u0 = E.d[usrc]
    xr = E.d[xsrc].rearrange("(c p) t -> p c t", p=128)
    out = E.d[dst]
    lc = E.sem("flc")
    lu, lx, stp = _dsem(E, "flu"), _dsem(E, "flx"), _dsem(E, "fst")
    fm, fe, fv = E.sem("ffm"), E.sem("ffe"), E.sem("ffv")
    from contextlib import ExitStack
    with ExitStack() as stk:
        ub = stk.enter_context(E.sbuf("f_u", [128, 2 * 1024], F32))
        xc = stk.enter_context(E.sbuf("f_x", [128, 2 * 1024], F32))
        sm = stk.enter_context(E.sbuf("f_s", [128, 3 * 1024], F32))
        jk = stk.enter_context(E.sbuf("f_j", [128, 2048], F32))
        idf = stk.enter_context(E.sbuf("f_id", [128, 128], F32))
        cols = stk.enter_context(E.sbuf("f_c", [128, 3 * 8], F32))
        ob = stk.enter_context(E.sbuf("f_o", [128, 2 * 1024], F32))
        pt = [stk.enter_context(E.psum(f"f_pt{i}", [128, 128], F32)) for i in range(4)]
        I_AD, I_SM, I_FN = {}, {}, {}
        cnt = 0
        for it in range(35):
            if it < 32:
                for c in range(8):
                    cnt += 1; I_AD[(it, c)] = cnt
            if 1 <= it <= 32:
                cnt += 5; I_SM[it - 1] = cnt
            if 2 <= it <= 34:
                cnt += 2; I_FN[it - 2] = cnt
        I_AC, I_LE = {}, {}
        cnt = 0
        for it in range(34):
            if 1 <= it <= 32:
                cnt += 2; I_AC[it - 1] = cnt
            if 2 <= it <= 33:
                cnt += 2; I_LE[it - 2] = cnt
        with nc.Block() as blk:
            @blk.sync
            def _(sy):
                sy.dma_start(idf[:, :], E.d["identf"][:, :]).then_inc(lc, 16)
                for tt in range(32):
                    if tt >= 2:
                        sy.wait_ge(fv, I_AD[(tt - 2, 7)])
                    sy.dma_start(ub[:, (tt % 2) * 1024:(tt % 2 + 1) * 1024],
                                 u0[tt * 128:(tt + 1) * 128, :]).then_inc(lu[tt % 2], 16)
                    sy.dma_start(xc[:, (tt % 2) * 1024:(tt % 2 + 1) * 1024].rearrange("p (c t) -> p c t", t=128),
                                 xr[:, :, tt * 128:(tt + 1) * 128]).then_inc(lx[tt % 2], 16)
                    if tt >= 2:
                        sy.wait_ge(fv, I_FN[tt - 2])
                        sy.dma_start(out[(tt - 2) * 128:(tt - 1) * 128, :],
                                     ob[:, (tt % 2) * 1024:(tt % 2 + 1) * 1024]).then_inc(stp[tt % 2], 16)
                for tt in (30, 31):
                    sy.wait_ge(fv, I_FN[tt])
                    sy.dma_start(out[tt * 128:(tt + 1) * 128, :],
                                 ob[:, (tt % 2) * 1024:(tt % 2 + 1) * 1024]).then_inc(stp[tt % 2], 16)
                sy.wait_ge(stp[0], 16 * 16)
                sy.wait_ge(stp[1], 16 * 16)

            @blk.tensor
            def _(t):
                t.wait_ge(lc, 16)
                for tt in range(32):
                    t.wait_ge(lx[tt % 2], 16 * (tt // 2 + 1))
                    for c in range(8):
                        k = tt * 8 + c
                        if k >= 4:
                            kk = k - 4
                            t.wait_ge(fv, I_AD[(kk // 8, kk % 8)])
                        t.transpose(pt[k % 4][:, :],
                                    xc[:, (tt % 2) * 1024 + c * 128:(tt % 2) * 1024 + (c + 1) * 128],
                                    idf[:, :]).then_inc(fm, 1)

            @blk.vector
            def _(v):
                for it in range(35):
                    if it < 32:
                        tt = it
                        base = (tt % 2) * 1024
                        sb = (tt % 3) * 1024
                        v.wait_ge(lu[tt % 2], 16 * (tt // 2 + 1))
                        if tt >= 3:
                            v.wait_ge(fv, I_FN[tt - 3])
                        for c in range(8):
                            k = tt * 8 + c
                            v.wait_ge(fm, k + 1)
                            v.tensor_tensor(sm[:, sb + c * 128:sb + (c + 1) * 128],
                                            ub[:, base + c * 128:base + (c + 1) * 128],
                                            pt[k % 4][:, :], ALU.add).then_inc(fv, 1)
                    if 1 <= it <= 32:
                        tt = it - 1
                        cb = (tt % 3) * 8
                        v.wait_ge(fe, I_AC[tt])
                        v.tensor_scalar(cols[:, cb + 0:cb + 1], cols[:, cb + 0:cb + 1],
                                        1.0 / 1024.0, None, ALU.mult).then_inc(fv, 1)
                        v.wait_ge(fv, I_SM[tt] - 4)
                        v.tensor_scalar(cols[:, cb + 1:cb + 2], cols[:, cb + 1:cb + 2],
                                        1.0 / 1024.0, None, ALU.mult).then_inc(fv, 1)
                        v.wait_ge(fv, I_SM[tt] - 3)
                        v.scalar_tensor_tensor(cols[:, cb + 2:cb + 3], cols[:, cb + 0:cb + 1], -1.0,
                                               cols[:, cb + 0:cb + 1], ALU.mult, ALU.mult).then_inc(fv, 1)
                        v.wait_ge(fv, I_SM[tt] - 2)
                        v.tensor_tensor(cols[:, cb + 3:cb + 4], cols[:, cb + 1:cb + 2],
                                        cols[:, cb + 2:cb + 3], ALU.add).then_inc(fv, 1)
                        v.wait_ge(fv, I_SM[tt] - 1)
                        v.tensor_scalar(cols[:, cb + 3:cb + 4], cols[:, cb + 3:cb + 4],
                                        1e-5, None, ALU.add).then_inc(fv, 1)
                    if 2 <= it <= 34:
                        tt = it - 2
                        cb = (tt % 3) * 8
                        sb = (tt % 3) * 1024
                        v.wait_ge(fe, I_LE[tt])
                        v.scalar_tensor_tensor(cols[:, cb + 5:cb + 6], cols[:, cb + 0:cb + 1], -1.0,
                                               cols[:, cb + 4:cb + 5], ALU.mult, ALU.mult).then_inc(fv, 1)
                        if tt >= 2:
                            v.wait_ge(stp[tt % 2], 16 * (tt // 2))
                        v.wait_ge(fv, I_FN[tt] - 1)
                        v.tensor_scalar(ob[:, (tt % 2) * 1024:(tt % 2 + 1) * 1024],
                                        sm[:, sb:sb + 1024],
                                        cols[:, cb + 4:cb + 5], cols[:, cb + 5:cb + 6],
                                        ALU.mult, ALU.add).then_inc(fv, 1)

            @blk.scalar
            def _(s_):
                for it in range(34):
                    if 1 <= it <= 32:
                        tt = it - 1
                        cb = (tt % 3) * 8
                        sb = (tt % 3) * 1024
                        s_.wait_ge(fv, I_AD[(tt, 7)])
                        if tt >= 3:
                            s_.wait_ge(fv, I_FN[tt - 3])
                        if tt >= 1:
                            s_.wait_ge(fe, I_AC[tt] - 2)
                        s_.activation(jk[:, 0:1024], sm[:, sb:sb + 1024], AF.Copy,
                                      accum_out=cols[:, cb + 0:cb + 1]).then_inc(fe, 1)
                        s_.wait_ge(fe, I_AC[tt] - 1)
                        s_.activation(jk[:, 1024:2048], sm[:, sb:sb + 1024], AF.Square,
                                      accum_out=cols[:, cb + 1:cb + 2]).then_inc(fe, 1)
                    if 2 <= it <= 33:
                        tt = it - 2
                        cb = (tt % 3) * 8
                        s_.wait_ge(fv, I_SM[tt])
                        s_.activation(cols[:, cb + 4:cb + 5], cols[:, cb + 3:cb + 4],
                                      AF.Ln).then_inc(fe, 1)
                        s_.wait_ge(fe, I_LE[tt] - 1)
                        s_.activation(cols[:, cb + 4:cb + 5], cols[:, cb + 4:cb + 5], AF.Exp,
                                      scale=-0.5).then_inc(fe, 1)
    return


ALL_SPECS = {}
ALL_SPECS.update(CONST_SPECS)
ALL_SPECS.update(WEIGHT_SPECS)
ALL_SPECS.update(STREAM_SPECS)
ALL_SPECS["x0"] = ([4096, 1024], F32)
ALL_SPECS["out"] = ([4096, 1024], F32)


class LazyDram:
    def __init__(self, nc, ext_in, ext_out):
        self.nc = nc
        self.ext_in = set(ext_in)
        self.ext_out = set(ext_out)
        self.aps = {}
        self.used_in = set()
        self.used_out = set()

    def __getitem__(self, name):
        if name not in self.aps:
            shape, dt = ALL_SPECS[name]
            if name in self.ext_in:
                kind = "ExternalInput"
                self.used_in.add(name)
            elif name in self.ext_out:
                kind = "ExternalOutput"
                self.used_out.add(name)
            else:
                kind = "Internal"
            self.aps[name] = self.nc.dram_tensor(name, shape, dt, kind=kind).ap()
        return self.aps[name]


def build_program(phases, ext_in, ext_out):
    from contextlib import ExitStack
    nc = bass.Bass("TRN2", target_bir_lowering=False)
    dram = LazyDram(nc, ext_in, ext_out)
    with ExitStack() as stack:
        E = Env(nc, stack, dram)
        import os
        recycle = os.environ.get("MK_NO_RECYCLE", "") != "1"
        for i, ph in enumerate(phases):
            ph(E)
            if recycle and i + 1 < len(phases):
                E.end_phase()
    return nc, dram.used_in, dram.used_out

def emit_B(E, src="s1", dst="t0"):
    nc = E.nc
    N = SW
    srcr = E.d[src].rearrange("(c p) t -> p c t", p=128)
    dstr = E.d[dst].rearrange("(c p) t -> p c t", p=128)
    lw = _dsem(E, "bw", 3, kind="sw") + _dsem(E, "bwh", 2)
    lx = _dsem(E, "bx")
    stp = _dsem(E, "bs")
    smm, sev, svv = E.sem("bm"), E.sem("be"), E.sem("bv")
    msz = E.sem("bz")
    V_PER = 24 + 8 + 16
    S_PER = 16
    M_PER = 24
    with (
        E.sbuf("b_wr", [128, 8 * 1024], BF16) as WrB,
        E.sbuf("b_wk", [128, 8 * 1024], BF16) as WkB,
        E.sbuf("b_wv", [128, 8 * 1024], BF16) as WvB,
        E.sbuf("b_mu", [128, 16], F32) as muB,
        E.sbuf("b_xs", [128, 2 * 8 * 513], F32) as xs,
        E.sbuf("b_d", [128, 4096], F32) as db,
        E.sbuf("b_xr", [128, 2 * 4096], BF16) as xrb,
        E.sbuf("b_xk", [128, 2 * 4096], BF16) as xkb,
        E.sbuf("b_kb", [128, 4096], BF16) as kb,
        E.sbuf("b_k2", [128, 4096], BF16) as k2b,
        E.sbuf("b_rb", [128, 4096], BF16) as rb,
        E.sbuf("b_od", [128, 2 * 4096], F32) as od,
        E.psum("b_pk0", [128, 512], F32) as pk0,
        E.psum("b_pk1", [128, 512], F32) as pk1,
        E.psum("b_pr0", [128, 512], F32) as pr0,
        E.psum("b_pr1", [128, 512], F32) as pr1,
        E.psum("b_pv0", [128, 512], F32) as pv0,
        E.psum("b_pv1", [128, 512], F32) as pv1,
        nc.Block() as blk,
    ):
        nst = NST
        PK = [pk0, pk1]; PR = [pr0, pr1]; PV = [pv0, pv1]

        @blk.gpsimd
        def _(g):
            g.dma_start(WrB[:, :].rearrange("p (c n) -> p c n", n=1024),
                        E.d["WrT"].rearrange("(c p) n -> p c n", p=128)).then_inc(lw[0], 16)
            g.dma_start(WkB[:, :].rearrange("p (c n) -> p c n", n=1024),
                        E.d["WkT"].rearrange("(c p) n -> p c n", p=128)).then_inc(lw[1], 16)
            g.dma_start(WvB[:, :].rearrange("p (c n) -> p c n", n=1024),
                        E.d["WvT"].rearrange("(c p) n -> p c n", p=128)).then_inc(lw[2], 16)

        @blk.sync
        def _(sy):
            sy.dma_start(muB[:, 0:8], E.d["muR"][:, :]).then_inc(lw[3], 16)
            sy.dma_start(muB[:, 8:16], E.d["muK"][:, :]).then_inc(lw[4], 16)
            for s in range(nst):
                base = (s % 2) * (8 * 513)
                if s >= 2:
                    sy.wait_ge(svv, V_PER * (s - 1))
                if s == 0:
                    sy.dma_start(
                        xs[:, 0:8 * 513].rearrange("p (c t) -> p c t", t=513)[:, :, 1:513],
                        srcr[:, :, 0:512]).then_inc(lx[0], 16)
                else:
                    sy.dma_start(
                        xs[:, base:base + 8 * 513].rearrange("p (c t) -> p c t", t=513),
                        srcr[:, :, s * N - 1:(s + 1) * N]).then_inc(lx[s % 2], 16)
                if s >= 1:
                    sy.wait_ge(svv, V_PER * s)
                    sy.dma_start(dstr[:, :, (s - 1) * N:s * N],
                                 od[:, ((s - 1) % 2) * 4096:((s - 1) % 2 + 1) * 4096].rearrange(
                                     "p (c t) -> p c t", t=N)).then_inc(stp[(s - 1) % 2], 16)
            sy.wait_ge(svv, V_PER * nst)
            sy.dma_start(dstr[:, :, (nst - 1) * N:nst * N],
                         od[:, ((nst - 1) % 2) * 4096:((nst - 1) % 2 + 1) * 4096].rearrange(
                             "p (c t) -> p c t", t=N)).then_inc(stp[(nst - 1) % 2], 16)
            sy.wait_ge(stp[0], 16 * (nst // 2))
            sy.wait_ge(stp[1], 16 * (nst // 2))

        @blk.vector
        def _(v):
            v.wait_ge(lw[3], 16)
            v.wait_ge(lw[4], 16)
            for s in range(nst):
                bx = (s % 2) * (8 * 513)
                ba = (s % 2) * 4096
                myv = V_PER * s
                v.wait_ge(lx[s % 2], 16 * (s // 2 + 1))
                if s == 0:
                    v.memset(xs[:, 0:8 * 513].rearrange("p (c t) -> p c t", t=513)[:, :, 0:1],
                             0.0).then_inc(msz, 1)
                    v.wait_ge(msz, 1)
                if s >= 2:
                    v.wait_ge(smm, M_PER * (s - 1))
                for c in range(8):
                    v.tensor_tensor(db[:, c * N:(c + 1) * N],
                                    xs[:, bx + c * 513 + 1:bx + c * 513 + 513],
                                    xs[:, bx + c * 513:bx + c * 513 + 512], ALU.subtract).then_inc(svv, 1)
                    myv += 1
                v.wait_ge(svv, myv)
                for c in range(8):
                    v.scalar_tensor_tensor(xrb[:, ba + c * N:ba + (c + 1) * N],
                                           db[:, c * N:(c + 1) * N], muB[:, c:c + 1],
                                           xs[:, bx + c * 513:bx + c * 513 + 512],
                                           ALU.mult, ALU.add).then_inc(svv, 1)
                    myv += 1
                for c in range(8):
                    v.scalar_tensor_tensor(xkb[:, ba + c * N:ba + (c + 1) * N],
                                           db[:, c * N:(c + 1) * N], muB[:, 8 + c:9 + c],
                                           xs[:, bx + c * 513:bx + c * 513 + 512],
                                           ALU.mult, ALU.add).then_inc(svv, 1)
                    myv += 1
                for m in range(8):
                    v.wait_ge(sev, S_PER * s + m + 1)
                    v.tensor_tensor(k2b[:, m * N:(m + 1) * N], kb[:, m * N:(m + 1) * N],
                                    kb[:, m * N:(m + 1) * N], ALU.mult).then_inc(svv, 1)
                    myv += 1
                if s >= 2:
                    v.wait_ge(stp[s % 2], 16 * (s // 2))
                for m in range(8):
                    v.wait_ge(smm, M_PER * s + 16 + m + 1)
                    v.wait_ge(sev, S_PER * s + 8 + m + 1)
                    v.tensor_tensor(od[:, ba + m * N:ba + (m + 1) * N],
                                    PV[m % 2][:, :], rb[:, m * N:(m + 1) * N], ALU.mult).then_inc(svv, 1)
                    myv += 1
                    v.wait_ge(svv, myv)
                    v.tensor_tensor(od[:, ba + m * N:ba + (m + 1) * N],
                                    od[:, ba + m * N:ba + (m + 1) * N],
                                    xs[:, bx + m * 513 + 1:bx + m * 513 + 513], ALU.add).then_inc(svv, 1)
                    myv += 1
                assert myv == V_PER * (s + 1)

        @blk.scalar
        def _(s_):
            for s in range(nst):
                mys = S_PER * s
                for m in range(8):
                    s_.wait_ge(smm, M_PER * s + m + 1)
                    s_.activation(kb[:, m * N:(m + 1) * N], PK[m % 2][:, :], AF.Relu).then_inc(sev, 1)
                    mys += 1
                for m in range(8):
                    s_.wait_ge(smm, M_PER * s + 8 + m + 1)
                    s_.activation(rb[:, m * N:(m + 1) * N], PR[m % 2][:, :], AF.Sigmoid).then_inc(sev, 1)
                    mys += 1
                assert mys == S_PER * (s + 1)

        @blk.tensor
        def _(t):
            t.wait_ge(lw[0], 16)
            t.wait_ge(lw[1], 16)
            t.wait_ge(lw[2], 16)
            for s in range(nst):
                ba = (s % 2) * 4096
                t.wait_ge(svv, V_PER * s + 24)
                for m in range(8):
                    if m >= 2:
                        t.wait_ge(sev, S_PER * s + (m - 1))
                    elif s >= 1:
                        t.wait_ge(sev, S_PER * (s - 1) + 8 + 8)
                    for c in range(8):
                        mm = t.matmul(PK[m % 2][:, :],
                                      WkB[:, c * 1024 + m * 128:c * 1024 + (m + 1) * 128],
                                      xkb[:, ba + c * N:ba + (c + 1) * N],
                                      start=(c == 0), stop=(c == 7))
                    mm.then_inc(smm, 1)
                for m in range(8):
                    if m >= 2:
                        t.wait_ge(sev, S_PER * s + 8 + (m - 1))
                    for c in range(8):
                        mm = t.matmul(PR[m % 2][:, :],
                                      WrB[:, c * 1024 + m * 128:c * 1024 + (m + 1) * 128],
                                      xrb[:, ba + c * N:ba + (c + 1) * N],
                                      start=(c == 0), stop=(c == 7))
                    mm.then_inc(smm, 1)
                t.wait_ge(svv, V_PER * s + 32)
                for m in range(8):
                    if m >= 2:
                        t.wait_ge(svv, V_PER * s + 32 + 2 * (m - 1) + 1)
                    for c in range(8):
                        mm = t.matmul(PV[m % 2][:, :],
                                      WvB[:, c * 1024 + m * 128:c * 1024 + (m + 1) * 128],
                                      k2b[:, c * N:(c + 1) * N],
                                      start=(c == 0), stop=(c == 7))
                    mm.then_inc(smm, 1)
    return

def emit_C(E, src="s0", dst="t0"):
    nc = E.nc
    N = SW
    srcr = E.d[src].rearrange("(c p) t -> p c t", p=128)
    dstr = E.d[dst].rearrange("(c p) t -> p c t", p=128)
    lw = _dsem(E, "cw", 3, kind="sw") + _dsem(E, "cwh", 1)
    lbc = _dsem(E, "cb", kind="sw")
    lxs = _dsem(E, "cx")
    stp = _dsem(E, "cs")
    qm, ce1, cv1 = E.sem("cqm"), E.sem("ce1"), E.sem("cv1")
    tmq, tve = E.sem("ctm"), E.sem("ctv")
    cm, ce, cv, mz = E.sem("ccm"), E.sem("cce"), E.sem("ccv"), E.sem("cmz")
    wm = E.sem("cwm")
    nst = NST
    from contextlib import ExitStack
    with ExitStack() as stk:
        AqkB = stk.enter_context(E.sbuf("c_aqk", [128, 8 * 128], BF16))
        AvB = stk.enter_context(E.sbuf("c_av", [128, 8 * 64], BF16))
        AoB = stk.enter_context(E.sbuf("c_ao", [64, 1024], BF16))
        idb = stk.enter_context(E.sbuf("c_idb", [128, 128], BF16))
        trib = stk.enter_context(E.sbuf("c_tri", [128, 128], BF16))
        onb = stk.enter_context(E.sbuf("c_onb", [128, 128], BF16))
        xb = stk.enter_context(E.sbuf("c_xb", [128, 2 * 4096], BF16))
        ph1 = stk.enter_context(E.sbuf("c_ph1", [128, 512], F32))
        ph2 = stk.enter_context(E.sbuf("c_ph2", [128, 512], F32))
        ph3 = stk.enter_context(E.sbuf("c_ph3", [128, 512], F32))
        qb = stk.enter_context(E.sbuf("c_qb", [64, 4096], BF16))
        kb = stk.enter_context(E.sbuf("c_kb", [64, 4096], BF16))
        vb = stk.enter_context(E.sbuf("c_vb", [64, 4096], BF16))
        ktm = stk.enter_context(E.sbuf("c_ktm", [128, 32 * 64], BF16))
        vtm = stk.enter_context(E.sbuf("c_vtm", [128, 32 * 64], BF16))
        STb = stk.enter_context(E.sbuf("c_ST", [64, 64], BF16))
        STf = stk.enter_context(E.sbuf("c_STf", [64, 64], F32))
        zf = stk.enter_context(E.sbuf("c_zf", [64, 1], F32))
        zi = stk.enter_context(E.sbuf("c_zi", [64, 1], F32))
        zb = stk.enter_context(E.sbuf("c_zb", [64, 1], BF16))
        gs = stk.enter_context(E.sbuf("c_gs", [128, 128], BF16))
        gks = stk.enter_context(E.sbuf("c_gks", [128, 128], BF16))
        gm = stk.enter_context(E.sbuf("c_gm", [128, 128], BF16))
        gkm = stk.enter_context(E.sbuf("c_gkm", [128, 128], BF16))
        dr = stk.enter_context(E.sbuf("c_dr", [1, 128], F32))
        dl = stk.enter_context(E.sbuf("c_dl", [1, 128], F32))
        drb = stk.enter_context(E.sbuf("c_drb", [1, 128], BF16))
        ob = stk.enter_context(E.sbuf("c_ob", [64, 128], BF16))
        obuf = stk.enter_context(E.sbuf("c_obuf", [64, 4096], BF16))
        xs = stk.enter_context(E.sbuf("c_xs", [128, 2 * 4096], F32))
        od = stk.enter_context(E.sbuf("c_od", [128, 2 * 4096], F32))
        with (
            E.psum("c_pqk0", [128, 512], F32) as pqk0,
            E.psum("c_pqk1", [128, 512], F32) as pqk1,
            E.psum("c_pv0", [64, 512], F32) as pv0,
            E.psum("c_pv1", [64, 512], F32) as pv1,
            nc.Block() as blk,
        ):
            PQK = [pqk0, pqk1]; PV = [pv0, pv1]

            @blk.gpsimd
            def _(g):
                g.dma_start(AqkB[:, :].rearrange("p (c n) -> p c n", n=128),
                            E.d["Aqk"].rearrange("(c p) n -> p c n", p=128)).then_inc(lw[0], 16)
                g.dma_start(AvB[:, :].rearrange("p (c n) -> p c n", n=64),
                            E.d["Av"].rearrange("(c p) n -> p c n", p=128)).then_inc(lw[1], 16)
                g.dma_start(AoB[:, :], E.d["Ao"][:, :]).then_inc(lw[2], 16)
                for s in range(nst):
                    if s >= 2:
                        g.wait_ge(qm, 2 * (s - 1))
                    g.dma_start(xb[:, (s % 2) * 4096:(s % 2 + 1) * 4096].rearrange("p (c t) -> p c t", t=N),
                                srcr[:, :, s * N:(s + 1) * N]).then_inc(lbc[s % 2], 16)

            @blk.sync
            def _(sy):
                sy.dma_start(idb[:, :], E.d["identb"][:, :]).then_inc(lw[3], 16)
                sy.dma_start(trib[:, :], E.d["tri"][:, :]).then_inc(lw[3], 16)
                sy.dma_start(onb[:, :], E.d["onesb"][:, :]).then_inc(lw[3], 16)

            @blk.tensor
            def _(t):
                t.wait_ge(lw[0], 16)
                t.wait_ge(lw[1], 16)
                for s in range(nst):
                    ba = (s % 2) * 4096
                    t.wait_ge(lbc[s % 2], 16 * (s // 2 + 1))
                    if s >= 2:
                        t.wait_ge(ce1, 4 * (s - 2) + 3)
                    for c in range(8):
                        mm = t.matmul(PQK[s % 2][:, :], AqkB[:, c * 128:(c + 1) * 128],
                                      xb[:, ba + c * N:ba + (c + 1) * N],
                                      start=(c == 0), stop=(c == 7))
                    mm.then_inc(qm, 1)
                    if s >= 2:
                        t.wait_ge(ce1, 4 * (s - 2) + 4)
                    for c in range(8):
                        mm = t.matmul(PV[s % 2][:, :], AvB[:, c * 64:(c + 1) * 64],
                                      xb[:, ba + c * N:ba + (c + 1) * N],
                                      start=(c == 0), stop=(c == 7))
                    mm.then_inc(qm, 1)

            @blk.scalar
            def _(s_):
                for s in range(nst):
                    s_.wait_ge(qm, 2 * s + 1)
                    if s >= 1:
                        s_.wait_ge(cv1, 2 * s)
                    s_.activation(ph1[:, :], PQK[s % 2][:, :], AF.Relu, scale=-1.0).then_inc(ce1, 1)
                    s_.wait_ge(ce1, 4 * s + 1)
                    s_.activation(ph2[:, :], ph1[:, :], AF.Exp, scale=-1.0).then_inc(ce1, 1)
                    s_.activation(ph3[:, :], PQK[s % 2][:, :], AF.Relu).then_inc(ce1, 1)
                    s_.wait_ge(qm, 2 * s + 2)
                    s_.copy(vb[:, s * N:(s + 1) * N], PV[s % 2][:, :]).then_inc(ce1, 1)

            @blk.vector
            def _(v):
                for s in range(nst):
                    v.wait_ge(ce1, 4 * s + 3)
                    v.tensor_tensor(qb[:, s * N:(s + 1) * N], ph2[0:64, :], ph3[0:64, :],
                                    ALU.add).then_inc(cv1, 1)
                    v.tensor_tensor(kb[:, s * N:(s + 1) * N], ph2[64:128, :], ph3[64:128, :],
                                    ALU.add).then_inc(cv1, 1)

        with (
            E.psum("c_pt0", [128, 64], BF16) as pt0,
            E.psum("c_pt1", [128, 64], BF16) as pt1,
            nc.Block() as blk2,
        ):
            PT = [pt0, pt1]

            @blk2.tensor
            def _(t):
                t.wait_ge(lw[3], 48)
                for k in range(64):
                    c = k // 2
                    srcb = kb if k % 2 == 0 else vb
                    if k >= 2:
                        t.wait_ge(tve, k - 1)
                    t.transpose(PT[k % 2][:, :], srcb[:, c * 128:(c + 1) * 128],
                                idb[0:64, 0:64]).then_inc(tmq, 1)

            @blk2.vector
            def _(v):
                for k in range(64):
                    c = k // 2
                    dstb = ktm if k % 2 == 0 else vtm
                    v.wait_ge(tmq, k + 1)
                    v.tensor_copy(dstb[:, c * 64:(c + 1) * 64], PT[k % 2][:, :]).then_inc(tve, 1)

        with (
            E.psum("c_pg", [128, 128], F32) as pg,
            E.psum("c_pgk", [128, 128], F32) as pgk,
            E.psum("c_pO", [64, 128], F32) as pO,
            E.psum("c_pd", [1, 128], F32) as pd,
            E.psum("c_pS", [64, 64], F32) as pS,
            E.psum("c_pbc", [64, 128], F32) as pbc,
            nc.Block() as blk3,
        ):
            @blk3.vector
            def _(v):
                v.memset(STf[:, :], 0.0).then_inc(mz, 1)
                v.memset(STb[:, :], 0.0).then_inc(mz, 1)
                v.memset(zf[:, :], 0.0).then_inc(mz, 1)
                v.memset(zb[:, :], 0.0).then_inc(mz, 1)
                v.wait_ge(mz, 4)
                for c in range(32):
                    v.wait_ge(ce, 5 * c + 1)
                    v.tensor_tensor(gm[:, :], gs[:, :], trib[:, :], ALU.mult).then_inc(cv, 1)
                    v.wait_ge(ce, 5 * c + 2)
                    v.tensor_tensor(gkm[:, :], gks[:, :], trib[:, :], ALU.mult).then_inc(cv, 1)
                    v.wait_ge(cm, 8 * c + 7)
                    v.tensor_tensor(STf[:, :], STf[:, :], pS[:, :], ALU.add).then_inc(cv, 1)
                    v.wait_ge(cv, 10 * c + 3)
                    v.tensor_copy(STb[:, :], STf[:, :]).then_inc(cv, 1)
                    v.reduce_sum(zi[:, :], kb[:, c * 128:(c + 1) * 128], axis=AX.X).then_inc(cv, 1)
                    v.wait_ge(cv, 10 * c + 5)
                    v.tensor_tensor(zf[:, :], zf[:, :], zi[:, :], ALU.add).then_inc(cv, 1)
                    v.wait_ge(cv, 10 * c + 6)
                    v.wait_ge(cm, 8 * c + 5)
                    v.tensor_copy(zb[:, :], zf[:, :]).then_inc(cv, 1)
                    v.wait_ge(cm, 8 * c + 6)
                    v.tensor_scalar(dr[:, :], pd[:, :], 1e-6, None, ALU.max).then_inc(cv, 1)
                    v.wait_ge(ce, 5 * c + 4)
                    v.tensor_copy(drb[:, :], dl[:, :]).then_inc(cv, 1)
                    v.wait_ge(ce, 5 * c + 5)
                    v.wait_ge(cm, 8 * c + 8)
                    v.tensor_tensor(obuf[:, c * 128:(c + 1) * 128], ob[:, :], pbc[:, :],
                                    ALU.mult).then_inc(cv, 1)

            @blk3.scalar
            def _(s_):
                for c in range(32):
                    s_.wait_ge(cm, 8 * c + 1)
                    if c >= 1:
                        s_.wait_ge(cv, 10 * (c - 1) + 1)
                    s_.copy(gs[:, :], pg[:, :]).then_inc(ce, 1)
                    s_.wait_ge(cm, 8 * c + 2)
                    if c >= 1:
                        s_.wait_ge(cv, 10 * (c - 1) + 2)
                    s_.copy(gks[:, :], pgk[:, :]).then_inc(ce, 1)
                    s_.wait_ge(cv, 10 * c + 8)
                    s_.activation(dl[:, :], dr[:, :], AF.Ln).then_inc(ce, 1)
                    s_.wait_ge(ce, 5 * c + 3)
                    s_.activation(dl[:, :], dl[:, :], AF.Exp, scale=-1.0).then_inc(ce, 1)
                    s_.wait_ge(cm, 8 * c + 4)
                    if c >= 1:
                        s_.wait_ge(cv, 10 * (c - 1) + 10)
                    s_.copy(ob[:, :], pO[:, :]).then_inc(ce, 1)

            @blk3.tensor
            def _(t):
                t.wait_ge(mz, 4)
                for c in range(32):
                    if c >= 1:
                        t.wait_ge(ce, 5 * (c - 1) + 1)
                    t.matmul(pg[:, :], vb[:, c * 128:(c + 1) * 128], qb[:, c * 128:(c + 1) * 128],
                             start=True, stop=True).then_inc(cm, 1)
                    if c >= 1:
                        t.wait_ge(ce, 5 * (c - 1) + 2)
                    t.matmul(pgk[:, :], kb[:, c * 128:(c + 1) * 128], qb[:, c * 128:(c + 1) * 128],
                             start=True, stop=True).then_inc(cm, 1)
                    if c >= 1:
                        t.wait_ge(cv, 10 * (c - 1) + 4)
                        t.wait_ge(ce, 5 * (c - 1) + 5)
                    t.matmul(pO[:, :], STb[:, :], qb[:, c * 128:(c + 1) * 128],
                             start=True, stop=False).then_inc(cm, 1)
                    t.wait_ge(cv, 10 * c + 1)
                    t.matmul(pO[:, :], ktm[:, c * 64:(c + 1) * 64], gm[:, :],
                             start=False, stop=True).then_inc(cm, 1)
                    if c >= 1:
                        t.wait_ge(cv, 10 * (c - 1) + 7)
                        t.wait_ge(cv, 10 * (c - 1) + 8)
                    t.matmul(pd[:, :], zb[:, :], qb[:, c * 128:(c + 1) * 128],
                             start=True, stop=False).then_inc(cm, 1)
                    t.wait_ge(cv, 10 * c + 2)
                    t.matmul(pd[:, :], onb[:, 0:1], gkm[:, :],
                             start=False, stop=True).then_inc(cm, 1)
                    if c >= 1:
                        t.wait_ge(cv, 10 * (c - 1) + 3)
                    t.matmul(pS[:, :], vtm[:, c * 64:(c + 1) * 64], ktm[:, c * 64:(c + 1) * 64],
                             start=True, stop=True).then_inc(cm, 1)
                    t.wait_ge(cv, 10 * c + 9)
                    t.matmul(pbc[:, :], onb[0:1, 0:64], drb[:, :], start=True,
                             stop=True).then_inc(cm, 1)

        with (
            E.psum("c_pw0", [128, 512], F32) as pw0,
            E.psum("c_pw1", [128, 512], F32) as pw1,
            nc.Block() as blk4,
        ):
            PW = [pw0, pw1]
            wv = E.sem("cwv")

            @blk4.sync
            def _(sy):
                for s in range(nst):
                    if s >= 2:
                        sy.wait_ge(wv, 8 * (s - 1))
                    sy.dma_start(xs[:, (s % 2) * 4096:(s % 2 + 1) * 4096].rearrange("p (c t) -> p c t", t=N),
                                 srcr[:, :, s * N:(s + 1) * N]).then_inc(lxs[s % 2], 16)
                    if s >= 1:
                        sy.wait_ge(wv, 8 * s)
                        sy.dma_start(dstr[:, :, (s - 1) * N:s * N],
                                     od[:, ((s - 1) % 2) * 4096:((s - 1) % 2 + 1) * 4096].rearrange(
                                         "p (c t) -> p c t", t=N)).then_inc(stp[(s - 1) % 2], 16)
                sy.wait_ge(wv, 8 * nst)
                sy.dma_start(dstr[:, :, (nst - 1) * N:nst * N],
                             od[:, ((nst - 1) % 2) * 4096:((nst - 1) % 2 + 1) * 4096].rearrange(
                                 "p (c t) -> p c t", t=N)).then_inc(stp[(nst - 1) % 2], 16)
                sy.wait_ge(stp[0], 16 * (nst // 2))
                sy.wait_ge(stp[1], 16 * (nst // 2))

            @blk4.tensor
            def _(t):
                t.wait_ge(lw[2], 16)
                for s in range(nst):
                    for m in range(8):
                        k = 8 * s + m
                        if k >= 2:
                            t.wait_ge(wv, k - 1)
                        t.matmul(PW[k % 2][:, :], AoB[:, m * 128:(m + 1) * 128],
                                 obuf[:, s * N:(s + 1) * N], start=True, stop=True).then_inc(wm, 1)

            @blk4.vector
            def _(v):
                for s in range(nst):
                    ba = (s % 2) * 4096
                    v.wait_ge(lxs[s % 2], 16 * (s // 2 + 1))
                    if s >= 2:
                        v.wait_ge(stp[s % 2], 16 * (s // 2))
                    for m in range(8):
                        k = 8 * s + m
                        v.wait_ge(wm, k + 1)
                        v.tensor_tensor(od[:, ba + m * N:ba + (m + 1) * N], PW[k % 2][:, :],
                                        xs[:, ba + m * N:ba + (m + 1) * N], ALU.add).then_inc(wv, 1)
    return

def emit_D(E, src="s1", dst="u0"):
    nc = E.nc
    srcr = E.d[src].rearrange("(c p) t -> p c t", p=128)
    u0 = E.d[dst]
    from contextlib import ExitStack
    with ExitStack() as stk:
        gates = stk.enter_context(E.sbuf("d_gates", [32, 4096], BF16))
        IndB = stk.enter_context(E.sbuf("d_ind", [32, 4096], BF16))
        lxs = _dsem(E, "dx")
        lcw = _dsem(E, "dc", 2)
        gm1, ge1, gv1 = E.sem("dgm"), E.sem("dge"), E.sem("dgv")
        with ExitStack() as stk1:
            xs = stk1.enter_context(E.sbuf("d_xs", [128, 2 * 4096], F32))
            SelB = stk1.enter_context(E.sbuf("d_sel", [128, 8 * 32], F32))
            gbuf = stk1.enter_context(E.sbuf("d_g", [128, 2 * 32], F32))
            lbuf = stk1.enter_context(E.sbuf("d_l", [128, 2 * 32], F32))
            gw = stk1.enter_context(E.sbuf("d_gw", [128, 2 * 32], F32))
            mcol = stk1.enter_context(E.sbuf("d_m", [128, 2], F32))
            keep = stk1.enter_context(E.sbuf("d_keep", [128, 2 * 32], F32))
            gfb = stk1.enter_context(E.sbuf("d_gfb", [128, 2 * 32], BF16))
            pl0 = stk1.enter_context(E.psum("d_pl0", [128, 32], F32))
            pl1 = stk1.enter_context(E.psum("d_pl1", [128, 32], F32))
            idb = stk1.enter_context(E.sbuf("d_idb", [128, 128], BF16))
            pt0 = stk1.enter_context(E.psum("d_pt0", [32, 128], BF16))
            pt1 = stk1.enter_context(E.psum("d_pt1", [32, 128], BF16))
            with nc.Block() as blk:
                PL = [pl0, pl1]
                PT = [pt0, pt1]

                @blk.sync
                def _(sy):
                    sy.dma_start(SelB[:, :].rearrange("p (c n) -> p c n", n=32),
                                 E.d["SelT"].rearrange("(c p) n -> p c n", p=128)).then_inc(lcw[0], 16)
                    sy.dma_start(IndB[:, :], E.d["IndE"][:, :]).then_inc(lcw[1], 16)
                    sy.dma_start(idb[:, :], E.d["identb"][:, :]).then_inc(lcw[1], 16)
                    for s in range(NST):
                        if s >= 2:
                            sy.wait_ge(gm1, 8 * (s - 1))
                        sy.dma_start(xs[:, (s % 2) * 4096:(s % 2 + 1) * 4096].rearrange("p (c t) -> p c t", t=512),
                                     srcr[:, :, s * 512:(s + 1) * 512]).then_inc(lxs[s % 2], 16)

                @blk.tensor
                def _(t):
                    t.wait_ge(lcw[0], 16)
                    t.wait_ge(lcw[1], 32)
                    for s in range(NST):
                        for tt in range(4):
                            K = s * 4 + tt
                            t.wait_ge(lxs[s % 2], 16 * (s // 2 + 1))
                            if K >= 2:
                                t.wait_ge(ge1, 3 * (K - 2) + 1 if K >= 3 else 2)
                            for c in range(8):
                                mm = t.matmul(PL[K % 2][:, :], xs[:, (s % 2) * 4096 + c * 512 + tt * 128:
                                                                  (s % 2) * 4096 + c * 512 + (tt + 1) * 128],
                                              SelB[:, c * 32:(c + 1) * 32],
                                              start=(c == 0), stop=(c == 7))
                            mm.then_inc(gm1, 1)
                            t.wait_ge(gv1, 14 * K)
                            if K >= 1:
                                KK = K - 1
                                if KK >= 2:
                                    t.wait_ge(ge1, 3 * KK - 1)
                                t.transpose(PT[KK % 2][:, :], gfb[:, (KK % 2) * 32:(KK % 2 + 1) * 32],
                                            idb[:, :]).then_inc(gm1, 1)
                    t.wait_ge(gv1, 14 * 32)
                    KK = 31
                    t.wait_ge(ge1, 3 * KK - 1)
                    t.transpose(PT[KK % 2][:, :], gfb[:, (KK % 2) * 32:(KK % 2 + 1) * 32],
                                idb[:, :]).then_inc(gm1, 1)

                @blk.scalar
                def _(s_):
                    for K in range(32):
                        s_.wait_ge(gm1, 2 * K + 1)
                        if K >= 2:
                            s_.wait_ge(gv1, 14 * (K - 2) + 14)
                        s_.activation(gbuf[:, (K % 2) * 32:(K % 2 + 1) * 32], PL[K % 2][:, :],
                                      AF.Sigmoid).then_inc(ge1, 1)
                        s_.copy(lbuf[:, (K % 2) * 32:(K % 2 + 1) * 32], PL[K % 2][:, :]).then_inc(ge1, 1)
                        if K >= 1:
                            KK = K - 1
                            s_.copy(gates[:, KK * 128:(KK + 1) * 128], PT[KK % 2][:, :]).then_inc(ge1, 1)
                    s_.wait_ge(gm1, 64)
                    s_.copy(gates[:, 31 * 128:32 * 128], PT[31 % 2][:, :]).then_inc(ge1, 1)

                @blk.vector
                def _(v):
                    for K in range(32):
                        b = (K % 2) * 32
                        myv = 14 * K
                        v.wait_ge(ge1, 3 * K + 2)
                        v.tensor_copy(gw[:, b:b + 32], lbuf[:, b:b + 32]).then_inc(gv1, 1)
                        myv += 1
                        for it in range(4):
                            v.wait_ge(gv1, myv)
                            v.reduce_max(mcol[:, K % 2:K % 2 + 1], gw[:, b:b + 32],
                                         axis=AX.X).then_inc(gv1, 1)
                            myv += 1
                            if it < 3:
                                v.wait_ge(gv1, myv)
                                v.tensor_scalar(keep[:, b:b + 32], gw[:, b:b + 32],
                                                mcol[:, K % 2:K % 2 + 1], None, ALU.is_ge).then_inc(gv1, 1)
                                myv += 1
                                v.wait_ge(gv1, myv)
                                v.scalar_tensor_tensor(gw[:, b:b + 32], keep[:, b:b + 32], -1e30,
                                                       gw[:, b:b + 32], ALU.mult, ALU.add).then_inc(gv1, 1)
                                myv += 1
                        v.wait_ge(gv1, myv)
                        v.tensor_scalar(keep[:, b:b + 32], lbuf[:, b:b + 32],
                                        mcol[:, K % 2:K % 2 + 1], None, ALU.is_ge).then_inc(gv1, 1)
                        myv += 1
                        v.wait_ge(gv1, myv)
                        v.tensor_tensor(keep[:, b:b + 32], gbuf[:, b:b + 32], keep[:, b:b + 32],
                                        ALU.mult).then_inc(gv1, 1)
                        myv += 1
                        v.wait_ge(gv1, myv)
                        v.tensor_copy(gfb[:, b:b + 32], keep[:, b:b + 32]).then_inc(gv1, 1)
                        myv += 1
                        assert myv == 14 * (K + 1), (myv, K)

        NG, GE, NSP, SWD = 4, 8, 16, 256
        wk = _dsem(E, "dwk", 8, kind="sw")
        wv = _dsem(E, "dwv", 8, kind="sw")
        lxb = _dsem(E, "dxb", kind="sw")
        du = _dsem(E, "du", NSP, kind="sw")
        km, bm, vm = E.sem("dkm"), E.sem("dbm"), E.sem("dvm")
        sev, vv, vo = E.sem("dse"), E.sem("dvv"), E.sem("dvo")
        with ExitStack() as stk2:
            KeB = stk2.enter_context(E.sbuf("d_ke", [128, 8 * 4096], BF16))
            VeB = stk2.enter_context(E.sbuf("d_ve", [128, 8 * 4096], BF16))
            xb = stk2.enter_context(E.sbuf("d_xb", [128, 2 * 2048], BF16))
            scb = stk2.enter_context(E.sbuf("d_scb", [128, 2 * 1024], BF16))
            scg = stk2.enter_context(E.sbuf("d_scg", [128, 2 * 1024], BF16))
            odb = stk2.enter_context(E.sbuf("d_od", [128, 2 * 2048], F32))
            po = [stk2.enter_context(E.psum(f"d_po{j}", [128, 512], F32)) for j in range(4)]
            psc = [stk2.enter_context(E.psum(f"d_psc{j}", [128, 256], F32)) for j in range(3)]
            pgb = stk2.enter_context(E.psum("d_pgb", [128, 256], F32))
            with nc.Block() as blk2:
                Keys3 = E.d["Keys"]
                Vals3 = E.d["Values"]

                @blk2.gpsimd
                def _(g):
                    for gi in range(NG):
                        for ee in range(GE):
                            e = gi * GE + ee
                            if gi >= 1:
                                g.wait_ge(km, 32 * 16 * gi)
                                g.wait_ge(vm, 8 * 16 * gi)
                            g.dma_start(KeB[:, ee * 4096:(ee + 1) * 4096].rearrange("p (c f) -> p c f", f=512),
                                        Keys3[e].rearrange("(c p) f -> p c f", p=128)).then_inc(wk[ee], 16)
                            g.dma_start(VeB[:, ee * 4096:(ee + 1) * 4096].rearrange("p (f v) -> p f v", v=1024),
                                        Vals3[e].rearrange("(f p) v -> p f v", p=128)).then_inc(wv[ee], 16)
                        for s in range(NSP):
                            P = gi * NSP + s
                            if P >= 2:
                                g.wait_ge(km, 32 * (P - 1))
                            g.dma_start(xb[:, (P % 2) * 2048:(P % 2 + 1) * 2048].rearrange("p (c t) -> p c t", t=SWD),
                                        srcr[:, :, s * SWD:(s + 1) * SWD]).then_inc(lxb[P % 2], 16)
                            if P >= 1:
                                PP = P - 1
                                gp, sp = divmod(PP, NSP)
                                g.wait_ge(vo, 4 * (PP + 1))
                                if gp >= 1:
                                    g.wait_ge(du[sp], 64 * gp)
                                for j in range(4):
                                    tsub, vh = divmod(j, 2)
                                    d = u0[sp * SWD + tsub * 128: sp * SWD + (tsub + 1) * 128,
                                           vh * 512:(vh + 1) * 512]
                                    srcap = odb[:, (PP % 2) * 2048 + j * 512:(PP % 2) * 2048 + (j + 1) * 512]
                                    if gp == 0:
                                        g.dma_start(d, srcap).then_inc(du[sp], 16)
                                    else:
                                        g.dma_start(d, srcap, accum_op=ALU.add).then_inc(du[sp], 16)
                    PP = NG * NSP - 1
                    gp, sp = divmod(PP, NSP)
                    g.wait_ge(vo, 4 * (PP + 1))
                    g.wait_ge(du[sp], 64 * gp)
                    for j in range(4):
                        tsub, vh = divmod(j, 2)
                        d = u0[sp * SWD + tsub * 128: sp * SWD + (tsub + 1) * 128,
                               vh * 512:(vh + 1) * 512]
                        g.dma_start(d, odb[:, (PP % 2) * 2048 + j * 512:(PP % 2) * 2048 + (j + 1) * 512],
                                    accum_op=ALU.add).then_inc(du[sp], 16)
                    for sp in range(NSP):
                        g.wait_ge(du[sp], 64 * NG)

                @blk2.tensor
                def _(t):
                    for gi in range(NG):
                        for s in range(NSP):
                            P = gi * NSP + s
                            t.wait_ge(lxb[P % 2], 16 * (P // 2 + 1))
                            for step in range(GE + 1):
                                if step < GE:
                                    ee = step
                                    EIX = P * 8 + ee
                                    t.wait_ge(wk[ee], 16 * (gi + 1))
                                    for fc in range(4):
                                        KIX = EIX * 4 + fc
                                        if KIX >= 3:
                                            t.wait_ge(sev, KIX - 2)
                                        for c in range(8):
                                            mm = t.matmul(psc[KIX % 3][:, :],
                                                          KeB[:, ee * 4096 + c * 512 + fc * 128:
                                                              ee * 4096 + c * 512 + (fc + 1) * 128],
                                                          xb[:, (P % 2) * 2048 + c * SWD:(P % 2) * 2048 + (c + 1) * SWD],
                                                          start=(c == 0), stop=(c == 7))
                                        mm.then_inc(km, 1)
                                    if EIX >= 1:
                                        t.wait_ge(vv, 4 * EIX)
                                    t.matmul(pgb[:, :], IndB[:, (gi * GE + ee) * 128:(gi * GE + ee + 1) * 128],
                                             gates[:, s * SWD:(s + 1) * SWD], start=True,
                                             stop=True).then_inc(bm, 1)
                                if step >= 1:
                                    ee = step - 1
                                    EIX = P * 8 + ee
                                    t.wait_ge(wv[ee], 16 * (gi + 1))
                                    t.wait_ge(vv, 4 * (EIX + 1))
                                    if ee == 0:
                                        t.wait_ge(vo, 4 * P)
                                    for fc in range(4):
                                        for j in range(4):
                                            tsub, vh = divmod(j, 2)
                                            mm = t.matmul(po[j][:, :],
                                                     scg[:, (EIX % 2) * 1024 + fc * 256 + tsub * 128:
                                                         (EIX % 2) * 1024 + fc * 256 + (tsub + 1) * 128],
                                                     VeB[:, ee * 4096 + fc * 1024 + vh * 512:
                                                         ee * 4096 + fc * 1024 + (vh + 1) * 512],
                                                     start=(ee == 0 and fc == 0), stop=(ee == 7 and fc == 3))
                                    mm.then_inc(vm, 1)

                @blk2.scalar
                def _(s_):
                    for P in range(NG * NSP):
                        for ee in range(GE):
                            EIX = P * 8 + ee
                            for fc in range(4):
                                KIX = EIX * 4 + fc
                                s_.wait_ge(km, KIX + 1)
                                if EIX >= 2:
                                    s_.wait_ge(vv, 4 * (EIX - 1))
                                s_.activation(scb[:, (EIX % 2) * 1024 + fc * 256:(EIX % 2) * 1024 + (fc + 1) * 256],
                                              psc[KIX % 3][:, :], AF.Relu).then_inc(sev, 1)

                @blk2.vector
                def _(v):
                    for P in range(NG * NSP):
                        for ee in range(GE):
                            EIX = P * 8 + ee
                            v.wait_ge(bm, EIX + 1)
                            for fc in range(4):
                                KIX = EIX * 4 + fc
                                v.wait_ge(sev, KIX + 1)
                                if EIX >= 2:
                                    v.wait_ge(vm, EIX - 1)
                                v.tensor_tensor(scg[:, (EIX % 2) * 1024 + fc * 256:(EIX % 2) * 1024 + (fc + 1) * 256],
                                                scb[:, (EIX % 2) * 1024 + fc * 256:(EIX % 2) * 1024 + (fc + 1) * 256],
                                                pgb[:, :], ALU.mult).then_inc(vv, 1)
                        v.wait_ge(vm, 8 * (P + 1))
                        if P >= 2:
                            PP = P - 2
                            gp, sp = divmod(PP, NSP)
                            v.wait_ge(du[sp], 64 * gp + 64)
                        for j in range(4):
                            v.tensor_copy(odb[:, (P % 2) * 2048 + j * 512:(P % 2) * 2048 + (j + 1) * 512],
                                          po[j][:, :]).then_inc(vo, 1)
    return


def emit_final(E, usrc="u0", xsrc="s1", dst="out"):
    nc = E.nc
    u0 = E.d[usrc]
    xr = E.d[xsrc].rearrange("(c p) t -> p c t", p=128)
    out = E.d[dst]
    lc = E.sem("flc")
    lu, lx, stp = _dsem(E, "flu"), _dsem(E, "flx"), _dsem(E, "fst")
    fm, fe, fv = E.sem("ffm"), E.sem("ffe"), E.sem("ffv")
    V_PER, S_PER = 15, 4
    from contextlib import ExitStack
    with ExitStack() as stk:
        ub = stk.enter_context(E.sbuf("f_u", [128, 2 * 1024], F32))
        xc = stk.enter_context(E.sbuf("f_x", [128, 2 * 1024], F32))
        sm = stk.enter_context(E.sbuf("f_s", [128, 2 * 1024], F32))
        jk = stk.enter_context(E.sbuf("f_j", [128, 2048], F32))
        idf = stk.enter_context(E.sbuf("f_id", [128, 128], F32))
        cols = stk.enter_context(E.sbuf("f_c", [128, 8], F32))
        ob = stk.enter_context(E.sbuf("f_o", [128, 2 * 1024], F32))
        pt0 = stk.enter_context(E.psum("f_pt0", [128, 128], F32))
        pt1 = stk.enter_context(E.psum("f_pt1", [128, 128], F32))
        with nc.Block() as blk:
            PT = [pt0, pt1]

            @blk.sync
            def _(sy):
                sy.dma_start(idf[:, :], E.d["identf"][:, :]).then_inc(lc, 16)
                for tt in range(32):
                    if tt >= 2:
                        sy.wait_ge(fv, V_PER * (tt - 1))
                    sy.dma_start(ub[:, (tt % 2) * 1024:(tt % 2 + 1) * 1024],
                                 u0[tt * 128:(tt + 1) * 128, :]).then_inc(lu[tt % 2], 16)
                    sy.dma_start(xc[:, (tt % 2) * 1024:(tt % 2 + 1) * 1024].rearrange("p (c t) -> p c t", t=128),
                                 xr[:, :, tt * 128:(tt + 1) * 128]).then_inc(lx[tt % 2], 16)
                    if tt >= 1:
                        sy.wait_ge(fv, V_PER * tt)
                        sy.dma_start(out[(tt - 1) * 128:tt * 128, :],
                                     ob[:, ((tt - 1) % 2) * 1024:((tt - 1) % 2 + 1) * 1024]
                                     ).then_inc(stp[(tt - 1) % 2], 16)
                sy.wait_ge(fv, V_PER * 32)
                sy.dma_start(out[31 * 128:32 * 128, :],
                             ob[:, (31 % 2) * 1024:(31 % 2 + 1) * 1024]).then_inc(stp[31 % 2], 16)
                sy.wait_ge(stp[0], 16 * 16)
                sy.wait_ge(stp[1], 16 * 16)

            @blk.tensor
            def _(t):
                t.wait_ge(lc, 16)
                for tt in range(32):
                    t.wait_ge(lx[tt % 2], 16 * (tt // 2 + 1))
                    for c in range(8):
                        k = tt * 8 + c
                        if k >= 2:
                            kk = k - 2
                            t.wait_ge(fv, V_PER * (kk // 8) + (kk % 8) + 1)
                        t.transpose(PT[k % 2][:, :],
                                    xc[:, (tt % 2) * 1024 + c * 128:(tt % 2) * 1024 + (c + 1) * 128],
                                    idf[:, :]).then_inc(fm, 1)

            @blk.vector
            def _(v):
                for tt in range(32):
                    base = (tt % 2) * 1024
                    myv = V_PER * tt
                    v.wait_ge(lu[tt % 2], 16 * (tt // 2 + 1))
                    for c in range(8):
                        k = tt * 8 + c
                        v.wait_ge(fm, k + 1)
                        v.tensor_tensor(sm[:, base + c * 128:base + (c + 1) * 128],
                                        ub[:, base + c * 128:base + (c + 1) * 128],
                                        PT[k % 2][:, :], ALU.add).then_inc(fv, 1)
                        myv += 1
                    v.wait_ge(fe, S_PER * tt + 2)
                    v.tensor_scalar(cols[:, 0:1], cols[:, 0:1], 1.0 / 1024.0, None, ALU.mult).then_inc(fv, 1)
                    myv += 1
                    v.wait_ge(fv, myv)
                    v.tensor_scalar(cols[:, 1:2], cols[:, 1:2], 1.0 / 1024.0, None, ALU.mult).then_inc(fv, 1)
                    myv += 1
                    v.wait_ge(fv, myv)
                    v.scalar_tensor_tensor(cols[:, 2:3], cols[:, 0:1], -1.0, cols[:, 0:1],
                                           ALU.mult, ALU.mult).then_inc(fv, 1)
                    myv += 1
                    v.wait_ge(fv, myv)
                    v.tensor_tensor(cols[:, 3:4], cols[:, 1:2], cols[:, 2:3], ALU.add).then_inc(fv, 1)
                    myv += 1
                    v.wait_ge(fv, myv)
                    v.tensor_scalar(cols[:, 3:4], cols[:, 3:4], 1e-5, None, ALU.add).then_inc(fv, 1)
                    myv += 1
                    v.wait_ge(fe, S_PER * tt + 4)
                    v.scalar_tensor_tensor(cols[:, 5:6], cols[:, 0:1], -1.0, cols[:, 4:5],
                                           ALU.mult, ALU.mult).then_inc(fv, 1)
                    myv += 1
                    if tt >= 2:
                        v.wait_ge(stp[tt % 2], 16 * (tt // 2))
                    v.wait_ge(fv, myv)
                    v.tensor_scalar(ob[:, base:base + 1024], sm[:, base:base + 1024],
                                    cols[:, 4:5], cols[:, 5:6], ALU.mult, ALU.add).then_inc(fv, 1)
                    myv += 1
                    assert myv == V_PER * (tt + 1), (myv, tt)

            @blk.scalar
            def _(s_):
                for tt in range(32):
                    base = (tt % 2) * 1024
                    s_.wait_ge(fv, V_PER * tt + 8)
                    if tt >= 1:
                        s_.wait_ge(fe, S_PER * (tt - 1) + 2)
                    s_.activation(jk[:, 0:1024], sm[:, base:base + 1024], AF.Copy,
                                  accum_out=cols[:, 0:1]).then_inc(fe, 1)
                    s_.wait_ge(fe, S_PER * tt + 1)
                    s_.activation(jk[:, 1024:2048], sm[:, base:base + 1024], AF.Square,
                                  accum_out=cols[:, 1:2]).then_inc(fe, 1)
                    s_.wait_ge(fv, V_PER * tt + 13)
                    s_.activation(cols[:, 4:5], cols[:, 3:4], AF.Ln).then_inc(fe, 1)
                    s_.wait_ge(fe, S_PER * tt + 3)
                    s_.activation(cols[:, 4:5], cols[:, 4:5], AF.Exp, scale=-0.5).then_inc(fe, 1)
    return

def emit_A(E, src="s0", dst="t0"):
    nc = E.nc
    N = SW
    srcr = E.d[src].rearrange("(c p) t -> p c t", p=128)
    dstr = E.d[dst].rearrange("(c p) t -> p c t", p=128)
    qsr, ksr, vsr = E.d["qs"], E.d["ks"], E.d["vs"]
    ssqT = E.d["ssqT"]
    from contextlib import ExitStack
    with ExitStack() as stk:
        O_all = stk.enter_context(E.sbuf("a_oall", [128, 8 * 4096], BF16))
        WoB = stk.enter_context(E.sbuf("a_wo", [128, 8 * 1024], BF16))
        onb = stk.enter_context(E.sbuf("a_onb", [128, 128], BF16))
        lwo = _dsem(E, "awo", 1, kind="sw") + _dsem(E, "awoh", 2)

        lww = _dsem(E, "aww", 3, kind="sw") + _dsem(E, "awwh", 2)
        lxb = _dsem(E, "axb", kind="sw")
        asp = _dsem(E, "asp")
        am, ae, av = E.sem("aam"), E.sem("aae"), E.sem("aav")
        mz1 = E.sem("amz1")
        with ExitStack() as stk1:
            WqB = stk1.enter_context(E.sbuf("a_wq", [128, 8 * 256], BF16))
            WkB = stk1.enter_context(E.sbuf("a_wk", [128, 8 * 256], BF16))
            WvB = stk1.enter_context(E.sbuf("a_wv", [128, 8 * 1024], BF16))
            blkB = stk1.enter_context(E.sbuf("a_blk", [128, 256], BF16))
            blkT = stk1.enter_context(E.sbuf("a_blkt", [128, 256], BF16))
            xb = stk1.enter_context(E.sbuf("a_xb", [128, 2 * 4096], BF16))
            sqb = stk1.enter_context(E.sbuf("a_sqb", [128, 2 * 512], BF16))
            us = stk1.enter_context(E.sbuf("a_us", [8, 2 * 512], F32))
            Ls = stk1.enter_context(E.sbuf("a_ls", [8, 2 * 512], F32))
            rns = stk1.enter_context(E.sbuf("a_rns", [8, 2 * 512], F32))
            rnb = stk1.enter_context(E.sbuf("a_rnb", [128, 2 * 512], BF16))
            Tb = stk1.enter_context(E.sbuf("a_tb", [128, 2 * 512], BF16))
            qnb = stk1.enter_context(E.sbuf("a_qnb", [128, 2 * 512], BF16))
            pm = [stk1.enter_context(E.psum(f"a_pm{i}", [128, 512], F32)) for i in range(3)]
            pss = [stk1.enter_context(E.psum(f"a_pss{i}", [128, 512], F32)) for i in range(2)]
            pbc = [stk1.enter_context(E.psum(f"a_pbc{i}", [128, 512], F32)) for i in range(2)]
            with nc.Block() as blk:
                def wsl(j):
                    def f(c):
                        if j < 2:
                            return WqB[:, c * 256 + j * 128:c * 256 + (j + 1) * 128]
                        if j < 4:
                            return WkB[:, c * 256 + (j - 2) * 128:c * 256 + (j - 1) * 128]
                        return WvB[:, c * 1024 + (j - 4) * 128:c * 1024 + (j - 3) * 128]
                    return f

                def blk_lhs(j):
                    return blkB[:, 0:128] if j < 4 else blkB[:, 128:256]

                def blkT_lhs(j):
                    return blkT[:, 0:128] if j < 4 else blkT[:, 128:256]

                def nh(j):
                    return 8 if j < 4 else 2

                def spill_dst(j, s):
                    if j < 2:
                        return qsr[j * 128:(j + 1) * 128, s * N:(s + 1) * N]
                    if j < 4:
                        return ksr[(j - 2) * 128:(j - 1) * 128, s * N:(s + 1) * N]
                    return vsr[(j - 4) * 128:(j - 3) * 128, s * N:(s + 1) * N]

                @blk.gpsimd
                def _(g):
                    g.dma_start(WqB[:, :].rearrange("p (c n) -> p c n", n=256),
                                E.d["Wq"].rearrange("(c p) n -> p c n", p=128)).then_inc(lww[0], 16)
                    g.dma_start(WkB[:, :].rearrange("p (c n) -> p c n", n=256),
                                E.d["Wk"].rearrange("(c p) n -> p c n", p=128)).then_inc(lww[1], 16)
                    g.dma_start(WvB[:, :].rearrange("p (c n) -> p c n", n=1024),
                                E.d["Wv"].rearrange("(c p) n -> p c n", p=128)).then_inc(lww[2], 16)
                    g.dma_start(WoB[:, :].rearrange("p (v n) -> p v n", n=1024),
                                E.d["Wout"].rearrange("(v p) n -> p v n", p=128)).then_inc(lwo[0], 16)
                    for s in range(NST):
                        if s >= 2:
                            g.wait_ge(am, 36 * (s - 1))
                        g.dma_start(xb[:, (s % 2) * 4096:(s % 2 + 1) * 4096].rearrange("p (c t) -> p c t", t=N),
                                    srcr[:, :, s * N:(s + 1) * N]).then_inc(lxb[s % 2], 16)

                @blk.sync
                def _(sy):
                    sy.dma_start(blkB[:, :], E.d["blkP"][:, :]).then_inc(lww[3], 16)
                    sy.dma_start(blkT[:, :], E.d["blkTP"][:, :]).then_inc(lww[4], 16)
                    sy.dma_start(onb[:, :], E.d["onesb"][:, :]).then_inc(lwo[1], 16)
                    I_QN_S = {}
                    _c = 0
                    for M in range(98):
                        if M < 96:
                            _c += 1
                        if 1 <= M <= 96:
                            _c += 1
                        if 2 <= M <= 97:
                            _c += 1
                            I_QN_S[M - 2] = _c
                    for s in range(NST):
                        for j in range(12):
                            M = s * 12 + j
                            sy.wait_ge(av, I_QN_S[M])
                            sy.dma_start(spill_dst(j, s),
                                         qnb[:, (M % 2) * 512:(M % 2 + 1) * 512]).then_inc(asp[M % 2], 16)
                    sy.wait_ge(asp[0], 16 * 48)
                    sy.wait_ge(asp[1], 16 * 48)

                I_SQ, I_LN, I_RN, I_TB = {}, {}, {}, {}
                cnt = 0
                for M in range(97):
                    if M < 96:
                        cnt += 1; I_SQ[M] = cnt
                    if M >= 1:
                        cnt += 1; I_LN[M - 1] = cnt
                        cnt += 1; I_RN[M - 1] = cnt
                        cnt += 1; I_TB[M - 1] = cnt
                I_MX, I_RB, I_QN = {}, {}, {}
                cnt = 0
                for M in range(98):
                    if M < 96:
                        cnt += 1; I_MX[M] = cnt
                    if 1 <= M <= 96:
                        cnt += 1; I_RB[M - 1] = cnt
                    if 2 <= M <= 97:
                        cnt += 1; I_QN[M - 2] = cnt
                I_PJ, I_SS, I_BC = {}, {}, {}
                cnt = 0
                for M in range(97):
                    if M < 96:
                        cnt += 1; I_PJ[M] = cnt
                        cnt += 1; I_SS[M] = cnt
                    if M >= 1:
                        cnt += 1; I_BC[M - 1] = cnt

                @blk.tensor
                def _(t):
                    t.wait_ge(lww[0], 16)
                    t.wait_ge(lww[1], 16)
                    t.wait_ge(lww[2], 16)
                    t.wait_ge(lww[3], 16)
                    t.wait_ge(lww[4], 16)
                    for M in range(97):
                        s, j = divmod(M, 12)
                        if M < 96:
                            W = wsl(j)
                            t.wait_ge(lxb[s % 2], 16 * (s // 2 + 1))
                            if M >= 3:
                                t.wait_ge(ae, I_TB[M - 3])
                            for c in range(8):
                                mm = t.matmul(pm[M % 3][:, :], W(c),
                                              xb[:, (s % 2) * 4096 + c * N:(s % 2) * 4096 + (c + 1) * N],
                                              start=(c == 0), stop=(c == 7))
                            mm.then_inc(am, 1)
                            t.wait_ge(ae, I_SQ[M])
                            if M >= 2:
                                t.wait_ge(av, I_MX[M - 2])
                            t.matmul(pss[M % 2][:, :], blk_lhs(j),
                                     sqb[:, (M % 2) * 512:(M % 2 + 1) * 512],
                                     start=True, stop=True).then_inc(am, 1)
                        if M >= 1:
                            MM = M - 1
                            jj = MM % 12
                            t.wait_ge(av, I_RB[MM])
                            if MM >= 2:
                                t.wait_ge(av, I_QN[MM - 2])
                            t.wait_ge(mz1, 1)
                            t.matmul(pbc[MM % 2][:, :], blkT_lhs(jj),
                                     rnb[:, (MM % 2) * 512:(MM % 2 + 1) * 512],
                                     start=True, stop=True).then_inc(am, 1)

                @blk.scalar
                def _(s_):
                    for M in range(97):
                        if M < 96:
                            j = M % 12
                            s_.wait_ge(am, I_PJ[M])
                            if M >= 2:
                                s_.wait_ge(am, I_SS[M - 2])
                            s_.activation(sqb[:, (M % 2) * 512:(M % 2 + 1) * 512], pm[M % 3][:, :],
                                          AF.Square).then_inc(ae, 1)
                        if M >= 1:
                            MM = M - 1
                            jj = MM % 12
                            s_.wait_ge(av, I_MX[MM])
                            s_.activation(Ls[0:nh(jj), (MM % 2) * 512:(MM % 2 + 1) * 512],
                                          us[0:nh(jj), (MM % 2) * 512:(MM % 2 + 1) * 512],
                                          AF.Ln).then_inc(ae, 1)
                            s_.wait_ge(ae, I_LN[MM])
                            if MM >= 2:
                                s_.wait_ge(av, I_RB[MM - 2])
                            s_.activation(rns[0:nh(jj), (MM % 2) * 512:(MM % 2 + 1) * 512],
                                          Ls[0:nh(jj), (MM % 2) * 512:(MM % 2 + 1) * 512],
                                          AF.Exp, scale=-0.5).then_inc(ae, 1)
                            if MM >= 2:
                                s_.wait_ge(av, I_QN[MM - 2])
                            s_.copy(Tb[:, (MM % 2) * 512:(MM % 2 + 1) * 512],
                                    pm[MM % 3][:, :]).then_inc(ae, 1)

                @blk.vector
                def _(v):
                    v.memset(rnb[:, :], 0.0).then_inc(mz1, 1)
                    for M in range(98):
                        if M < 96:
                            j = M % 12
                            v.wait_ge(am, I_SS[M])
                            if M >= 2:
                                v.wait_ge(ae, I_LN[M - 2])
                            v.tensor_scalar(us[0:nh(j), (M % 2) * 512:(M % 2 + 1) * 512],
                                            pss[M % 2][0:nh(j), :], 1e-18, None,
                                            ALU.max).then_inc(av, 1)
                        if 1 <= M <= 96:
                            MM = M - 1
                            jj = MM % 12
                            v.wait_ge(ae, I_RN[MM])
                            v.tensor_copy(rnb[0:nh(jj), (MM % 2) * 512:(MM % 2 + 1) * 512],
                                          rns[0:nh(jj), (MM % 2) * 512:(MM % 2 + 1) * 512]).then_inc(av, 1)
                        if 2 <= M <= 97:
                            MM = M - 2
                            jj = MM % 12
                            v.wait_ge(ae, I_TB[MM])
                            v.wait_ge(am, I_BC[MM])
                            if MM >= 2:
                                v.wait_ge(asp[MM % 2], 16 * (MM // 2))
                            v.tensor_tensor(qnb[:, (MM % 2) * 512:(MM % 2 + 1) * 512],
                                            Tb[:, (MM % 2) * 512:(MM % 2 + 1) * 512],
                                            pbc[MM % 2][:, :], ALU.mult).then_inc(av, 1)

        hq, hk, hvt, hs = _dsem(E, "ahq"), _dsem(E, "ahk"), _dsem(E, "ahv"), _dsem(E, "ahs")
        fsim, fom = E.sem("afs"), E.sem("afm")
        fres, frev = E.sem("afre"), E.sem("afrv")
        fvsq, fgsq, fqs = E.sem("afvq"), E.sem("afgq"), E.sem("afqs")
        foev, fvo, fss, fvq = E.sem("afoe"), E.sem("afvo"), E.sem("afss"), E.sem("afvz")
        RELU_ENG = ['s', 'v'] * 8
        SQ_ENG = ['v', 'g', 'v', 'g', 'v', 'g', 'v', 's',
                  'v', 'g', 'v', 's', 'g', 's', 'g', 's']

        def _prefix(pat):
            d = {}
            for e in set(pat):
                c = [0]
                for x in pat:
                    c.append(c[-1] + (1 if x == e else 0))
                d[e] = c
            return d

        RP, QP = _prefix(RELU_ENG), _prefix(SQ_ENG)
        R_SEM = {'s': fres, 'v': frev}
        Q_SEM = {'s': fqs, 'v': fvsq, 'g': fgsq}

        def r_count(h8, p):
            e = RELU_ENG[p]
            return R_SEM[e], RP[e][16] * h8 + RP[e][p + 1]

        def q_count(h8, p):
            e = SQ_ENG[p]
            return Q_SEM[e], QP[e][16] * h8 + QP[e][p + 1]

        with ExitStack() as stk2:
            Qh = stk2.enter_context(E.sbuf("a_qh", [128, 2 * 4096], BF16))
            Kh = stk2.enter_context(E.sbuf("a_kh", [128, 2 * 4096], BF16))
            Vtm = stk2.enter_context(E.sbuf("a_vtm", [128, 2 * 4096], BF16))
            bkv = stk2.enter_context(E.sbuf("a_bkv", [128, 2], BF16))
            rb = stk2.enter_context(E.sbuf("a_rb", [128, 6 * 1024], BF16))
            sqq = stk2.enter_context(E.sbuf("a_sqq", [128, 6 * 1024], BF16))
            osqb = stk2.enter_context(E.sbuf("a_osq", [128, 2 * 512], BF16))
            ssqs = stk2.enter_context(E.sbuf("a_ssqs", [1, 2 * 4096], F32))
            psim = [stk2.enter_context(E.psum(f"a_psim{i}", [128, 1024], F32)) for i in range(3)]
            pO = [stk2.enter_context(E.psum("a_pO0", [128, 512], F32))]
            pq = [stk2.enter_context(E.psum("a_pq0", [1, 512], F32))]
            msA = E.sem("amsA")
            with nc.Block() as blk2:
                @blk2.sync
                def _(sy):
                    sy.wait_ge(msA, 4)
                    sy.dma_start(bkv[:, :], E.d["blkv"][:, :]).then_inc(lwo[2], 16)
                    for h in range(16):
                        if h >= 2:
                            sy.wait_ge(fsim, 128 * (h - 1))
                        sy.dma_start(Qh[0:16, (h % 2) * 4096:(h % 2 + 1) * 4096],
                                     qsr[h * 16:(h + 1) * 16, :]).then_inc(hq[h % 2], 16)
                        sy.dma_start(Kh[0:16, (h % 2) * 4096:(h % 2 + 1) * 4096],
                                     ksr[h * 16:(h + 1) * 16, :]).then_inc(hk[h % 2], 16)
                        if h >= 2:
                            sy.wait_ge(fom, 256 * (h - 1))
                        for jt in range(32):
                            sy.dma_start(Vtm[:, (h % 2) * 4096 + jt * 128:(h % 2) * 4096 + jt * 128 + 64],
                                         vsr[h * 64:(h + 1) * 64, jt * 128:(jt + 1) * 128],
                                         transpose=True).then_inc(hvt[h % 2], 16)
                        if h >= 1:
                            sy.wait_ge(fvq, 8 * h)
                            sy.dma_start(ssqT[h - 1:h, :],
                                         ssqs[:, ((h - 1) % 2) * 4096:((h - 1) % 2 + 1) * 4096]
                                         ).then_inc(hs[(h - 1) % 2], 16)
                    sy.wait_ge(fvq, 8 * 16)
                    sy.dma_start(ssqT[15:16, :], ssqs[:, (15 % 2) * 4096:(15 % 2 + 1) * 4096]
                                 ).then_inc(hs[15 % 2], 16)
                    sy.wait_ge(hs[0], 16 * 8)
                    sy.wait_ge(hs[1], 16 * 8)

                @blk2.tensor
                def _(t):
                    t.wait_ge(lwo[1], 16)
                    t.wait_ge(lwo[2], 16)
                    t.wait_ge(msA, 4)
                    pend_ssq = []

                    def emit_ssq(t):
                        h8p, hp, stp_ = pend_ssq.pop(0)
                        t.wait_ge(fvo, h8p + 1)
                        if h8p >= 1:
                            t.wait_ge(fvq, h8p)
                        t.matmul(pq[0][:, :], bkv[:, 0:1],
                                 osqb[:, (h8p % 2) * 512:(h8p % 2 + 1) * 512],
                                 start=True, stop=True).then_inc(fss, 1)

                    for h in range(16):
                        t.wait_ge(hq[h % 2], 16 * (h // 2 + 1))
                        t.wait_ge(hk[h % 2], 16 * (h // 2 + 1))
                        t.wait_ge(hvt[h % 2], 16 * 32 * (h // 2 + 1))
                        for st in range(8):
                            h8 = h * 8 + st
                            for p in range(19):
                                if p == 4 and pend_ssq:
                                    emit_ssq(t)
                                if p < 16:
                                    PIX = h8 * 16 + p
                                    if PIX >= 3:
                                        eng, cnt = r_count(*divmod(PIX - 3, 16))
                                        t.wait_ge(eng, cnt)
                                    for jj in range(2):
                                        j = 2 * p + jj
                                        mm = t.matmul(psim[p % 3][:, jj * 512:(jj + 1) * 512],
                                                 Kh[:, (h % 2) * 4096 + j * 128:(h % 2) * 4096 + (j + 1) * 128],
                                                 Qh[:, (h % 2) * 4096 + st * 512:(h % 2) * 4096 + (st + 1) * 512],
                                                 start=True, stop=True)
                                    mm.then_inc(fsim, 1)
                                if p >= 3:
                                    po_ = p - 3
                                    eng, cnt = q_count(h8, po_)
                                    t.wait_ge(eng, cnt)
                                    if po_ == 0 and h8 >= 1:
                                        t.wait_ge(foev, h8)
                                    for jj in range(2):
                                        jo = 2 * po_ + jj
                                        t.matmul(pO[0][:, :],
                                                 Vtm[:, (h % 2) * 4096 + jo * 128:(h % 2) * 4096 + (jo + 1) * 128],
                                                 sqq[:, (po_ % 6) * 1024 + jj * 512:(po_ % 6) * 1024 + (jj + 1) * 512],
                                                 start=(jo == 0), stop=(jo == 31)).then_inc(fom, 1)
                            pend_ssq.append((h8, h, st))
                        while pend_ssq:
                            emit_ssq(t)

                @blk2.scalar
                def _(s_):
                    for h in range(16):
                        for st in range(8):
                            h8 = h * 8 + st
                            for p in range(16):
                                PIX = h8 * 16 + p
                                if RELU_ENG[p] == 's':
                                    s_.wait_ge(fsim, PIX + 1)
                                    if PIX >= 6:
                                        eng, cnt = q_count(*divmod(PIX - 6, 16))
                                        s_.wait_ge(eng, cnt)
                                    s_.activation(rb[:, (p % 6) * 1024:(p % 6 + 1) * 1024],
                                                  psim[p % 3][:, :], AF.Relu).then_inc(fres, 1)
                                if SQ_ENG[p] == 's':
                                    eng, cnt = r_count(h8, p)
                                    s_.wait_ge(eng, cnt)
                                    if PIX >= 6:
                                        s_.wait_ge(fom, 2 * PIX - 10)
                                    s_.activation(sqq[:, (p % 6) * 1024:(p % 6 + 1) * 1024],
                                                  rb[:, (p % 6) * 1024:(p % 6 + 1) * 1024],
                                                  AF.Square).then_inc(fqs, 1)
                            s_.wait_ge(fom, 32 * (h8 + 1))
                            ch = h // 2
                            pbase = (h % 2) * 64
                            s_.copy(O_all[pbase:pbase + 64, ch * 4096 + st * 512:ch * 4096 + (st + 1) * 512],
                                    pO[0][0:64, :]).then_inc(foev, 1)
                            if h8 >= 1:
                                hp, stp_ = divmod(h8 - 1, 8)
                                s_.wait_ge(fss, h8)
                                if hp >= 2 and stp_ == 0:
                                    s_.wait_ge(hs[hp % 2], 16 * (hp // 2))
                                s_.copy(ssqs[:, (hp % 2) * 4096 + stp_ * 512:(hp % 2) * 4096 + (stp_ + 1) * 512],
                                        pq[0][:, :]).then_inc(fvq, 1)
                    s_.wait_ge(fss, 128)
                    s_.copy(ssqs[:, (15 % 2) * 4096 + 7 * 512:(15 % 2) * 4096 + 8 * 512],
                            pq[0][:, :]).then_inc(fvq, 1)

                @blk2.vector
                def _(v):
                    v.memset(Qh[:, :], 0.0).then_inc(msA, 1)
                    v.memset(Kh[:, :], 0.0).then_inc(msA, 1)
                    v.memset(Vtm[:, :], 0.0).then_inc(msA, 1)
                    v.memset(osqb[:, :], 0.0).then_inc(msA, 1)
                    for h in range(16):
                        for st in range(8):
                            h8 = h * 8 + st
                            for p in range(16):
                                PIX = h8 * 16 + p
                                if RELU_ENG[p] == 'v':
                                    v.wait_ge(fsim, PIX + 1)
                                    if PIX >= 6:
                                        eng, cnt = q_count(*divmod(PIX - 6, 16))
                                        v.wait_ge(eng, cnt)
                                    v.tensor_scalar(rb[:, (p % 6) * 1024:(p % 6 + 1) * 1024],
                                                    psim[p % 3][:, :], 0.0, None,
                                                    ALU.max).then_inc(frev, 1)
                                if SQ_ENG[p] == 'v':
                                    eng, cnt = r_count(h8, p)
                                    v.wait_ge(eng, cnt)
                                    if PIX >= 6:
                                        v.wait_ge(fom, 2 * PIX - 10)
                                    v.tensor_tensor(sqq[:, (p % 6) * 1024:(p % 6 + 1) * 1024],
                                                    rb[:, (p % 6) * 1024:(p % 6 + 1) * 1024],
                                                    rb[:, (p % 6) * 1024:(p % 6 + 1) * 1024],
                                                    ALU.mult).then_inc(fvsq, 1)


                @blk2.gpsimd
                def _(g):
                    g.wait_ge(msA, 4)
                    for h in range(16):
                        for st in range(8):
                            h8 = h * 8 + st
                            for p in range(16):
                                if SQ_ENG[p] != 'g':
                                    continue
                                PIX = h8 * 16 + p
                                eng, cnt = r_count(h8, p)
                                g.wait_ge(eng, cnt)
                                if PIX >= 6:
                                    g.wait_ge(fom, 2 * PIX - 10)
                                g.tensor_tensor(sqq[:, (p % 6) * 1024:(p % 6 + 1) * 1024],
                                                rb[:, (p % 6) * 1024:(p % 6 + 1) * 1024],
                                                rb[:, (p % 6) * 1024:(p % 6 + 1) * 1024],
                                                ALU.mult).then_inc(fgsq, 1)
                            g.wait_ge(foev, h8 + 1)
                            if h8 >= 2:
                                g.wait_ge(fss, h8 - 1)
                            ch = h // 2
                            pbase = (h % 2) * 64
                            g.tensor_tensor(osqb[0:64, (h8 % 2) * 512:(h8 % 2 + 1) * 512],
                                            O_all[pbase:pbase + 64, ch * 4096 + st * 512:ch * 4096 + (st + 1) * 512],
                                            O_all[pbase:pbase + 64, ch * 4096 + st * 512:ch * 4096 + (st + 1) * 512],
                                            ALU.mult).then_inc(fvo, 1)

        lfs, sfb = _dsem(E, "alf"), _dsem(E, "afb")
        a3e, a3v = E.sem("a3e"), E.sem("a3v")
        lxs = _dsem(E, "axs")
        lfb = _dsem(E, "albf")
        lb2 = E.sem("alb2")
        stp = _dsem(E, "ast")
        wfm, wvn, wm, wv = E.sem("awf"), E.sem("awn"), E.sem("awm"), E.sem("awv")
        fbT = E.d["fbT"]
        with ExitStack() as stk3a:
            fs2 = stk3a.enter_context(E.sbuf("a_fs2", [2, 2 * 4096], F32))
            fL = stk3a.enter_context(E.sbuf("a_fL", [2, 4096], F32))
            fn = stk3a.enter_context(E.sbuf("a_fn", [2, 4096], F32))
            fiv = stk3a.enter_context(E.sbuf("a_fiv", [2, 4096], F32))
            fbc = stk3a.enter_context(E.sbuf("a_fbc", [2, 2 * 4096], BF16))
            with nc.Block() as blk3a:
                @blk3a.sync
                def _(sy):
                    for q in range(8):
                        if q >= 2:
                            sy.wait_ge(a3v, 2 * (q - 1))
                        sy.dma_start(fs2[:, (q % 2) * 4096:(q % 2 + 1) * 4096],
                                     ssqT[2 * q:2 * q + 2, :]).then_inc(lfs[q % 2], 16)
                        if q >= 1:
                            sy.wait_ge(a3v, 2 * q)
                            sy.dma_start(fbT[:, (q - 1) * 4096:q * 4096],
                                         fbc[:, ((q - 1) % 2) * 4096:((q - 1) % 2 + 1) * 4096]
                                         ).then_inc(sfb[(q - 1) % 2], 16)
                    sy.wait_ge(a3v, 16)
                    sy.dma_start(fbT[:, 7 * 4096:8 * 4096],
                                 fbc[:, (7 % 2) * 4096:(7 % 2 + 1) * 4096]).then_inc(sfb[7 % 2], 16)
                    sy.wait_ge(sfb[0], 16 * 4)
                    sy.wait_ge(sfb[1], 16 * 4)

                @blk3a.vector
                def _(v):
                    for q in range(8):
                        v.wait_ge(lfs[q % 2], 16 * (q // 2 + 1))
                        if q >= 1:
                            v.wait_ge(a3e, 4 * q)
                        v.tensor_scalar(fs2[:, (q % 2) * 4096:(q % 2 + 1) * 4096],
                                        fs2[:, (q % 2) * 4096:(q % 2 + 1) * 4096], 1e-18, None,
                                        ALU.max).then_inc(a3v, 1)
                        v.wait_ge(a3e, 4 * (q + 1))
                        if q >= 2:
                            v.wait_ge(sfb[q % 2], 16 * (q // 2))
                        v.tensor_tensor(fbc[:, (q % 2) * 4096:(q % 2 + 1) * 4096],
                                        fn[:, :], fiv[:, :], ALU.mult).then_inc(a3v, 1)

                @blk3a.scalar
                def _(s_):
                    for q in range(8):
                        s_.wait_ge(a3v, 2 * q + 1)
                        s_.activation(fL[:, :], fs2[:, (q % 2) * 4096:(q % 2 + 1) * 4096],
                                      AF.Ln).then_inc(a3e, 1)
                        s_.wait_ge(a3e, 4 * q + 1)
                        s_.activation(fn[:, :], fL[:, :], AF.Exp, scale=0.5).then_inc(a3e, 1)
                        s_.wait_ge(a3e, 4 * q + 2)
                        s_.activation(fiv[:, :], fL[:, :], AF.Exp, scale=-0.5).then_inc(a3e, 1)
                        s_.wait_ge(a3e, 4 * q + 3)
                        s_.activation(fn[:, :], fn[:, :], AF.Tanh).then_inc(a3e, 1)

        with ExitStack() as stk3:
            fbs = stk3.enter_context(E.sbuf("a_fbs", [2, 2 * 4096], BF16))
            blk2B = stk3.enter_context(E.sbuf("a_blk2", [2, 128], BF16))
            Onb = stk3.enter_context(E.sbuf("a_onb2", [128, 2 * 4096], BF16))
            xs = stk3.enter_context(E.sbuf("a_xs", [128, 2 * 4096], F32))
            od = stk3.enter_context(E.sbuf("a_od", [128, 2 * 4096], F32))
            pf = [stk3.enter_context(E.psum(f"a_pf{i}", [128, 512], F32)) for i in range(2)]
            pw = [stk3.enter_context(E.psum(f"a_pw{i}", [128, 512], F32)) for i in range(2)]
            with nc.Block() as blk3:
                @blk3.sync
                def _(sy):
                    sy.dma_start(blk2B[:, :], E.d["blk2"][:, :]).then_inc(lb2, 16)
                    for s in range(NST):
                        if s >= 2:
                            sy.wait_ge(wv, 8 * (s - 1))
                        sy.dma_start(xs[:, (s % 2) * 4096:(s % 2 + 1) * 4096].rearrange("p (c t) -> p c t", t=N),
                                     srcr[:, :, s * N:(s + 1) * N]).then_inc(lxs[s % 2], 16)
                        if s >= 2:
                            sy.wait_ge(wfm, 8 * (s - 1))
                        sy.dma_start(fbs[:, (s % 2) * 4096:(s % 2 + 1) * 4096].rearrange("h (ch t) -> h ch t", t=512),
                                     fbT.rearrange("h (ch t) -> h ch t", t=4096)[:, :, s * 512:(s + 1) * 512]
                                     ).then_inc(lfb[s % 2], 16)
                        if s >= 1:
                            sy.wait_ge(wv, 8 * s)
                            sy.dma_start(dstr[:, :, (s - 1) * N:s * N],
                                         od[:, ((s - 1) % 2) * 4096:((s - 1) % 2 + 1) * 4096].rearrange(
                                             "p (c t) -> p c t", t=N)).then_inc(stp[(s - 1) % 2], 16)
                    sy.wait_ge(wv, 8 * NST)
                    sy.dma_start(dstr[:, :, (NST - 1) * N:NST * N],
                                 od[:, ((NST - 1) % 2) * 4096:((NST - 1) % 2 + 1) * 4096].rearrange(
                                     "p (c t) -> p c t", t=N)).then_inc(stp[(NST - 1) % 2], 16)
                    sy.wait_ge(stp[0], 16 * (NST // 2))
                    sy.wait_ge(stp[1], 16 * (NST // 2))

                @blk3.vector
                def _(v):
                    for s in range(NST):
                        ba = (s % 2) * 4096
                        v.wait_ge(lxs[s % 2], 16 * (s // 2 + 1))
                        if s >= 2:
                            v.wait_ge(wm, 8 * (s - 1))
                        for ch in range(8):
                            k = 8 * s + ch
                            v.wait_ge(wfm, k + 1)
                            v.tensor_tensor(Onb[:, ba + ch * 512:ba + (ch + 1) * 512],
                                            O_all[:, ch * 4096 + s * 512:ch * 4096 + (s + 1) * 512],
                                            pf[k % 2][:, :], ALU.mult).then_inc(wvn, 1)
                        if s >= 2:
                            v.wait_ge(stp[s % 2], 16 * (s // 2))
                        for dch in range(8):
                            k = 8 * s + dch
                            v.wait_ge(wm, k + 1)
                            v.tensor_tensor(od[:, ba + dch * 512:ba + (dch + 1) * 512],
                                            pw[k % 2][:, :],
                                            xs[:, ba + dch * 512:ba + (dch + 1) * 512],
                                            ALU.add).then_inc(wv, 1)

                @blk3.tensor
                def _(t):
                    t.wait_ge(lb2, 16)
                    t.wait_ge(lwo[0], 16)
                    for s in range(NST):
                        t.wait_ge(lfb[s % 2], 16 * (s // 2 + 1))
                        for ch in range(8):
                            k = 8 * s + ch
                            if k >= 2:
                                t.wait_ge(wvn, k - 1)
                            t.matmul(pf[k % 2][:, :], blk2B[:, :],
                                     fbs[:, (s % 2) * 4096 + ch * 512:(s % 2) * 4096 + (ch + 1) * 512],
                                     start=True, stop=True).then_inc(wfm, 1)
                        t.wait_ge(wvn, 8 * (s + 1))
                        for dch in range(8):
                            k = 8 * s + dch
                            if k >= 2:
                                t.wait_ge(wv, k - 1)
                            for vch in range(8):
                                mm = t.matmul(pw[k % 2][:, :],
                                              WoB[:, vch * 1024 + dch * 128:vch * 1024 + (dch + 1) * 128],
                                              Onb[:, (s % 2) * 4096 + vch * 512:(s % 2) * 4096 + (vch + 1) * 512],
                                              start=(vch == 0), stop=(vch == 7))
                            mm.then_inc(wm, 1)
    return


FUSED = True

_PIPELINE = [
    ("entry", lambda E: emit_entry(E, "x0", "s0"), {"x0"}, {"s0"}),
    ("A", lambda E: emit_A(E, "s0", "t0"), {"s0"}, {"t0"}),
    ("lnA", lambda E: emit_ln(E, "t0", "s1"), {"t0"}, {"s1"}),
    ("B", lambda E: emit_B(E, "s1", "t0"), {"s1"}, {"t0"}),
    ("lnB", lambda E: emit_ln(E, "t0", "s0"), {"t0"}, {"s0"}),
    ("C", lambda E: emit_C(E, "s0", "t0"), {"s0"}, {"t0"}),
    ("lnC", lambda E: emit_ln(E, "t0", "s1"), {"t0"}, {"s1"}),
    ("D", lambda E: emit_D(E, "s1", "u0"), {"s1"}, {"u0"}),
    ("final", lambda E: emit_final(E, "u0", "s1", "out"), {"u0", "s1"}, {"out"}),
]

_PROG_CACHE = {}

def _get_prog(key, phases, ext_in, ext_out):
    if key not in _PROG_CACHE:
        _PROG_CACHE[key] = build_program(phases, ext_in, ext_out)
    return _PROG_CACHE[key]


def kernel(**inputs):
    ncores = 8
    W = host_weights(inputs)
    C = host_consts()
    shared = dict(W)
    shared.update(C)
    x0s = [np.ascontiguousarray(inputs["L0_attn_tokens"][c]) for c in range(ncores)]
    if FUSED:
        nc, used_in, _ = _get_prog("fused", [p[1] for p in _PIPELINE],
                                   set(shared) | {"x0"}, {"out"})
        in_maps = []
        for c in range(ncores):
            im = {k: shared[k] for k in used_in if k in shared}
            im["x0"] = x0s[c]
            in_maps.append(im)
        res = run_bass_kernel_spmd(nc, in_maps, core_ids=list(range(ncores)))
        return np.stack([res.results[c]["out"] for c in range(ncores)], axis=0)
    state = [{"x0": x0s[c]} for c in range(ncores)]
    for name, fn, sin, sout in _PIPELINE:
        nc, used_in, used_out = _get_prog(name, [fn], set(shared) | sin, sout)
        in_maps = []
        for c in range(ncores):
            im = {k: shared[k] for k in used_in if k in shared}
            for k in sin:
                if k in used_in:
                    im[k] = state[c][k]
            in_maps.append(im)
        res = run_bass_kernel_spmd(nc, in_maps, core_ids=list(range(ncores)))
        for c in range(ncores):
            for k in used_out:
                state[c][k] = np.ascontiguousarray(res.results[c][k])
    return np.stack([state[c]["out"] for c in range(ncores)], axis=0)


# version 19
# speedup vs baseline: 1.0718x; 1.0718x over previous
import numpy as np
import ml_dtypes
import concourse.bass as bass
import concourse.mybir as mybir
from concourse.bass_utils import run_bass_kernel_spmd

F32 = mybir.dt.float32
BF16 = mybir.dt.bfloat16
AF = mybir.ActivationFunctionType
ALU = mybir.AluOpType
AX = mybir.AxisListType

T = 4096
D = 1024
NC = 8
NST = 8
SW = 512

def host_consts():
    eye = np.eye(128)
    tri = (np.arange(128)[:, None] <= np.arange(128)[None, :]).astype(np.float32)
    blkq = (np.arange(128)[:, None] // 16 == np.arange(8)[None, :]).astype(np.float32)
    blkv = (np.arange(128)[:, None] // 64 == np.arange(2)[None, :]).astype(np.float32)
    blk2 = (np.arange(128)[None, :] // 64 == np.arange(2)[:, None]).astype(np.float32)
    IndE = np.zeros((32, 32 * 128), dtype=np.float32)
    for e in range(32):
        IndE[e, e * 128:(e + 1) * 128] = 1.0
    b = ml_dtypes.bfloat16
    return {
        "identf": eye.astype(np.float32),
        "identb": eye.astype(b),
        "onesb": np.ones((128, 128), dtype=b),
        "onesf": np.ones((1, 128), dtype=np.float32),
        "tri": tri.astype(b),
        "blkq": blkq.astype(b),
        "blkv": blkv.astype(b),
        "blk2": blk2.astype(b),
        "blkqT": np.ascontiguousarray(blkq.T).astype(b),
        "blkvT": np.ascontiguousarray(blkv.T).astype(b),
        "blkP": np.concatenate([np.pad(blkq, ((0, 0), (0, 120))),
                                np.pad(blkv, ((0, 0), (0, 126)))], axis=1).astype(b),
        "blkTP": np.concatenate([np.pad(blkq.T, ((0, 120), (0, 0))),
                                 np.pad(blkv.T, ((0, 126), (0, 0)))], axis=1).astype(b),
        "IndE": IndE.astype(b),
    }

CONST_SPECS = {
    "identf": ([128, 128], F32), "identb": ([128, 128], BF16),
    "onesb": ([128, 128], BF16), "onesf": ([1, 128], F32),
    "tri": ([128, 128], BF16), "blkq": ([128, 8], BF16),
    "blkv": ([128, 2], BF16), "blk2": ([2, 128], BF16),
    "blkqT": ([8, 128], BF16), "blkvT": ([2, 128], BF16),
    "blkP": ([128, 256], BF16), "blkTP": ([128, 256], BF16),
    "IndE": ([32, 4096], BF16),
}

WEIGHT_SPECS = {
    "Wq": ([1024, 256], F32), "Wk": ([1024, 256], F32), "Wv": ([1024, 1024], F32),
    "Wout": ([1024, 1024], F32),
    "WrT": ([1024, 1024], F32), "WkT": ([1024, 1024], F32), "WvT": ([1024, 1024], F32),
    "muR": ([128, 8], F32), "muK": ([128, 8], F32),
    "Aqk": ([1024, 128], F32), "Av": ([1024, 64], F32), "Ao": ([64, 1024], F32),
    "SelT": ([1024, 32], F32),
    "Keys": ([32, 1024, 512], F32), "Values": ([32, 512, 1024], F32),
}

STREAM_SPECS = {
    "s0": ([1024, 4096], F32), "s1": ([1024, 4096], F32), "t0": ([1024, 4096], F32),
    "qs": ([256, 4096], BF16), "ks": ([256, 4096], BF16), "vs": ([1024, 4096], BF16),
    "ssqT": ([16, 4096], F32), "u0": ([4096, 1024], F32), "fbT": ([2, 32768], BF16),
}

def host_weights(inputs):
    c = np.ascontiguousarray
    Wkv = inputs["L0_attn_Wkv"]
    Wk = c(Wkv.reshape(1024, 16, 80)[:, :, :16].reshape(1024, 256))
    Wv = c(Wkv.reshape(1024, 16, 80)[:, :, 16:].reshape(1024, 1024))
    return {
        "Wq": c(inputs["L0_attn_Wq"]), "Wk": Wk, "Wv": Wv,
        "Wout": c(inputs["L0_attn_Wout"]),
        "WrT": c(inputs["L0_ffn_W_r"].T), "WkT": c(inputs["L0_ffn_W_k"].T),
        "WvT": c(inputs["L0_ffn_W_v"].T),
        "muR": c(inputs["L0_ffn_mu_r"].reshape(8, 128).T),
        "muK": c(inputs["L0_ffn_mu_k"].reshape(8, 128).T),
        "Aqk": c(np.concatenate([inputs["L1_attn_W_q"], inputs["L1_attn_W_k"]], axis=1)),
        "Av": c(inputs["L1_attn_W_v"]), "Ao": c(inputs["L1_attn_W_o"]),
        "SelT": c(inputs["L1_ffn_expert_sel"].T),
        "Keys": c(inputs["L1_ffn_keys"]), "Values": c(inputs["L1_ffn_values"]),
    }



class Env:
    def __init__(self, nc, stack, dram):
        self.nc = nc
        self.stack = stack
        self.d = dram
        self._n = 0
        self._cur = []

    def sem(self, name, kind="eng"):
        self._n += 1
        h = self.nc.alloc_semaphore(name=f"{name}{self._n}")
        self._cur.append(h)
        return h

    def end_phase(self):
        if not self._cur:
            return
        self.nc.clear_and_free_semaphores(self._cur)
        self.nc.all_engine_barrier()
        self._cur = []

    def sbuf(self, name, shape, dt):
        self._n += 1
        return self.nc.sbuf_tensor(f"{name}_u{self._n}", shape, dt)

    def psum(self, name, shape, dt=F32):
        self._n += 1
        return self.nc.psum_tensor(f"{name}_u{self._n}", shape, dt)


def _dsem(E, name, n=2, kind="hw"):
    return [E.sem(f"{name}_{i}", kind) for i in range(n)]


def _transpose_pass(E, src, dst, tm_to_fm):
    nc = E.nc
    if tm_to_fm:
        tm = E.d[src]
        fmr = E.d[dst].rearrange("(c p) t -> p c t", p=128)
    else:
        fmr = E.d[src].rearrange("(c p) t -> p c t", p=128)
        tm = E.d[dst]
    lc = E.sem("tlc", "hw")
    ldp, stp = _dsem(E, "tld"), _dsem(E, "tst")
    mmq, evq = E.sem("tmm"), E.sem("tev")
    with (
        E.sbuf("t_xin", [128, 2 * 1024], F32) as xin,
        E.sbuf("t_idf", [128, 128], F32) as idf,
        E.sbuf("t_g", [128, 2 * 1024], F32) as sg,
        E.psum("t_pt0", [128, 128], F32) as pt0,
        E.psum("t_pt1", [128, 128], F32) as pt1,
        nc.Block() as blk,
    ):
        def in_slice(tt):
            p = (tt % 2) * 1024
            if tm_to_fm:
                return xin[:, p:p + 1024], tm[tt * 128:(tt + 1) * 128, :]
            return (xin[:, p:p + 1024].rearrange("p (c t) -> p c t", t=128),
                    fmr[:, :, tt * 128:(tt + 1) * 128])

        def out_slice(tt):
            p = (tt % 2) * 1024
            if tm_to_fm:
                return (fmr[:, :, tt * 128:(tt + 1) * 128],
                        sg[:, p:p + 1024].rearrange("p (c t) -> p c t", t=128))
            return tm[tt * 128:(tt + 1) * 128, :], sg[:, p:p + 1024]

        @blk.sync
        def _(sy):
            sy.dma_start(idf[:, :], E.d["identf"][:, :]).then_inc(lc, 16)
            for tt in range(32):
                if tt >= 2:
                    sy.wait_ge(mmq, 8 * (tt - 1))
                d, sr = in_slice(tt)
                sy.dma_start(d, sr).then_inc(ldp[tt % 2], 16)
                if tt >= 1:
                    sy.wait_ge(evq, 8 * tt)
                    d, sr = out_slice(tt - 1)
                    sy.dma_start(d, sr).then_inc(stp[(tt - 1) % 2], 16)
            sy.wait_ge(evq, 8 * 32)
            d, sr = out_slice(31)
            sy.dma_start(d, sr).then_inc(stp[31 % 2], 16)
            sy.wait_ge(stp[0], 16 * 16)
            sy.wait_ge(stp[1], 16 * 16)

        @blk.tensor
        def _(t):
            t.wait_ge(lc, 16)
            for tt in range(32):
                t.wait_ge(ldp[tt % 2], 16 * (tt // 2 + 1))
                for c in range(8):
                    k = tt * 8 + c
                    if k >= 2:
                        t.wait_ge(evq, k - 1)
                    t.transpose((pt0 if k % 2 == 0 else pt1)[:, :],
                                xin[:, (tt % 2) * 1024 + c * 128:(tt % 2) * 1024 + (c + 1) * 128],
                                idf[:, :]).then_inc(mmq, 1)

        @blk.scalar
        def _(s):
            for tt in range(32):
                for c in range(8):
                    k = tt * 8 + c
                    s.wait_ge(mmq, k + 1)
                    if c == 0 and tt >= 2:
                        s.wait_ge(stp[tt % 2], 16 * (tt // 2))
                    s.copy(sg[:, (tt % 2) * 1024 + c * 128:(tt % 2) * 1024 + (c + 1) * 128],
                           (pt0 if k % 2 == 0 else pt1)[:, :]).then_inc(evq, 1)
    return


def emit_entry(E, src="x0", dst="s0"):
    _transpose_pass(E, src, dst, tm_to_fm=True)


def emit_exit(E, src, dst="out"):
    _transpose_pass(E, src, dst, tm_to_fm=False)


def emit_ln(E, src, dst, residual=None):
    nc = E.nc
    N = SW
    srcr = E.d[src].rearrange("(c p) t -> p c t", p=128)
    resr = E.d[residual].rearrange("(c p) t -> p c t", p=128) if residual else None
    dstr = E.d[dst].rearrange("(c p) t -> p c t", p=128)
    lcp = _dsem(E, "lc")
    lbp, lfp, lrp, stp = _dsem(E, "lb", kind="sw"), _dsem(E, "lf"), _dsem(E, "lr"), _dsem(E, "ls")
    smm, sev, svv = E.sem("lm"), E.sem("le"), E.sem("lv")
    V_PER = (16 if residual else 0) + 3 + 2 + 16
    S_PER = 8 + 2 + 2
    R16 = 16 if residual else 0
    with (
        E.sbuf("ln_xs", [128, 2 * 4096], F32) as xs,
        (E.sbuf("ln_rs", [128, 2 * 4096], F32) if residual else E.sbuf("ln_rs", [1, 1], F32)) as rs,
        E.sbuf("ln_xb", [128, 2 * 4096], BF16) as xb,
        E.sbuf("ln_sq", [128, 4096], BF16) as sqb,
        E.sbuf("ln_onb", [128, 128], BF16) as onb,
        E.sbuf("ln_onf", [1, 128], F32) as onf,
        E.sbuf("ln_rows", [1, 6 * 512], F32) as rows,
        E.sbuf("ln_od", [128, 2 * 4096], F32) as od,
        E.psum("ln_pa", [1, 512], F32) as psA,
        E.psum("ln_pb", [1, 512], F32) as psB,
        E.psum("ln_pbc", [128, 2 * 512], F32) as psBC,
        nc.Block() as blk,
    ):
        nst = NST

        @blk.gpsimd
        def _(g):
            if not residual:
                for s in range(nst):
                    if s >= 2:
                        g.wait_ge(sev, S_PER * (s - 1))
                    g.dma_start(xb[:, (s % 2) * 4096:(s % 2 + 1) * 4096].rearrange("p (c t) -> p c t", t=N),
                                srcr[:, :, s * N:(s + 1) * N]).then_inc(lbp[s % 2], 16)

        @blk.sync
        def _(sy):
            sy.dma_start(onb[:, :], E.d["onesb"][:, :]).then_inc(lcp[0], 16)
            sy.dma_start(onf[:, :], E.d["onesf"][:, :]).then_inc(lcp[1], 16)
            for s in range(nst):
                if s >= 2:
                    sy.wait_ge(svv, V_PER * (s - 1))
                sy.dma_start(xs[:, (s % 2) * 4096:(s % 2 + 1) * 4096].rearrange("p (c t) -> p c t", t=N),
                             srcr[:, :, s * N:(s + 1) * N]).then_inc(lfp[s % 2], 16)
                if residual:
                    sy.dma_start(rs[:, (s % 2) * 4096:(s % 2 + 1) * 4096].rearrange("p (c t) -> p c t", t=N),
                                 resr[:, :, s * N:(s + 1) * N]).then_inc(lrp[s % 2], 16)
                if s >= 1:
                    sy.wait_ge(svv, V_PER * s)
                    sy.dma_start(dstr[:, :, (s - 1) * N:s * N],
                                 od[:, ((s - 1) % 2) * 4096:((s - 1) % 2 + 1) * 4096].rearrange(
                                     "p (c t) -> p c t", t=N)).then_inc(stp[(s - 1) % 2], 16)
            sy.wait_ge(svv, V_PER * nst)
            sy.dma_start(dstr[:, :, (nst - 1) * N:nst * N],
                         od[:, ((nst - 1) % 2) * 4096:((nst - 1) % 2 + 1) * 4096].rearrange(
                             "p (c t) -> p c t", t=N)).then_inc(stp[(nst - 1) % 2], 16)
            sy.wait_ge(stp[0], 16 * (nst // 2))
            sy.wait_ge(stp[1], 16 * (nst // 2))

        @blk.vector
        def _(v):
            for s in range(nst):
                base = (s % 2) * 4096
                myv = V_PER * s
                v.wait_ge(lfp[s % 2], 16 * (s // 2 + 1))
                if residual:
                    v.wait_ge(lrp[s % 2], 16 * (s // 2 + 1))
                    if s >= 2:
                        v.wait_ge(smm, 4 * (s - 1))
                    for c in range(8):
                        v.tensor_tensor(xs[:, base + c * N:base + (c + 1) * N],
                                        xs[:, base + c * N:base + (c + 1) * N],
                                        rs[:, base + c * N:base + (c + 1) * N], ALU.add).then_inc(svv, 1)
                        myv += 1
                    v.wait_ge(svv, myv)
                    for c in range(8):
                        v.tensor_copy(xb[:, base + c * N:base + (c + 1) * N],
                                      xs[:, base + c * N:base + (c + 1) * N]).then_inc(svv, 1)
                        myv += 1
                v.wait_ge(sev, S_PER * s + 8 + 2)
                v.tensor_tensor(rows[:, 2 * 512:3 * 512], rows[:, 0:512], rows[:, 0:512],
                                ALU.mult).then_inc(svv, 1)
                myv += 1
                v.wait_ge(svv, myv)
                v.tensor_tensor(rows[:, 3 * 512:4 * 512], rows[:, 512:1024], rows[:, 2 * 512:3 * 512],
                                ALU.subtract).then_inc(svv, 1)
                myv += 1
                v.wait_ge(svv, myv)
                v.tensor_scalar(rows[:, 3 * 512:4 * 512], rows[:, 3 * 512:4 * 512], 1e-5, None,
                                ALU.add).then_inc(svv, 1)
                myv += 1
                v.wait_ge(sev, S_PER * s + 8 + 2 + 2)
                v.tensor_tensor(rows[:, 5 * 512:6 * 512], rows[:, 0:512], rows[:, 4 * 512:5 * 512],
                                ALU.mult).then_inc(svv, 1)
                myv += 1
                v.wait_ge(svv, myv)
                v.tensor_scalar(rows[:, 5 * 512:6 * 512], rows[:, 5 * 512:6 * 512], -1.0, None,
                                ALU.mult).then_inc(svv, 1)
                myv += 1
                v.wait_ge(smm, 4 * s + 4)
                if s >= 2:
                    v.wait_ge(stp[s % 2], 16 * (s // 2))
                for c in range(8):
                    v.tensor_tensor(od[:, base + c * N:base + (c + 1) * N],
                                    xs[:, base + c * N:base + (c + 1) * N],
                                    psBC[:, 0:512], ALU.mult).then_inc(svv, 1)
                    myv += 1
                v.wait_ge(svv, myv)
                for c in range(8):
                    v.tensor_tensor(od[:, base + c * N:base + (c + 1) * N],
                                    od[:, base + c * N:base + (c + 1) * N],
                                    psBC[:, 512:1024], ALU.add).then_inc(svv, 1)
                    myv += 1
                assert myv == V_PER * (s + 1), (myv, s)

        @blk.scalar
        def _(s_):
            for s in range(nst):
                base = (s % 2) * 4096
                mys = S_PER * s
                if residual:
                    s_.wait_ge(svv, V_PER * s + 16)
                else:
                    s_.wait_ge(lbp[s % 2], 16 * (s // 2 + 1))
                if s >= 1:
                    s_.wait_ge(smm, 4 * (s - 1) + 2)
                for c in range(8):
                    s_.activation(sqb[:, c * N:(c + 1) * N], xb[:, base + c * N:base + (c + 1) * N],
                                  AF.Square).then_inc(sev, 1)
                    mys += 1
                s_.wait_ge(smm, 4 * s + 2)
                if s >= 1:
                    s_.wait_ge(svv, V_PER * (s - 1) + R16 + 4)
                s_.mul(rows[:, 0:512], psA[:, :], 1.0 / 1024.0).then_inc(sev, 1)
                s_.mul(rows[:, 512:1024], psB[:, :], 1.0 / 1024.0).then_inc(sev, 1)
                mys += 2
                s_.wait_ge(svv, V_PER * s + R16 + 3)
                s_.activation(rows[:, 4 * 512:5 * 512], rows[:, 3 * 512:4 * 512], AF.Ln).then_inc(sev, 1)
                mys += 1
                s_.wait_ge(sev, mys)
                s_.activation(rows[:, 4 * 512:5 * 512], rows[:, 4 * 512:5 * 512], AF.Exp,
                              scale=-0.5).then_inc(sev, 1)
                mys += 1
                assert mys == S_PER * (s + 1)

        @blk.tensor
        def _(t):
            t.wait_ge(lcp[0], 16)
            t.wait_ge(lcp[1], 16)
            for s in range(nst):
                base = (s % 2) * 4096
                if residual:
                    t.wait_ge(svv, V_PER * s + 16)
                else:
                    t.wait_ge(lbp[s % 2], 16 * (s // 2 + 1))
                if s >= 1:
                    t.wait_ge(sev, S_PER * (s - 1) + 10)
                for c in range(8):
                    mm = t.matmul(psA[:, :], onb[:, 0:1], xb[:, base + c * N:base + (c + 1) * N],
                                  start=(c == 0), stop=(c == 7))
                    if c == 7:
                        mm.then_inc(smm, 1)
                t.wait_ge(sev, S_PER * s + 8)
                for c in range(8):
                    mm = t.matmul(psB[:, :], onb[:, 0:1], sqb[:, c * N:(c + 1) * N],
                                  start=(c == 0), stop=(c == 7))
                    if c == 7:
                        mm.then_inc(smm, 1)
                t.wait_ge(svv, V_PER * s + R16 + 5)
                t.matmul(psBC[:, 0:512], onf[:, :], rows[:, 4 * 512:5 * 512], start=True,
                         stop=True).then_inc(smm, 1)
                t.matmul(psBC[:, 512:1024], onf[:, :], rows[:, 5 * 512:6 * 512], start=True,
                         stop=True).then_inc(smm, 1)
    return


ALL_SPECS = {}
ALL_SPECS.update(CONST_SPECS)
ALL_SPECS.update(WEIGHT_SPECS)
ALL_SPECS.update(STREAM_SPECS)
ALL_SPECS["x0"] = ([4096, 1024], F32)
ALL_SPECS["out"] = ([4096, 1024], F32)


class LazyDram:
    def __init__(self, nc, ext_in, ext_out):
        self.nc = nc
        self.ext_in = set(ext_in)
        self.ext_out = set(ext_out)
        self.aps = {}
        self.used_in = set()
        self.used_out = set()

    def __getitem__(self, name):
        if name not in self.aps:
            shape, dt = ALL_SPECS[name]
            if name in self.ext_in:
                kind = "ExternalInput"
                self.used_in.add(name)
            elif name in self.ext_out:
                kind = "ExternalOutput"
                self.used_out.add(name)
            else:
                kind = "Internal"
            self.aps[name] = self.nc.dram_tensor(name, shape, dt, kind=kind).ap()
        return self.aps[name]


def build_program(phases, ext_in, ext_out):
    from contextlib import ExitStack
    nc = bass.Bass("TRN2", target_bir_lowering=False)
    dram = LazyDram(nc, ext_in, ext_out)
    with ExitStack() as stack:
        E = Env(nc, stack, dram)
        import os
        recycle = os.environ.get("MK_NO_RECYCLE", "") != "1"
        for i, ph in enumerate(phases):
            ph(E)
            if recycle and i + 1 < len(phases):
                E.end_phase()
    return nc, dram.used_in, dram.used_out

def emit_B(E, src="s1", dst="t0"):
    nc = E.nc
    N = SW
    srcr = E.d[src].rearrange("(c p) t -> p c t", p=128)
    dstr = E.d[dst].rearrange("(c p) t -> p c t", p=128)
    lw = _dsem(E, "bw", 3, kind="sw") + _dsem(E, "bwh", 2)
    lx = _dsem(E, "bx")
    stp = _dsem(E, "bs")
    smm, sev, svv = E.sem("bm"), E.sem("be"), E.sem("bv")
    msz = E.sem("bz")
    V_PER = 24 + 8 + 16
    S_PER = 16
    M_PER = 24
    with (
        E.sbuf("b_wr", [128, 8 * 1024], BF16) as WrB,
        E.sbuf("b_wk", [128, 8 * 1024], BF16) as WkB,
        E.sbuf("b_wv", [128, 8 * 1024], BF16) as WvB,
        E.sbuf("b_mu", [128, 16], F32) as muB,
        E.sbuf("b_xs", [128, 2 * 8 * 513], F32) as xs,
        E.sbuf("b_d", [128, 4096], F32) as db,
        E.sbuf("b_xr", [128, 2 * 4096], BF16) as xrb,
        E.sbuf("b_xk", [128, 2 * 4096], BF16) as xkb,
        E.sbuf("b_kb", [128, 4096], BF16) as kb,
        E.sbuf("b_k2", [128, 4096], BF16) as k2b,
        E.sbuf("b_rb", [128, 4096], BF16) as rb,
        E.sbuf("b_od", [128, 2 * 4096], F32) as od,
        E.psum("b_pk0", [128, 512], F32) as pk0,
        E.psum("b_pk1", [128, 512], F32) as pk1,
        E.psum("b_pr0", [128, 512], F32) as pr0,
        E.psum("b_pr1", [128, 512], F32) as pr1,
        E.psum("b_pv0", [128, 512], F32) as pv0,
        E.psum("b_pv1", [128, 512], F32) as pv1,
        nc.Block() as blk,
    ):
        nst = NST
        PK = [pk0, pk1]; PR = [pr0, pr1]; PV = [pv0, pv1]

        @blk.gpsimd
        def _(g):
            g.dma_start(WrB[:, :].rearrange("p (c n) -> p c n", n=1024),
                        E.d["WrT"].rearrange("(c p) n -> p c n", p=128)).then_inc(lw[0], 16)
            g.dma_start(WkB[:, :].rearrange("p (c n) -> p c n", n=1024),
                        E.d["WkT"].rearrange("(c p) n -> p c n", p=128)).then_inc(lw[1], 16)
            g.dma_start(WvB[:, :].rearrange("p (c n) -> p c n", n=1024),
                        E.d["WvT"].rearrange("(c p) n -> p c n", p=128)).then_inc(lw[2], 16)

        @blk.sync
        def _(sy):
            sy.dma_start(muB[:, 0:8], E.d["muR"][:, :]).then_inc(lw[3], 16)
            sy.dma_start(muB[:, 8:16], E.d["muK"][:, :]).then_inc(lw[4], 16)
            for s in range(nst):
                base = (s % 2) * (8 * 513)
                if s >= 2:
                    sy.wait_ge(svv, V_PER * (s - 1))
                if s == 0:
                    sy.dma_start(
                        xs[:, 0:8 * 513].rearrange("p (c t) -> p c t", t=513)[:, :, 1:513],
                        srcr[:, :, 0:512]).then_inc(lx[0], 16)
                else:
                    sy.dma_start(
                        xs[:, base:base + 8 * 513].rearrange("p (c t) -> p c t", t=513),
                        srcr[:, :, s * N - 1:(s + 1) * N]).then_inc(lx[s % 2], 16)
                if s >= 1:
                    sy.wait_ge(svv, V_PER * s)
                    sy.dma_start(dstr[:, :, (s - 1) * N:s * N],
                                 od[:, ((s - 1) % 2) * 4096:((s - 1) % 2 + 1) * 4096].rearrange(
                                     "p (c t) -> p c t", t=N)).then_inc(stp[(s - 1) % 2], 16)
            sy.wait_ge(svv, V_PER * nst)
            sy.dma_start(dstr[:, :, (nst - 1) * N:nst * N],
                         od[:, ((nst - 1) % 2) * 4096:((nst - 1) % 2 + 1) * 4096].rearrange(
                             "p (c t) -> p c t", t=N)).then_inc(stp[(nst - 1) % 2], 16)
            sy.wait_ge(stp[0], 16 * (nst // 2))
            sy.wait_ge(stp[1], 16 * (nst // 2))

        @blk.vector
        def _(v):
            v.wait_ge(lw[3], 16)
            v.wait_ge(lw[4], 16)
            for s in range(nst):
                bx = (s % 2) * (8 * 513)
                ba = (s % 2) * 4096
                myv = V_PER * s
                v.wait_ge(lx[s % 2], 16 * (s // 2 + 1))
                if s == 0:
                    v.memset(xs[:, 0:8 * 513].rearrange("p (c t) -> p c t", t=513)[:, :, 0:1],
                             0.0).then_inc(msz, 1)
                    v.wait_ge(msz, 1)
                if s >= 2:
                    v.wait_ge(smm, M_PER * (s - 1))
                for c in range(8):
                    v.tensor_tensor(db[:, c * N:(c + 1) * N],
                                    xs[:, bx + c * 513 + 1:bx + c * 513 + 513],
                                    xs[:, bx + c * 513:bx + c * 513 + 512], ALU.subtract).then_inc(svv, 1)
                    myv += 1
                v.wait_ge(svv, myv)
                for c in range(8):
                    v.scalar_tensor_tensor(xrb[:, ba + c * N:ba + (c + 1) * N],
                                           db[:, c * N:(c + 1) * N], muB[:, c:c + 1],
                                           xs[:, bx + c * 513:bx + c * 513 + 512],
                                           ALU.mult, ALU.add).then_inc(svv, 1)
                    myv += 1
                for c in range(8):
                    v.scalar_tensor_tensor(xkb[:, ba + c * N:ba + (c + 1) * N],
                                           db[:, c * N:(c + 1) * N], muB[:, 8 + c:9 + c],
                                           xs[:, bx + c * 513:bx + c * 513 + 512],
                                           ALU.mult, ALU.add).then_inc(svv, 1)
                    myv += 1
                for m in range(8):
                    v.wait_ge(sev, S_PER * s + m + 1)
                    v.tensor_tensor(k2b[:, m * N:(m + 1) * N], kb[:, m * N:(m + 1) * N],
                                    kb[:, m * N:(m + 1) * N], ALU.mult).then_inc(svv, 1)
                    myv += 1
                if s >= 2:
                    v.wait_ge(stp[s % 2], 16 * (s // 2))
                for m in range(8):
                    v.wait_ge(smm, M_PER * s + 16 + m + 1)
                    v.wait_ge(sev, S_PER * s + 8 + m + 1)
                    v.tensor_tensor(od[:, ba + m * N:ba + (m + 1) * N],
                                    PV[m % 2][:, :], rb[:, m * N:(m + 1) * N], ALU.mult).then_inc(svv, 1)
                    myv += 1
                    v.wait_ge(svv, myv)
                    v.tensor_tensor(od[:, ba + m * N:ba + (m + 1) * N],
                                    od[:, ba + m * N:ba + (m + 1) * N],
                                    xs[:, bx + m * 513 + 1:bx + m * 513 + 513], ALU.add).then_inc(svv, 1)
                    myv += 1
                assert myv == V_PER * (s + 1)

        @blk.scalar
        def _(s_):
            for s in range(nst):
                mys = S_PER * s
                for m in range(8):
                    s_.wait_ge(smm, M_PER * s + m + 1)
                    s_.activation(kb[:, m * N:(m + 1) * N], PK[m % 2][:, :], AF.Relu).then_inc(sev, 1)
                    mys += 1
                for m in range(8):
                    s_.wait_ge(smm, M_PER * s + 8 + m + 1)
                    s_.activation(rb[:, m * N:(m + 1) * N], PR[m % 2][:, :], AF.Sigmoid).then_inc(sev, 1)
                    mys += 1
                assert mys == S_PER * (s + 1)

        @blk.tensor
        def _(t):
            t.wait_ge(lw[0], 16)
            t.wait_ge(lw[1], 16)
            t.wait_ge(lw[2], 16)
            for s in range(nst):
                ba = (s % 2) * 4096
                t.wait_ge(svv, V_PER * s + 24)
                for m in range(8):
                    if m >= 2:
                        t.wait_ge(sev, S_PER * s + (m - 1))
                    elif s >= 1:
                        t.wait_ge(sev, S_PER * (s - 1) + 8 + 8)
                    for c in range(8):
                        mm = t.matmul(PK[m % 2][:, :],
                                      WkB[:, c * 1024 + m * 128:c * 1024 + (m + 1) * 128],
                                      xkb[:, ba + c * N:ba + (c + 1) * N],
                                      start=(c == 0), stop=(c == 7))
                    mm.then_inc(smm, 1)
                for m in range(8):
                    if m >= 2:
                        t.wait_ge(sev, S_PER * s + 8 + (m - 1))
                    for c in range(8):
                        mm = t.matmul(PR[m % 2][:, :],
                                      WrB[:, c * 1024 + m * 128:c * 1024 + (m + 1) * 128],
                                      xrb[:, ba + c * N:ba + (c + 1) * N],
                                      start=(c == 0), stop=(c == 7))
                    mm.then_inc(smm, 1)
                t.wait_ge(svv, V_PER * s + 32)
                for m in range(8):
                    if m >= 2:
                        t.wait_ge(svv, V_PER * s + 32 + 2 * (m - 1) + 1)
                    for c in range(8):
                        mm = t.matmul(PV[m % 2][:, :],
                                      WvB[:, c * 1024 + m * 128:c * 1024 + (m + 1) * 128],
                                      k2b[:, c * N:(c + 1) * N],
                                      start=(c == 0), stop=(c == 7))
                    mm.then_inc(smm, 1)
    return

def emit_C(E, src="s0", dst="t0"):
    nc = E.nc
    N = SW
    srcr = E.d[src].rearrange("(c p) t -> p c t", p=128)
    dstr = E.d[dst].rearrange("(c p) t -> p c t", p=128)
    lw = _dsem(E, "cw", 3, kind="sw") + _dsem(E, "cwh", 1)
    lbc = _dsem(E, "cb", kind="sw")
    lxs = _dsem(E, "cx")
    stp = _dsem(E, "cs")
    qm, ce1, cv1 = E.sem("cqm"), E.sem("ce1"), E.sem("cv1")
    tmq, tve = E.sem("ctm"), E.sem("ctv")
    cm, ce, cv, mz = E.sem("ccm"), E.sem("cce"), E.sem("ccv"), E.sem("cmz")
    wm = E.sem("cwm")
    nst = NST
    from contextlib import ExitStack
    with ExitStack() as stk:
        AqkB = stk.enter_context(E.sbuf("c_aqk", [128, 8 * 128], BF16))
        AvB = stk.enter_context(E.sbuf("c_av", [128, 8 * 64], BF16))
        AoB = stk.enter_context(E.sbuf("c_ao", [64, 1024], BF16))
        idb = stk.enter_context(E.sbuf("c_idb", [128, 128], BF16))
        trib = stk.enter_context(E.sbuf("c_tri", [128, 128], BF16))
        onb = stk.enter_context(E.sbuf("c_onb", [128, 128], BF16))
        xb = stk.enter_context(E.sbuf("c_xb", [128, 2 * 4096], BF16))
        ph1 = stk.enter_context(E.sbuf("c_ph1", [128, 512], F32))
        ph2 = stk.enter_context(E.sbuf("c_ph2", [128, 512], F32))
        ph3 = stk.enter_context(E.sbuf("c_ph3", [128, 512], F32))
        qb = stk.enter_context(E.sbuf("c_qb", [64, 4096], BF16))
        kb = stk.enter_context(E.sbuf("c_kb", [64, 4096], BF16))
        vb = stk.enter_context(E.sbuf("c_vb", [64, 4096], BF16))
        ktm = stk.enter_context(E.sbuf("c_ktm", [128, 32 * 64], BF16))
        vtm = stk.enter_context(E.sbuf("c_vtm", [128, 32 * 64], BF16))
        STb = stk.enter_context(E.sbuf("c_ST", [64, 64], BF16))
        STf = stk.enter_context(E.sbuf("c_STf", [64, 64], F32))
        zf = stk.enter_context(E.sbuf("c_zf", [64, 1], F32))
        zi = stk.enter_context(E.sbuf("c_zi", [64, 1], F32))
        zb = stk.enter_context(E.sbuf("c_zb", [64, 1], BF16))
        gs = stk.enter_context(E.sbuf("c_gs", [128, 128], BF16))
        gks = stk.enter_context(E.sbuf("c_gks", [128, 128], BF16))
        gm = stk.enter_context(E.sbuf("c_gm", [128, 128], BF16))
        gkm = stk.enter_context(E.sbuf("c_gkm", [128, 128], BF16))
        dr = stk.enter_context(E.sbuf("c_dr", [1, 128], F32))
        dl = stk.enter_context(E.sbuf("c_dl", [1, 128], F32))
        drb = stk.enter_context(E.sbuf("c_drb", [1, 128], BF16))
        ob = stk.enter_context(E.sbuf("c_ob", [64, 128], BF16))
        obuf = stk.enter_context(E.sbuf("c_obuf", [64, 4096], BF16))
        xs = stk.enter_context(E.sbuf("c_xs", [128, 2 * 4096], F32))
        od = stk.enter_context(E.sbuf("c_od", [128, 2 * 4096], F32))
        with (
            E.psum("c_pqk0", [128, 512], F32) as pqk0,
            E.psum("c_pqk1", [128, 512], F32) as pqk1,
            E.psum("c_pv0", [64, 512], F32) as pv0,
            E.psum("c_pv1", [64, 512], F32) as pv1,
            nc.Block() as blk,
        ):
            PQK = [pqk0, pqk1]; PV = [pv0, pv1]

            @blk.gpsimd
            def _(g):
                g.dma_start(AqkB[:, :].rearrange("p (c n) -> p c n", n=128),
                            E.d["Aqk"].rearrange("(c p) n -> p c n", p=128)).then_inc(lw[0], 16)
                g.dma_start(AvB[:, :].rearrange("p (c n) -> p c n", n=64),
                            E.d["Av"].rearrange("(c p) n -> p c n", p=128)).then_inc(lw[1], 16)
                g.dma_start(AoB[:, :], E.d["Ao"][:, :]).then_inc(lw[2], 16)
                for s in range(nst):
                    if s >= 2:
                        g.wait_ge(qm, 2 * (s - 1))
                    g.dma_start(xb[:, (s % 2) * 4096:(s % 2 + 1) * 4096].rearrange("p (c t) -> p c t", t=N),
                                srcr[:, :, s * N:(s + 1) * N]).then_inc(lbc[s % 2], 16)

            @blk.sync
            def _(sy):
                sy.dma_start(idb[:, :], E.d["identb"][:, :]).then_inc(lw[3], 16)
                sy.dma_start(trib[:, :], E.d["tri"][:, :]).then_inc(lw[3], 16)
                sy.dma_start(onb[:, :], E.d["onesb"][:, :]).then_inc(lw[3], 16)

            @blk.tensor
            def _(t):
                t.wait_ge(lw[0], 16)
                t.wait_ge(lw[1], 16)
                for s in range(nst):
                    ba = (s % 2) * 4096
                    t.wait_ge(lbc[s % 2], 16 * (s // 2 + 1))
                    if s >= 2:
                        t.wait_ge(ce1, 4 * (s - 2) + 3)
                    for c in range(8):
                        mm = t.matmul(PQK[s % 2][:, :], AqkB[:, c * 128:(c + 1) * 128],
                                      xb[:, ba + c * N:ba + (c + 1) * N],
                                      start=(c == 0), stop=(c == 7))
                    mm.then_inc(qm, 1)
                    if s >= 2:
                        t.wait_ge(ce1, 4 * (s - 2) + 4)
                    for c in range(8):
                        mm = t.matmul(PV[s % 2][:, :], AvB[:, c * 64:(c + 1) * 64],
                                      xb[:, ba + c * N:ba + (c + 1) * N],
                                      start=(c == 0), stop=(c == 7))
                    mm.then_inc(qm, 1)

            @blk.scalar
            def _(s_):
                for s in range(nst):
                    s_.wait_ge(qm, 2 * s + 1)
                    if s >= 1:
                        s_.wait_ge(cv1, 2 * s)
                    s_.activation(ph1[:, :], PQK[s % 2][:, :], AF.Relu, scale=-1.0).then_inc(ce1, 1)
                    s_.wait_ge(ce1, 4 * s + 1)
                    s_.activation(ph2[:, :], ph1[:, :], AF.Exp, scale=-1.0).then_inc(ce1, 1)
                    s_.activation(ph3[:, :], PQK[s % 2][:, :], AF.Relu).then_inc(ce1, 1)
                    s_.wait_ge(qm, 2 * s + 2)
                    s_.copy(vb[:, s * N:(s + 1) * N], PV[s % 2][:, :]).then_inc(ce1, 1)

            @blk.vector
            def _(v):
                for s in range(nst):
                    v.wait_ge(ce1, 4 * s + 3)
                    v.tensor_tensor(qb[:, s * N:(s + 1) * N], ph2[0:64, :], ph3[0:64, :],
                                    ALU.add).then_inc(cv1, 1)
                    v.tensor_tensor(kb[:, s * N:(s + 1) * N], ph2[64:128, :], ph3[64:128, :],
                                    ALU.add).then_inc(cv1, 1)

        with (
            E.psum("c_pt0", [128, 64], BF16) as pt0,
            E.psum("c_pt1", [128, 64], BF16) as pt1,
            nc.Block() as blk2,
        ):
            PT = [pt0, pt1]

            @blk2.tensor
            def _(t):
                t.wait_ge(lw[3], 48)
                for k in range(64):
                    c = k // 2
                    srcb = kb if k % 2 == 0 else vb
                    if k >= 2:
                        t.wait_ge(tve, k - 1)
                    t.transpose(PT[k % 2][:, :], srcb[:, c * 128:(c + 1) * 128],
                                idb[0:64, 0:64]).then_inc(tmq, 1)

            @blk2.vector
            def _(v):
                for k in range(64):
                    c = k // 2
                    dstb = ktm if k % 2 == 0 else vtm
                    v.wait_ge(tmq, k + 1)
                    v.tensor_copy(dstb[:, c * 64:(c + 1) * 64], PT[k % 2][:, :]).then_inc(tve, 1)

        with (
            E.psum("c_pg", [128, 128], F32) as pg,
            E.psum("c_pgk", [128, 128], F32) as pgk,
            E.psum("c_pO", [64, 128], F32) as pO,
            E.psum("c_pd", [1, 128], F32) as pd,
            E.psum("c_pS", [64, 64], F32) as pS,
            E.psum("c_pbc", [64, 128], F32) as pbc,
            nc.Block() as blk3,
        ):
            @blk3.vector
            def _(v):
                v.memset(STf[:, :], 0.0).then_inc(mz, 1)
                v.memset(STb[:, :], 0.0).then_inc(mz, 1)
                v.memset(zf[:, :], 0.0).then_inc(mz, 1)
                v.memset(zb[:, :], 0.0).then_inc(mz, 1)
                v.wait_ge(mz, 4)
                for c in range(32):
                    v.wait_ge(ce, 5 * c + 1)
                    v.tensor_tensor(gm[:, :], gs[:, :], trib[:, :], ALU.mult).then_inc(cv, 1)
                    v.wait_ge(ce, 5 * c + 2)
                    v.tensor_tensor(gkm[:, :], gks[:, :], trib[:, :], ALU.mult).then_inc(cv, 1)
                    v.wait_ge(cm, 8 * c + 7)
                    v.tensor_tensor(STf[:, :], STf[:, :], pS[:, :], ALU.add).then_inc(cv, 1)
                    v.wait_ge(cv, 10 * c + 3)
                    v.tensor_copy(STb[:, :], STf[:, :]).then_inc(cv, 1)
                    v.reduce_sum(zi[:, :], kb[:, c * 128:(c + 1) * 128], axis=AX.X).then_inc(cv, 1)
                    v.wait_ge(cv, 10 * c + 5)
                    v.tensor_tensor(zf[:, :], zf[:, :], zi[:, :], ALU.add).then_inc(cv, 1)
                    v.wait_ge(cv, 10 * c + 6)
                    v.wait_ge(cm, 8 * c + 5)
                    v.tensor_copy(zb[:, :], zf[:, :]).then_inc(cv, 1)
                    v.wait_ge(cm, 8 * c + 6)
                    v.tensor_scalar(dr[:, :], pd[:, :], 1e-6, None, ALU.max).then_inc(cv, 1)
                    v.wait_ge(ce, 5 * c + 4)
                    v.tensor_copy(drb[:, :], dl[:, :]).then_inc(cv, 1)
                    v.wait_ge(ce, 5 * c + 5)
                    v.wait_ge(cm, 8 * c + 8)
                    v.tensor_tensor(obuf[:, c * 128:(c + 1) * 128], ob[:, :], pbc[:, :],
                                    ALU.mult).then_inc(cv, 1)

            @blk3.scalar
            def _(s_):
                for c in range(32):
                    s_.wait_ge(cm, 8 * c + 1)
                    if c >= 1:
                        s_.wait_ge(cv, 10 * (c - 1) + 1)
                    s_.copy(gs[:, :], pg[:, :]).then_inc(ce, 1)
                    s_.wait_ge(cm, 8 * c + 2)
                    if c >= 1:
                        s_.wait_ge(cv, 10 * (c - 1) + 2)
                    s_.copy(gks[:, :], pgk[:, :]).then_inc(ce, 1)
                    s_.wait_ge(cv, 10 * c + 8)
                    s_.activation(dl[:, :], dr[:, :], AF.Ln).then_inc(ce, 1)
                    s_.wait_ge(ce, 5 * c + 3)
                    s_.activation(dl[:, :], dl[:, :], AF.Exp, scale=-1.0).then_inc(ce, 1)
                    s_.wait_ge(cm, 8 * c + 4)
                    if c >= 1:
                        s_.wait_ge(cv, 10 * (c - 1) + 10)
                    s_.copy(ob[:, :], pO[:, :]).then_inc(ce, 1)

            @blk3.tensor
            def _(t):
                t.wait_ge(mz, 4)
                for c in range(32):
                    if c >= 1:
                        t.wait_ge(ce, 5 * (c - 1) + 1)
                    t.matmul(pg[:, :], vb[:, c * 128:(c + 1) * 128], qb[:, c * 128:(c + 1) * 128],
                             start=True, stop=True).then_inc(cm, 1)
                    if c >= 1:
                        t.wait_ge(ce, 5 * (c - 1) + 2)
                    t.matmul(pgk[:, :], kb[:, c * 128:(c + 1) * 128], qb[:, c * 128:(c + 1) * 128],
                             start=True, stop=True).then_inc(cm, 1)
                    if c >= 1:
                        t.wait_ge(cv, 10 * (c - 1) + 4)
                        t.wait_ge(ce, 5 * (c - 1) + 5)
                    t.matmul(pO[:, :], STb[:, :], qb[:, c * 128:(c + 1) * 128],
                             start=True, stop=False).then_inc(cm, 1)
                    t.wait_ge(cv, 10 * c + 1)
                    t.matmul(pO[:, :], ktm[:, c * 64:(c + 1) * 64], gm[:, :],
                             start=False, stop=True).then_inc(cm, 1)
                    if c >= 1:
                        t.wait_ge(cv, 10 * (c - 1) + 7)
                        t.wait_ge(cv, 10 * (c - 1) + 8)
                    t.matmul(pd[:, :], zb[:, :], qb[:, c * 128:(c + 1) * 128],
                             start=True, stop=False).then_inc(cm, 1)
                    t.wait_ge(cv, 10 * c + 2)
                    t.matmul(pd[:, :], onb[:, 0:1], gkm[:, :],
                             start=False, stop=True).then_inc(cm, 1)
                    if c >= 1:
                        t.wait_ge(cv, 10 * (c - 1) + 3)
                    t.matmul(pS[:, :], vtm[:, c * 64:(c + 1) * 64], ktm[:, c * 64:(c + 1) * 64],
                             start=True, stop=True).then_inc(cm, 1)
                    t.wait_ge(cv, 10 * c + 9)
                    t.matmul(pbc[:, :], onb[0:1, 0:64], drb[:, :], start=True,
                             stop=True).then_inc(cm, 1)

        with (
            E.psum("c_pw0", [128, 512], F32) as pw0,
            E.psum("c_pw1", [128, 512], F32) as pw1,
            nc.Block() as blk4,
        ):
            PW = [pw0, pw1]
            wv = E.sem("cwv")

            @blk4.sync
            def _(sy):
                for s in range(nst):
                    if s >= 2:
                        sy.wait_ge(wv, 8 * (s - 1))
                    sy.dma_start(xs[:, (s % 2) * 4096:(s % 2 + 1) * 4096].rearrange("p (c t) -> p c t", t=N),
                                 srcr[:, :, s * N:(s + 1) * N]).then_inc(lxs[s % 2], 16)
                    if s >= 1:
                        sy.wait_ge(wv, 8 * s)
                        sy.dma_start(dstr[:, :, (s - 1) * N:s * N],
                                     od[:, ((s - 1) % 2) * 4096:((s - 1) % 2 + 1) * 4096].rearrange(
                                         "p (c t) -> p c t", t=N)).then_inc(stp[(s - 1) % 2], 16)
                sy.wait_ge(wv, 8 * nst)
                sy.dma_start(dstr[:, :, (nst - 1) * N:nst * N],
                             od[:, ((nst - 1) % 2) * 4096:((nst - 1) % 2 + 1) * 4096].rearrange(
                                 "p (c t) -> p c t", t=N)).then_inc(stp[(nst - 1) % 2], 16)
                sy.wait_ge(stp[0], 16 * (nst // 2))
                sy.wait_ge(stp[1], 16 * (nst // 2))

            @blk4.tensor
            def _(t):
                t.wait_ge(lw[2], 16)
                for s in range(nst):
                    for m in range(8):
                        k = 8 * s + m
                        if k >= 2:
                            t.wait_ge(wv, k - 1)
                        t.matmul(PW[k % 2][:, :], AoB[:, m * 128:(m + 1) * 128],
                                 obuf[:, s * N:(s + 1) * N], start=True, stop=True).then_inc(wm, 1)

            @blk4.vector
            def _(v):
                for s in range(nst):
                    ba = (s % 2) * 4096
                    v.wait_ge(lxs[s % 2], 16 * (s // 2 + 1))
                    if s >= 2:
                        v.wait_ge(stp[s % 2], 16 * (s // 2))
                    for m in range(8):
                        k = 8 * s + m
                        v.wait_ge(wm, k + 1)
                        v.tensor_tensor(od[:, ba + m * N:ba + (m + 1) * N], PW[k % 2][:, :],
                                        xs[:, ba + m * N:ba + (m + 1) * N], ALU.add).then_inc(wv, 1)
    return

def emit_D(E, src="s1", dst="u0"):
    nc = E.nc
    srcr = E.d[src].rearrange("(c p) t -> p c t", p=128)
    u0 = E.d[dst]
    from contextlib import ExitStack
    with ExitStack() as stk:
        gates = stk.enter_context(E.sbuf("d_gates", [32, 4096], BF16))
        IndB = stk.enter_context(E.sbuf("d_ind", [32, 4096], BF16))
        lxs = _dsem(E, "dx")
        lcw = _dsem(E, "dc", 2)
        gm1, ge1, gv1 = E.sem("dgm"), E.sem("dge"), E.sem("dgv")
        with ExitStack() as stk1:
            xs = stk1.enter_context(E.sbuf("d_xs", [128, 2 * 4096], F32))
            SelB = stk1.enter_context(E.sbuf("d_sel", [128, 8 * 32], F32))
            gbuf = stk1.enter_context(E.sbuf("d_g", [128, 2 * 32], F32))
            lbuf = stk1.enter_context(E.sbuf("d_l", [128, 2 * 32], F32))
            gw = stk1.enter_context(E.sbuf("d_gw", [128, 2 * 32], F32))
            mcol = stk1.enter_context(E.sbuf("d_m", [128, 2], F32))
            keep = stk1.enter_context(E.sbuf("d_keep", [128, 2 * 32], F32))
            gfb = stk1.enter_context(E.sbuf("d_gfb", [128, 2 * 32], BF16))
            pl0 = stk1.enter_context(E.psum("d_pl0", [128, 32], F32))
            pl1 = stk1.enter_context(E.psum("d_pl1", [128, 32], F32))
            idb = stk1.enter_context(E.sbuf("d_idb", [128, 128], BF16))
            pt0 = stk1.enter_context(E.psum("d_pt0", [32, 128], BF16))
            pt1 = stk1.enter_context(E.psum("d_pt1", [32, 128], BF16))
            with nc.Block() as blk:
                PL = [pl0, pl1]
                PT = [pt0, pt1]

                @blk.sync
                def _(sy):
                    sy.dma_start(SelB[:, :].rearrange("p (c n) -> p c n", n=32),
                                 E.d["SelT"].rearrange("(c p) n -> p c n", p=128)).then_inc(lcw[0], 16)
                    sy.dma_start(IndB[:, :], E.d["IndE"][:, :]).then_inc(lcw[1], 16)
                    sy.dma_start(idb[:, :], E.d["identb"][:, :]).then_inc(lcw[1], 16)
                    for s in range(NST):
                        if s >= 2:
                            sy.wait_ge(gm1, 8 * (s - 1))
                        sy.dma_start(xs[:, (s % 2) * 4096:(s % 2 + 1) * 4096].rearrange("p (c t) -> p c t", t=512),
                                     srcr[:, :, s * 512:(s + 1) * 512]).then_inc(lxs[s % 2], 16)

                @blk.tensor
                def _(t):
                    t.wait_ge(lcw[0], 16)
                    t.wait_ge(lcw[1], 32)
                    for s in range(NST):
                        for tt in range(4):
                            K = s * 4 + tt
                            t.wait_ge(lxs[s % 2], 16 * (s // 2 + 1))
                            if K >= 2:
                                t.wait_ge(ge1, 3 * (K - 2) + 1 if K >= 3 else 2)
                            for c in range(8):
                                mm = t.matmul(PL[K % 2][:, :], xs[:, (s % 2) * 4096 + c * 512 + tt * 128:
                                                                  (s % 2) * 4096 + c * 512 + (tt + 1) * 128],
                                              SelB[:, c * 32:(c + 1) * 32],
                                              start=(c == 0), stop=(c == 7))
                            mm.then_inc(gm1, 1)
                            t.wait_ge(gv1, 14 * K)
                            if K >= 1:
                                KK = K - 1
                                if KK >= 2:
                                    t.wait_ge(ge1, 3 * KK - 1)
                                t.transpose(PT[KK % 2][:, :], gfb[:, (KK % 2) * 32:(KK % 2 + 1) * 32],
                                            idb[:, :]).then_inc(gm1, 1)
                    t.wait_ge(gv1, 14 * 32)
                    KK = 31
                    t.wait_ge(ge1, 3 * KK - 1)
                    t.transpose(PT[KK % 2][:, :], gfb[:, (KK % 2) * 32:(KK % 2 + 1) * 32],
                                idb[:, :]).then_inc(gm1, 1)

                @blk.scalar
                def _(s_):
                    for K in range(32):
                        s_.wait_ge(gm1, 2 * K + 1)
                        if K >= 2:
                            s_.wait_ge(gv1, 14 * (K - 2) + 14)
                        s_.activation(gbuf[:, (K % 2) * 32:(K % 2 + 1) * 32], PL[K % 2][:, :],
                                      AF.Sigmoid).then_inc(ge1, 1)
                        s_.copy(lbuf[:, (K % 2) * 32:(K % 2 + 1) * 32], PL[K % 2][:, :]).then_inc(ge1, 1)
                        if K >= 1:
                            KK = K - 1
                            s_.copy(gates[:, KK * 128:(KK + 1) * 128], PT[KK % 2][:, :]).then_inc(ge1, 1)
                    s_.wait_ge(gm1, 64)
                    s_.copy(gates[:, 31 * 128:32 * 128], PT[31 % 2][:, :]).then_inc(ge1, 1)

                @blk.vector
                def _(v):
                    for K in range(32):
                        b = (K % 2) * 32
                        myv = 14 * K
                        v.wait_ge(ge1, 3 * K + 2)
                        v.tensor_copy(gw[:, b:b + 32], lbuf[:, b:b + 32]).then_inc(gv1, 1)
                        myv += 1
                        for it in range(4):
                            v.wait_ge(gv1, myv)
                            v.reduce_max(mcol[:, K % 2:K % 2 + 1], gw[:, b:b + 32],
                                         axis=AX.X).then_inc(gv1, 1)
                            myv += 1
                            if it < 3:
                                v.wait_ge(gv1, myv)
                                v.tensor_scalar(keep[:, b:b + 32], gw[:, b:b + 32],
                                                mcol[:, K % 2:K % 2 + 1], None, ALU.is_ge).then_inc(gv1, 1)
                                myv += 1
                                v.wait_ge(gv1, myv)
                                v.scalar_tensor_tensor(gw[:, b:b + 32], keep[:, b:b + 32], -1e30,
                                                       gw[:, b:b + 32], ALU.mult, ALU.add).then_inc(gv1, 1)
                                myv += 1
                        v.wait_ge(gv1, myv)
                        v.tensor_scalar(keep[:, b:b + 32], lbuf[:, b:b + 32],
                                        mcol[:, K % 2:K % 2 + 1], None, ALU.is_ge).then_inc(gv1, 1)
                        myv += 1
                        v.wait_ge(gv1, myv)
                        v.tensor_tensor(keep[:, b:b + 32], gbuf[:, b:b + 32], keep[:, b:b + 32],
                                        ALU.mult).then_inc(gv1, 1)
                        myv += 1
                        v.wait_ge(gv1, myv)
                        v.tensor_copy(gfb[:, b:b + 32], keep[:, b:b + 32]).then_inc(gv1, 1)
                        myv += 1
                        assert myv == 14 * (K + 1), (myv, K)

        NG, GE, NSP, SWD = 4, 8, 16, 256
        wk = _dsem(E, "dwk", 8, kind="sw")
        wv = _dsem(E, "dwv", 8, kind="sw")
        lxb = _dsem(E, "dxb", kind="sw")
        du = _dsem(E, "du", NSP, kind="sw")
        km, bm, vm = E.sem("dkm"), E.sem("dbm"), E.sem("dvm")
        sev, vv, vo = E.sem("dse"), E.sem("dvv"), E.sem("dvo")
        with ExitStack() as stk2:
            KeB = stk2.enter_context(E.sbuf("d_ke", [128, 8 * 4096], BF16))
            VeB = stk2.enter_context(E.sbuf("d_ve", [128, 8 * 4096], BF16))
            xb = stk2.enter_context(E.sbuf("d_xb", [128, 2 * 2048], BF16))
            scb = stk2.enter_context(E.sbuf("d_scb", [128, 2 * 1024], BF16))
            scg = stk2.enter_context(E.sbuf("d_scg", [128, 2 * 1024], BF16))
            odb = stk2.enter_context(E.sbuf("d_od", [128, 2 * 2048], F32))
            po = [stk2.enter_context(E.psum(f"d_po{j}", [128, 512], F32)) for j in range(4)]
            psc = [stk2.enter_context(E.psum(f"d_psc{j}", [128, 256], F32)) for j in range(3)]
            pgb = stk2.enter_context(E.psum("d_pgb", [128, 256], F32))
            with nc.Block() as blk2:
                Keys3 = E.d["Keys"]
                Vals3 = E.d["Values"]

                @blk2.gpsimd
                def _(g):
                    for gi in range(NG):
                        for ee in range(GE):
                            e = gi * GE + ee
                            if gi >= 1:
                                g.wait_ge(km, 32 * 16 * gi)
                                g.wait_ge(vm, 8 * 16 * gi)
                            g.dma_start(KeB[:, ee * 4096:(ee + 1) * 4096].rearrange("p (c f) -> p c f", f=512),
                                        Keys3[e].rearrange("(c p) f -> p c f", p=128)).then_inc(wk[ee], 16)
                            g.dma_start(VeB[:, ee * 4096:(ee + 1) * 4096].rearrange("p (f v) -> p f v", v=1024),
                                        Vals3[e].rearrange("(f p) v -> p f v", p=128)).then_inc(wv[ee], 16)
                        for s in range(NSP):
                            P = gi * NSP + s
                            if P >= 2:
                                g.wait_ge(km, 32 * (P - 1))
                            g.dma_start(xb[:, (P % 2) * 2048:(P % 2 + 1) * 2048].rearrange("p (c t) -> p c t", t=SWD),
                                        srcr[:, :, s * SWD:(s + 1) * SWD]).then_inc(lxb[P % 2], 16)
                            if P >= 1:
                                PP = P - 1
                                gp, sp = divmod(PP, NSP)
                                g.wait_ge(vo, 4 * (PP + 1))
                                if gp >= 1:
                                    g.wait_ge(du[sp], 64 * gp)
                                for j in range(4):
                                    tsub, vh = divmod(j, 2)
                                    d = u0[sp * SWD + tsub * 128: sp * SWD + (tsub + 1) * 128,
                                           vh * 512:(vh + 1) * 512]
                                    srcap = odb[:, (PP % 2) * 2048 + j * 512:(PP % 2) * 2048 + (j + 1) * 512]
                                    if gp == 0:
                                        g.dma_start(d, srcap).then_inc(du[sp], 16)
                                    else:
                                        g.dma_start(d, srcap, accum_op=ALU.add).then_inc(du[sp], 16)
                    PP = NG * NSP - 1
                    gp, sp = divmod(PP, NSP)
                    g.wait_ge(vo, 4 * (PP + 1))
                    g.wait_ge(du[sp], 64 * gp)
                    for j in range(4):
                        tsub, vh = divmod(j, 2)
                        d = u0[sp * SWD + tsub * 128: sp * SWD + (tsub + 1) * 128,
                               vh * 512:(vh + 1) * 512]
                        g.dma_start(d, odb[:, (PP % 2) * 2048 + j * 512:(PP % 2) * 2048 + (j + 1) * 512],
                                    accum_op=ALU.add).then_inc(du[sp], 16)
                    for sp in range(NSP):
                        g.wait_ge(du[sp], 64 * NG)

                @blk2.tensor
                def _(t):
                    for gi in range(NG):
                        for s in range(NSP):
                            P = gi * NSP + s
                            t.wait_ge(lxb[P % 2], 16 * (P // 2 + 1))
                            for step in range(GE + 1):
                                if step < GE:
                                    ee = step
                                    EIX = P * 8 + ee
                                    t.wait_ge(wk[ee], 16 * (gi + 1))
                                    for fc in range(4):
                                        KIX = EIX * 4 + fc
                                        if KIX >= 3:
                                            t.wait_ge(sev, KIX - 2)
                                        for c in range(8):
                                            mm = t.matmul(psc[KIX % 3][:, :],
                                                          KeB[:, ee * 4096 + c * 512 + fc * 128:
                                                              ee * 4096 + c * 512 + (fc + 1) * 128],
                                                          xb[:, (P % 2) * 2048 + c * SWD:(P % 2) * 2048 + (c + 1) * SWD],
                                                          start=(c == 0), stop=(c == 7))
                                        mm.then_inc(km, 1)
                                    if EIX >= 1:
                                        t.wait_ge(vv, 4 * EIX)
                                    t.matmul(pgb[:, :], IndB[:, (gi * GE + ee) * 128:(gi * GE + ee + 1) * 128],
                                             gates[:, s * SWD:(s + 1) * SWD], start=True,
                                             stop=True).then_inc(bm, 1)
                                if step >= 1:
                                    ee = step - 1
                                    EIX = P * 8 + ee
                                    t.wait_ge(wv[ee], 16 * (gi + 1))
                                    t.wait_ge(vv, 4 * (EIX + 1))
                                    if ee == 0:
                                        t.wait_ge(vo, 4 * P)
                                    for fc in range(4):
                                        for j in range(4):
                                            tsub, vh = divmod(j, 2)
                                            mm = t.matmul(po[j][:, :],
                                                     scg[:, (EIX % 2) * 1024 + fc * 256 + tsub * 128:
                                                         (EIX % 2) * 1024 + fc * 256 + (tsub + 1) * 128],
                                                     VeB[:, ee * 4096 + fc * 1024 + vh * 512:
                                                         ee * 4096 + fc * 1024 + (vh + 1) * 512],
                                                     start=(ee == 0 and fc == 0), stop=(ee == 7 and fc == 3))
                                    mm.then_inc(vm, 1)

                @blk2.scalar
                def _(s_):
                    for P in range(NG * NSP):
                        for ee in range(GE):
                            EIX = P * 8 + ee
                            for fc in range(4):
                                KIX = EIX * 4 + fc
                                s_.wait_ge(km, KIX + 1)
                                if EIX >= 2:
                                    s_.wait_ge(vv, 4 * (EIX - 1))
                                s_.activation(scb[:, (EIX % 2) * 1024 + fc * 256:(EIX % 2) * 1024 + (fc + 1) * 256],
                                              psc[KIX % 3][:, :], AF.Relu).then_inc(sev, 1)

                @blk2.vector
                def _(v):
                    for P in range(NG * NSP):
                        for ee in range(GE):
                            EIX = P * 8 + ee
                            v.wait_ge(bm, EIX + 1)
                            for fc in range(4):
                                KIX = EIX * 4 + fc
                                v.wait_ge(sev, KIX + 1)
                                if EIX >= 2:
                                    v.wait_ge(vm, EIX - 1)
                                v.tensor_tensor(scg[:, (EIX % 2) * 1024 + fc * 256:(EIX % 2) * 1024 + (fc + 1) * 256],
                                                scb[:, (EIX % 2) * 1024 + fc * 256:(EIX % 2) * 1024 + (fc + 1) * 256],
                                                pgb[:, :], ALU.mult).then_inc(vv, 1)
                        v.wait_ge(vm, 8 * (P + 1))
                        if P >= 2:
                            PP = P - 2
                            gp, sp = divmod(PP, NSP)
                            v.wait_ge(du[sp], 64 * gp + 64)
                        for j in range(4):
                            v.tensor_copy(odb[:, (P % 2) * 2048 + j * 512:(P % 2) * 2048 + (j + 1) * 512],
                                          po[j][:, :]).then_inc(vo, 1)
    return


def emit_final(E, usrc="u0", xsrc="s1", dst="out"):
    nc = E.nc
    u0 = E.d[usrc]
    xr = E.d[xsrc].rearrange("(c p) t -> p c t", p=128)
    out = E.d[dst]
    lc = E.sem("flc")
    lu, lx, stp = _dsem(E, "flu"), _dsem(E, "flx"), _dsem(E, "fst")
    fm, fe, fv = E.sem("ffm"), E.sem("ffe"), E.sem("ffv")
    from contextlib import ExitStack
    with ExitStack() as stk:
        ub = stk.enter_context(E.sbuf("f_u", [128, 2 * 1024], F32))
        xc = stk.enter_context(E.sbuf("f_x", [128, 2 * 1024], F32))
        sm = stk.enter_context(E.sbuf("f_s", [128, 3 * 1024], F32))
        jk = stk.enter_context(E.sbuf("f_j", [128, 2048], F32))
        idf = stk.enter_context(E.sbuf("f_id", [128, 128], F32))
        cols = stk.enter_context(E.sbuf("f_c", [128, 3 * 8], F32))
        ob = stk.enter_context(E.sbuf("f_o", [128, 2 * 1024], F32))
        pt = [stk.enter_context(E.psum(f"f_pt{i}", [128, 128], F32)) for i in range(4)]
        I_AD, I_SM, I_FN = {}, {}, {}
        cnt = 0
        for it in range(35):
            if it < 32:
                for c in range(8):
                    cnt += 1; I_AD[(it, c)] = cnt
            if 1 <= it <= 32:
                cnt += 5; I_SM[it - 1] = cnt
            if 2 <= it <= 34:
                cnt += 2; I_FN[it - 2] = cnt
        I_AC, I_LE = {}, {}
        cnt = 0
        for it in range(34):
            if 1 <= it <= 32:
                cnt += 2; I_AC[it - 1] = cnt
            if 2 <= it <= 33:
                cnt += 2; I_LE[it - 2] = cnt
        with nc.Block() as blk:
            @blk.sync
            def _(sy):
                sy.dma_start(idf[:, :], E.d["identf"][:, :]).then_inc(lc, 16)
                for tt in range(32):
                    if tt >= 2:
                        sy.wait_ge(fv, I_AD[(tt - 2, 7)])
                    sy.dma_start(ub[:, (tt % 2) * 1024:(tt % 2 + 1) * 1024],
                                 u0[tt * 128:(tt + 1) * 128, :]).then_inc(lu[tt % 2], 16)
                    sy.dma_start(xc[:, (tt % 2) * 1024:(tt % 2 + 1) * 1024].rearrange("p (c t) -> p c t", t=128),
                                 xr[:, :, tt * 128:(tt + 1) * 128]).then_inc(lx[tt % 2], 16)
                    if tt >= 2:
                        sy.wait_ge(fv, I_FN[tt - 2])
                        sy.dma_start(out[(tt - 2) * 128:(tt - 1) * 128, :],
                                     ob[:, (tt % 2) * 1024:(tt % 2 + 1) * 1024]).then_inc(stp[tt % 2], 16)
                for tt in (30, 31):
                    sy.wait_ge(fv, I_FN[tt])
                    sy.dma_start(out[tt * 128:(tt + 1) * 128, :],
                                 ob[:, (tt % 2) * 1024:(tt % 2 + 1) * 1024]).then_inc(stp[tt % 2], 16)
                sy.wait_ge(stp[0], 16 * 16)
                sy.wait_ge(stp[1], 16 * 16)

            @blk.tensor
            def _(t):
                t.wait_ge(lc, 16)
                for tt in range(32):
                    t.wait_ge(lx[tt % 2], 16 * (tt // 2 + 1))
                    for c in range(8):
                        k = tt * 8 + c
                        if k >= 4:
                            kk = k - 4
                            t.wait_ge(fv, I_AD[(kk // 8, kk % 8)])
                        t.transpose(pt[k % 4][:, :],
                                    xc[:, (tt % 2) * 1024 + c * 128:(tt % 2) * 1024 + (c + 1) * 128],
                                    idf[:, :]).then_inc(fm, 1)

            @blk.vector
            def _(v):
                for it in range(35):
                    if it < 32:
                        tt = it
                        base = (tt % 2) * 1024
                        sb = (tt % 3) * 1024
                        v.wait_ge(lu[tt % 2], 16 * (tt // 2 + 1))
                        if tt >= 3:
                            v.wait_ge(fv, I_FN[tt - 3])
                        for c in range(8):
                            k = tt * 8 + c
                            v.wait_ge(fm, k + 1)
                            v.tensor_tensor(sm[:, sb + c * 128:sb + (c + 1) * 128],
                                            ub[:, base + c * 128:base + (c + 1) * 128],
                                            pt[k % 4][:, :], ALU.add).then_inc(fv, 1)
                    if 1 <= it <= 32:
                        tt = it - 1
                        cb = (tt % 3) * 8
                        v.wait_ge(fe, I_AC[tt])
                        v.tensor_scalar(cols[:, cb + 0:cb + 1], cols[:, cb + 0:cb + 1],
                                        1.0 / 1024.0, None, ALU.mult).then_inc(fv, 1)
                        v.wait_ge(fv, I_SM[tt] - 4)
                        v.tensor_scalar(cols[:, cb + 1:cb + 2], cols[:, cb + 1:cb + 2],
                                        1.0 / 1024.0, None, ALU.mult).then_inc(fv, 1)
                        v.wait_ge(fv, I_SM[tt] - 3)
                        v.scalar_tensor_tensor(cols[:, cb + 2:cb + 3], cols[:, cb + 0:cb + 1], -1.0,
                                               cols[:, cb + 0:cb + 1], ALU.mult, ALU.mult).then_inc(fv, 1)
                        v.wait_ge(fv, I_SM[tt] - 2)
                        v.tensor_tensor(cols[:, cb + 3:cb + 4], cols[:, cb + 1:cb + 2],
                                        cols[:, cb + 2:cb + 3], ALU.add).then_inc(fv, 1)
                        v.wait_ge(fv, I_SM[tt] - 1)
                        v.tensor_scalar(cols[:, cb + 3:cb + 4], cols[:, cb + 3:cb + 4],
                                        1e-5, None, ALU.add).then_inc(fv, 1)
                    if 2 <= it <= 34:
                        tt = it - 2
                        cb = (tt % 3) * 8
                        sb = (tt % 3) * 1024
                        v.wait_ge(fe, I_LE[tt])
                        v.scalar_tensor_tensor(cols[:, cb + 5:cb + 6], cols[:, cb + 0:cb + 1], -1.0,
                                               cols[:, cb + 4:cb + 5], ALU.mult, ALU.mult).then_inc(fv, 1)
                        if tt >= 2:
                            v.wait_ge(stp[tt % 2], 16 * (tt // 2))
                        v.wait_ge(fv, I_FN[tt] - 1)
                        v.tensor_scalar(ob[:, (tt % 2) * 1024:(tt % 2 + 1) * 1024],
                                        sm[:, sb:sb + 1024],
                                        cols[:, cb + 4:cb + 5], cols[:, cb + 5:cb + 6],
                                        ALU.mult, ALU.add).then_inc(fv, 1)

            @blk.scalar
            def _(s_):
                for it in range(34):
                    if 1 <= it <= 32:
                        tt = it - 1
                        cb = (tt % 3) * 8
                        sb = (tt % 3) * 1024
                        s_.wait_ge(fv, I_AD[(tt, 7)])
                        if tt >= 3:
                            s_.wait_ge(fv, I_FN[tt - 3])
                        if tt >= 1:
                            s_.wait_ge(fe, I_AC[tt] - 2)
                        s_.activation(jk[:, 0:1024], sm[:, sb:sb + 1024], AF.Copy,
                                      accum_out=cols[:, cb + 0:cb + 1]).then_inc(fe, 1)
                        s_.wait_ge(fe, I_AC[tt] - 1)
                        s_.activation(jk[:, 1024:2048], sm[:, sb:sb + 1024], AF.Square,
                                      accum_out=cols[:, cb + 1:cb + 2]).then_inc(fe, 1)
                    if 2 <= it <= 33:
                        tt = it - 2
                        cb = (tt % 3) * 8
                        s_.wait_ge(fv, I_SM[tt])
                        s_.activation(cols[:, cb + 4:cb + 5], cols[:, cb + 3:cb + 4],
                                      AF.Ln).then_inc(fe, 1)
                        s_.wait_ge(fe, I_LE[tt] - 1)
                        s_.activation(cols[:, cb + 4:cb + 5], cols[:, cb + 4:cb + 5], AF.Exp,
                                      scale=-0.5).then_inc(fe, 1)
    return


ALL_SPECS = {}
ALL_SPECS.update(CONST_SPECS)
ALL_SPECS.update(WEIGHT_SPECS)
ALL_SPECS.update(STREAM_SPECS)
ALL_SPECS["x0"] = ([4096, 1024], F32)
ALL_SPECS["out"] = ([4096, 1024], F32)


class LazyDram:
    def __init__(self, nc, ext_in, ext_out):
        self.nc = nc
        self.ext_in = set(ext_in)
        self.ext_out = set(ext_out)
        self.aps = {}
        self.used_in = set()
        self.used_out = set()

    def __getitem__(self, name):
        if name not in self.aps:
            shape, dt = ALL_SPECS[name]
            if name in self.ext_in:
                kind = "ExternalInput"
                self.used_in.add(name)
            elif name in self.ext_out:
                kind = "ExternalOutput"
                self.used_out.add(name)
            else:
                kind = "Internal"
            self.aps[name] = self.nc.dram_tensor(name, shape, dt, kind=kind).ap()
        return self.aps[name]


def build_program(phases, ext_in, ext_out):
    from contextlib import ExitStack
    nc = bass.Bass("TRN2", target_bir_lowering=False)
    dram = LazyDram(nc, ext_in, ext_out)
    with ExitStack() as stack:
        E = Env(nc, stack, dram)
        import os
        recycle = os.environ.get("MK_NO_RECYCLE", "") != "1"
        for i, ph in enumerate(phases):
            ph(E)
            if recycle and i + 1 < len(phases):
                E.end_phase()
    return nc, dram.used_in, dram.used_out

def emit_B(E, src="s1", dst="t0"):
    nc = E.nc
    N = SW
    srcr = E.d[src].rearrange("(c p) t -> p c t", p=128)
    dstr = E.d[dst].rearrange("(c p) t -> p c t", p=128)
    lw = _dsem(E, "bw", 3, kind="sw") + _dsem(E, "bwh", 2)
    lx = _dsem(E, "bx")
    stp = _dsem(E, "bs")
    smm, sev, svv = E.sem("bm"), E.sem("be"), E.sem("bv")
    msz = E.sem("bz")
    V_PER = 24 + 8 + 16
    S_PER = 16
    M_PER = 24
    with (
        E.sbuf("b_wr", [128, 8 * 1024], BF16) as WrB,
        E.sbuf("b_wk", [128, 8 * 1024], BF16) as WkB,
        E.sbuf("b_wv", [128, 8 * 1024], BF16) as WvB,
        E.sbuf("b_mu", [128, 16], F32) as muB,
        E.sbuf("b_xs", [128, 2 * 8 * 513], F32) as xs,
        E.sbuf("b_d", [128, 4096], F32) as db,
        E.sbuf("b_xr", [128, 2 * 4096], BF16) as xrb,
        E.sbuf("b_xk", [128, 2 * 4096], BF16) as xkb,
        E.sbuf("b_kb", [128, 4096], BF16) as kb,
        E.sbuf("b_k2", [128, 4096], BF16) as k2b,
        E.sbuf("b_rb", [128, 4096], BF16) as rb,
        E.sbuf("b_od", [128, 2 * 4096], F32) as od,
        E.psum("b_pk0", [128, 512], F32) as pk0,
        E.psum("b_pk1", [128, 512], F32) as pk1,
        E.psum("b_pr0", [128, 512], F32) as pr0,
        E.psum("b_pr1", [128, 512], F32) as pr1,
        E.psum("b_pv0", [128, 512], F32) as pv0,
        E.psum("b_pv1", [128, 512], F32) as pv1,
        nc.Block() as blk,
    ):
        nst = NST
        PK = [pk0, pk1]; PR = [pr0, pr1]; PV = [pv0, pv1]

        @blk.gpsimd
        def _(g):
            g.dma_start(WrB[:, :].rearrange("p (c n) -> p c n", n=1024),
                        E.d["WrT"].rearrange("(c p) n -> p c n", p=128)).then_inc(lw[0], 16)
            g.dma_start(WkB[:, :].rearrange("p (c n) -> p c n", n=1024),
                        E.d["WkT"].rearrange("(c p) n -> p c n", p=128)).then_inc(lw[1], 16)
            g.dma_start(WvB[:, :].rearrange("p (c n) -> p c n", n=1024),
                        E.d["WvT"].rearrange("(c p) n -> p c n", p=128)).then_inc(lw[2], 16)

        @blk.sync
        def _(sy):
            sy.dma_start(muB[:, 0:8], E.d["muR"][:, :]).then_inc(lw[3], 16)
            sy.dma_start(muB[:, 8:16], E.d["muK"][:, :]).then_inc(lw[4], 16)
            for s in range(nst):
                base = (s % 2) * (8 * 513)
                if s >= 2:
                    sy.wait_ge(svv, V_PER * (s - 1))
                if s == 0:
                    sy.dma_start(
                        xs[:, 0:8 * 513].rearrange("p (c t) -> p c t", t=513)[:, :, 1:513],
                        srcr[:, :, 0:512]).then_inc(lx[0], 16)
                else:
                    sy.dma_start(
                        xs[:, base:base + 8 * 513].rearrange("p (c t) -> p c t", t=513),
                        srcr[:, :, s * N - 1:(s + 1) * N]).then_inc(lx[s % 2], 16)
                if s >= 1:
                    sy.wait_ge(svv, V_PER * s)
                    sy.dma_start(dstr[:, :, (s - 1) * N:s * N],
                                 od[:, ((s - 1) % 2) * 4096:((s - 1) % 2 + 1) * 4096].rearrange(
                                     "p (c t) -> p c t", t=N)).then_inc(stp[(s - 1) % 2], 16)
            sy.wait_ge(svv, V_PER * nst)
            sy.dma_start(dstr[:, :, (nst - 1) * N:nst * N],
                         od[:, ((nst - 1) % 2) * 4096:((nst - 1) % 2 + 1) * 4096].rearrange(
                             "p (c t) -> p c t", t=N)).then_inc(stp[(nst - 1) % 2], 16)
            sy.wait_ge(stp[0], 16 * (nst // 2))
            sy.wait_ge(stp[1], 16 * (nst // 2))

        @blk.vector
        def _(v):
            v.wait_ge(lw[3], 16)
            v.wait_ge(lw[4], 16)
            for s in range(nst):
                bx = (s % 2) * (8 * 513)
                ba = (s % 2) * 4096
                myv = V_PER * s
                v.wait_ge(lx[s % 2], 16 * (s // 2 + 1))
                if s == 0:
                    v.memset(xs[:, 0:8 * 513].rearrange("p (c t) -> p c t", t=513)[:, :, 0:1],
                             0.0).then_inc(msz, 1)
                    v.wait_ge(msz, 1)
                if s >= 2:
                    v.wait_ge(smm, M_PER * (s - 1))
                for c in range(8):
                    v.tensor_tensor(db[:, c * N:(c + 1) * N],
                                    xs[:, bx + c * 513 + 1:bx + c * 513 + 513],
                                    xs[:, bx + c * 513:bx + c * 513 + 512], ALU.subtract).then_inc(svv, 1)
                    myv += 1
                v.wait_ge(svv, myv)
                for c in range(8):
                    v.scalar_tensor_tensor(xrb[:, ba + c * N:ba + (c + 1) * N],
                                           db[:, c * N:(c + 1) * N], muB[:, c:c + 1],
                                           xs[:, bx + c * 513:bx + c * 513 + 512],
                                           ALU.mult, ALU.add).then_inc(svv, 1)
                    myv += 1
                for c in range(8):
                    v.scalar_tensor_tensor(xkb[:, ba + c * N:ba + (c + 1) * N],
                                           db[:, c * N:(c + 1) * N], muB[:, 8 + c:9 + c],
                                           xs[:, bx + c * 513:bx + c * 513 + 512],
                                           ALU.mult, ALU.add).then_inc(svv, 1)
                    myv += 1
                for m in range(8):
                    v.wait_ge(sev, S_PER * s + m + 1)
                    v.tensor_tensor(k2b[:, m * N:(m + 1) * N], kb[:, m * N:(m + 1) * N],
                                    kb[:, m * N:(m + 1) * N], ALU.mult).then_inc(svv, 1)
                    myv += 1
                if s >= 2:
                    v.wait_ge(stp[s % 2], 16 * (s // 2))
                for m in range(8):
                    v.wait_ge(smm, M_PER * s + 16 + m + 1)
                    v.wait_ge(sev, S_PER * s + 8 + m + 1)
                    v.tensor_tensor(od[:, ba + m * N:ba + (m + 1) * N],
                                    PV[m % 2][:, :], rb[:, m * N:(m + 1) * N], ALU.mult).then_inc(svv, 1)
                    myv += 1
                    v.wait_ge(svv, myv)
                    v.tensor_tensor(od[:, ba + m * N:ba + (m + 1) * N],
                                    od[:, ba + m * N:ba + (m + 1) * N],
                                    xs[:, bx + m * 513 + 1:bx + m * 513 + 513], ALU.add).then_inc(svv, 1)
                    myv += 1
                assert myv == V_PER * (s + 1)

        @blk.scalar
        def _(s_):
            for s in range(nst):
                mys = S_PER * s
                for m in range(8):
                    s_.wait_ge(smm, M_PER * s + m + 1)
                    s_.activation(kb[:, m * N:(m + 1) * N], PK[m % 2][:, :], AF.Relu).then_inc(sev, 1)
                    mys += 1
                for m in range(8):
                    s_.wait_ge(smm, M_PER * s + 8 + m + 1)
                    s_.activation(rb[:, m * N:(m + 1) * N], PR[m % 2][:, :], AF.Sigmoid).then_inc(sev, 1)
                    mys += 1
                assert mys == S_PER * (s + 1)

        @blk.tensor
        def _(t):
            t.wait_ge(lw[0], 16)
            t.wait_ge(lw[1], 16)
            t.wait_ge(lw[2], 16)
            for s in range(nst):
                ba = (s % 2) * 4096
                t.wait_ge(svv, V_PER * s + 24)
                for m in range(8):
                    if m >= 2:
                        t.wait_ge(sev, S_PER * s + (m - 1))
                    elif s >= 1:
                        t.wait_ge(sev, S_PER * (s - 1) + 8 + 8)
                    for c in range(8):
                        mm = t.matmul(PK[m % 2][:, :],
                                      WkB[:, c * 1024 + m * 128:c * 1024 + (m + 1) * 128],
                                      xkb[:, ba + c * N:ba + (c + 1) * N],
                                      start=(c == 0), stop=(c == 7))
                    mm.then_inc(smm, 1)
                for m in range(8):
                    if m >= 2:
                        t.wait_ge(sev, S_PER * s + 8 + (m - 1))
                    for c in range(8):
                        mm = t.matmul(PR[m % 2][:, :],
                                      WrB[:, c * 1024 + m * 128:c * 1024 + (m + 1) * 128],
                                      xrb[:, ba + c * N:ba + (c + 1) * N],
                                      start=(c == 0), stop=(c == 7))
                    mm.then_inc(smm, 1)
                t.wait_ge(svv, V_PER * s + 32)
                for m in range(8):
                    if m >= 2:
                        t.wait_ge(svv, V_PER * s + 32 + 2 * (m - 1) + 1)
                    for c in range(8):
                        mm = t.matmul(PV[m % 2][:, :],
                                      WvB[:, c * 1024 + m * 128:c * 1024 + (m + 1) * 128],
                                      k2b[:, c * N:(c + 1) * N],
                                      start=(c == 0), stop=(c == 7))
                    mm.then_inc(smm, 1)
    return

def emit_C(E, src="s0", dst="t0"):
    nc = E.nc
    N = SW
    srcr = E.d[src].rearrange("(c p) t -> p c t", p=128)
    dstr = E.d[dst].rearrange("(c p) t -> p c t", p=128)
    lw = _dsem(E, "cw", 3, kind="sw") + _dsem(E, "cwh", 1)
    lbc = _dsem(E, "cb", kind="sw")
    lxs = _dsem(E, "cx")
    stp = _dsem(E, "cs")
    qm, ce1, cv1 = E.sem("cqm"), E.sem("ce1"), E.sem("cv1")
    tmq, tve = E.sem("ctm"), E.sem("ctv")
    cm, ce, cv, mz = E.sem("ccm"), E.sem("cce"), E.sem("ccv"), E.sem("cmz")
    wm = E.sem("cwm")
    nst = NST
    from contextlib import ExitStack
    with ExitStack() as stk:
        AqkB = stk.enter_context(E.sbuf("c_aqk", [128, 8 * 128], BF16))
        AvB = stk.enter_context(E.sbuf("c_av", [128, 8 * 64], BF16))
        AoB = stk.enter_context(E.sbuf("c_ao", [64, 1024], BF16))
        idb = stk.enter_context(E.sbuf("c_idb", [128, 128], BF16))
        trib = stk.enter_context(E.sbuf("c_tri", [128, 128], BF16))
        onb = stk.enter_context(E.sbuf("c_onb", [128, 128], BF16))
        xb = stk.enter_context(E.sbuf("c_xb", [128, 2 * 4096], BF16))
        ph1 = stk.enter_context(E.sbuf("c_ph1", [128, 512], F32))
        ph2 = stk.enter_context(E.sbuf("c_ph2", [128, 512], F32))
        ph3 = stk.enter_context(E.sbuf("c_ph3", [128, 512], F32))
        qb = stk.enter_context(E.sbuf("c_qb", [64, 4096], BF16))
        kb = stk.enter_context(E.sbuf("c_kb", [64, 4096], BF16))
        vb = stk.enter_context(E.sbuf("c_vb", [64, 4096], BF16))
        ktm = stk.enter_context(E.sbuf("c_ktm", [128, 32 * 64], BF16))
        vtm = stk.enter_context(E.sbuf("c_vtm", [128, 32 * 64], BF16))
        STb = stk.enter_context(E.sbuf("c_ST", [64, 64], BF16))
        STf = stk.enter_context(E.sbuf("c_STf", [64, 64], F32))
        zf = stk.enter_context(E.sbuf("c_zf", [64, 1], F32))
        zi = stk.enter_context(E.sbuf("c_zi", [64, 1], F32))
        zb = stk.enter_context(E.sbuf("c_zb", [64, 1], BF16))
        gs = stk.enter_context(E.sbuf("c_gs", [128, 128], BF16))
        gks = stk.enter_context(E.sbuf("c_gks", [128, 128], BF16))
        gm = stk.enter_context(E.sbuf("c_gm", [128, 128], BF16))
        gkm = stk.enter_context(E.sbuf("c_gkm", [128, 128], BF16))
        dr = stk.enter_context(E.sbuf("c_dr", [1, 128], F32))
        dl = stk.enter_context(E.sbuf("c_dl", [1, 128], F32))
        drb = stk.enter_context(E.sbuf("c_drb", [1, 128], BF16))
        ob = stk.enter_context(E.sbuf("c_ob", [64, 128], BF16))
        obuf = stk.enter_context(E.sbuf("c_obuf", [64, 4096], BF16))
        xs = stk.enter_context(E.sbuf("c_xs", [128, 2 * 4096], F32))
        od = stk.enter_context(E.sbuf("c_od", [128, 2 * 4096], F32))
        with (
            E.psum("c_pqk0", [128, 512], F32) as pqk0,
            E.psum("c_pqk1", [128, 512], F32) as pqk1,
            E.psum("c_pv0", [64, 512], F32) as pv0,
            E.psum("c_pv1", [64, 512], F32) as pv1,
            nc.Block() as blk,
        ):
            PQK = [pqk0, pqk1]; PV = [pv0, pv1]

            @blk.gpsimd
            def _(g):
                g.dma_start(AqkB[:, :].rearrange("p (c n) -> p c n", n=128),
                            E.d["Aqk"].rearrange("(c p) n -> p c n", p=128)).then_inc(lw[0], 16)
                g.dma_start(AvB[:, :].rearrange("p (c n) -> p c n", n=64),
                            E.d["Av"].rearrange("(c p) n -> p c n", p=128)).then_inc(lw[1], 16)
                g.dma_start(AoB[:, :], E.d["Ao"][:, :]).then_inc(lw[2], 16)
                for s in range(nst):
                    if s >= 2:
                        g.wait_ge(qm, 2 * (s - 1))
                    g.dma_start(xb[:, (s % 2) * 4096:(s % 2 + 1) * 4096].rearrange("p (c t) -> p c t", t=N),
                                srcr[:, :, s * N:(s + 1) * N]).then_inc(lbc[s % 2], 16)

            @blk.sync
            def _(sy):
                sy.dma_start(idb[:, :], E.d["identb"][:, :]).then_inc(lw[3], 16)
                sy.dma_start(trib[:, :], E.d["tri"][:, :]).then_inc(lw[3], 16)
                sy.dma_start(onb[:, :], E.d["onesb"][:, :]).then_inc(lw[3], 16)

            @blk.tensor
            def _(t):
                t.wait_ge(lw[0], 16)
                t.wait_ge(lw[1], 16)
                for s in range(nst):
                    ba = (s % 2) * 4096
                    t.wait_ge(lbc[s % 2], 16 * (s // 2 + 1))
                    if s >= 2:
                        t.wait_ge(ce1, 4 * (s - 2) + 3)
                    for c in range(8):
                        mm = t.matmul(PQK[s % 2][:, :], AqkB[:, c * 128:(c + 1) * 128],
                                      xb[:, ba + c * N:ba + (c + 1) * N],
                                      start=(c == 0), stop=(c == 7))
                    mm.then_inc(qm, 1)
                    if s >= 2:
                        t.wait_ge(ce1, 4 * (s - 2) + 4)
                    for c in range(8):
                        mm = t.matmul(PV[s % 2][:, :], AvB[:, c * 64:(c + 1) * 64],
                                      xb[:, ba + c * N:ba + (c + 1) * N],
                                      start=(c == 0), stop=(c == 7))
                    mm.then_inc(qm, 1)

            @blk.scalar
            def _(s_):
                for s in range(nst):
                    s_.wait_ge(qm, 2 * s + 1)
                    if s >= 1:
                        s_.wait_ge(cv1, 2 * s)
                    s_.activation(ph1[:, :], PQK[s % 2][:, :], AF.Relu, scale=-1.0).then_inc(ce1, 1)
                    s_.wait_ge(ce1, 4 * s + 1)
                    s_.activation(ph2[:, :], ph1[:, :], AF.Exp, scale=-1.0).then_inc(ce1, 1)
                    s_.activation(ph3[:, :], PQK[s % 2][:, :], AF.Relu).then_inc(ce1, 1)
                    s_.wait_ge(qm, 2 * s + 2)
                    s_.copy(vb[:, s * N:(s + 1) * N], PV[s % 2][:, :]).then_inc(ce1, 1)

            @blk.vector
            def _(v):
                for s in range(nst):
                    v.wait_ge(ce1, 4 * s + 3)
                    v.tensor_tensor(qb[:, s * N:(s + 1) * N], ph2[0:64, :], ph3[0:64, :],
                                    ALU.add).then_inc(cv1, 1)
                    v.tensor_tensor(kb[:, s * N:(s + 1) * N], ph2[64:128, :], ph3[64:128, :],
                                    ALU.add).then_inc(cv1, 1)

        with (
            E.psum("c_pt0", [128, 64], BF16) as pt0,
            E.psum("c_pt1", [128, 64], BF16) as pt1,
            nc.Block() as blk2,
        ):
            PT = [pt0, pt1]

            @blk2.tensor
            def _(t):
                t.wait_ge(lw[3], 48)
                for k in range(64):
                    c = k // 2
                    srcb = kb if k % 2 == 0 else vb
                    if k >= 2:
                        t.wait_ge(tve, k - 1)
                    t.transpose(PT[k % 2][:, :], srcb[:, c * 128:(c + 1) * 128],
                                idb[0:64, 0:64]).then_inc(tmq, 1)

            @blk2.vector
            def _(v):
                for k in range(64):
                    c = k // 2
                    dstb = ktm if k % 2 == 0 else vtm
                    v.wait_ge(tmq, k + 1)
                    v.tensor_copy(dstb[:, c * 64:(c + 1) * 64], PT[k % 2][:, :]).then_inc(tve, 1)

        with (
            E.psum("c_pg", [128, 128], F32) as pg,
            E.psum("c_pgk", [128, 128], F32) as pgk,
            E.psum("c_pO", [64, 128], F32) as pO,
            E.psum("c_pd", [1, 128], F32) as pd,
            E.psum("c_pS", [64, 64], F32) as pS,
            E.psum("c_pbc", [64, 128], F32) as pbc,
            nc.Block() as blk3,
        ):
            @blk3.vector
            def _(v):
                v.memset(STf[:, :], 0.0).then_inc(mz, 1)
                v.memset(STb[:, :], 0.0).then_inc(mz, 1)
                v.memset(zf[:, :], 0.0).then_inc(mz, 1)
                v.memset(zb[:, :], 0.0).then_inc(mz, 1)
                v.wait_ge(mz, 4)
                for c in range(32):
                    v.wait_ge(ce, 5 * c + 1)
                    v.tensor_tensor(gm[:, :], gs[:, :], trib[:, :], ALU.mult).then_inc(cv, 1)
                    v.wait_ge(ce, 5 * c + 2)
                    v.tensor_tensor(gkm[:, :], gks[:, :], trib[:, :], ALU.mult).then_inc(cv, 1)
                    v.wait_ge(cm, 8 * c + 7)
                    v.tensor_tensor(STf[:, :], STf[:, :], pS[:, :], ALU.add).then_inc(cv, 1)
                    v.wait_ge(cv, 10 * c + 3)
                    v.tensor_copy(STb[:, :], STf[:, :]).then_inc(cv, 1)
                    v.reduce_sum(zi[:, :], kb[:, c * 128:(c + 1) * 128], axis=AX.X).then_inc(cv, 1)
                    v.wait_ge(cv, 10 * c + 5)
                    v.tensor_tensor(zf[:, :], zf[:, :], zi[:, :], ALU.add).then_inc(cv, 1)
                    v.wait_ge(cv, 10 * c + 6)
                    v.wait_ge(cm, 8 * c + 5)
                    v.tensor_copy(zb[:, :], zf[:, :]).then_inc(cv, 1)
                    v.wait_ge(cm, 8 * c + 6)
                    v.tensor_scalar(dr[:, :], pd[:, :], 1e-6, None, ALU.max).then_inc(cv, 1)
                    v.wait_ge(ce, 5 * c + 4)
                    v.tensor_copy(drb[:, :], dl[:, :]).then_inc(cv, 1)
                    v.wait_ge(ce, 5 * c + 5)
                    v.wait_ge(cm, 8 * c + 8)
                    v.tensor_tensor(obuf[:, c * 128:(c + 1) * 128], ob[:, :], pbc[:, :],
                                    ALU.mult).then_inc(cv, 1)

            @blk3.scalar
            def _(s_):
                for c in range(32):
                    s_.wait_ge(cm, 8 * c + 1)
                    if c >= 1:
                        s_.wait_ge(cv, 10 * (c - 1) + 1)
                    s_.copy(gs[:, :], pg[:, :]).then_inc(ce, 1)
                    s_.wait_ge(cm, 8 * c + 2)
                    if c >= 1:
                        s_.wait_ge(cv, 10 * (c - 1) + 2)
                    s_.copy(gks[:, :], pgk[:, :]).then_inc(ce, 1)
                    s_.wait_ge(cv, 10 * c + 8)
                    s_.activation(dl[:, :], dr[:, :], AF.Ln).then_inc(ce, 1)
                    s_.wait_ge(ce, 5 * c + 3)
                    s_.activation(dl[:, :], dl[:, :], AF.Exp, scale=-1.0).then_inc(ce, 1)
                    s_.wait_ge(cm, 8 * c + 4)
                    if c >= 1:
                        s_.wait_ge(cv, 10 * (c - 1) + 10)
                    s_.copy(ob[:, :], pO[:, :]).then_inc(ce, 1)

            @blk3.tensor
            def _(t):
                t.wait_ge(mz, 4)
                for c in range(32):
                    if c >= 1:
                        t.wait_ge(ce, 5 * (c - 1) + 1)
                    t.matmul(pg[:, :], vb[:, c * 128:(c + 1) * 128], qb[:, c * 128:(c + 1) * 128],
                             start=True, stop=True).then_inc(cm, 1)
                    if c >= 1:
                        t.wait_ge(ce, 5 * (c - 1) + 2)
                    t.matmul(pgk[:, :], kb[:, c * 128:(c + 1) * 128], qb[:, c * 128:(c + 1) * 128],
                             start=True, stop=True).then_inc(cm, 1)
                    if c >= 1:
                        t.wait_ge(cv, 10 * (c - 1) + 4)
                        t.wait_ge(ce, 5 * (c - 1) + 5)
                    t.matmul(pO[:, :], STb[:, :], qb[:, c * 128:(c + 1) * 128],
                             start=True, stop=False).then_inc(cm, 1)
                    t.wait_ge(cv, 10 * c + 1)
                    t.matmul(pO[:, :], ktm[:, c * 64:(c + 1) * 64], gm[:, :],
                             start=False, stop=True).then_inc(cm, 1)
                    if c >= 1:
                        t.wait_ge(cv, 10 * (c - 1) + 7)
                        t.wait_ge(cv, 10 * (c - 1) + 8)
                    t.matmul(pd[:, :], zb[:, :], qb[:, c * 128:(c + 1) * 128],
                             start=True, stop=False).then_inc(cm, 1)
                    t.wait_ge(cv, 10 * c + 2)
                    t.matmul(pd[:, :], onb[:, 0:1], gkm[:, :],
                             start=False, stop=True).then_inc(cm, 1)
                    if c >= 1:
                        t.wait_ge(cv, 10 * (c - 1) + 3)
                    t.matmul(pS[:, :], vtm[:, c * 64:(c + 1) * 64], ktm[:, c * 64:(c + 1) * 64],
                             start=True, stop=True).then_inc(cm, 1)
                    t.wait_ge(cv, 10 * c + 9)
                    t.matmul(pbc[:, :], onb[0:1, 0:64], drb[:, :], start=True,
                             stop=True).then_inc(cm, 1)

        with (
            E.psum("c_pw0", [128, 512], F32) as pw0,
            E.psum("c_pw1", [128, 512], F32) as pw1,
            nc.Block() as blk4,
        ):
            PW = [pw0, pw1]
            wv = E.sem("cwv")

            @blk4.sync
            def _(sy):
                for s in range(nst):
                    if s >= 2:
                        sy.wait_ge(wv, 8 * (s - 1))
                    sy.dma_start(xs[:, (s % 2) * 4096:(s % 2 + 1) * 4096].rearrange("p (c t) -> p c t", t=N),
                                 srcr[:, :, s * N:(s + 1) * N]).then_inc(lxs[s % 2], 16)
                    if s >= 1:
                        sy.wait_ge(wv, 8 * s)
                        sy.dma_start(dstr[:, :, (s - 1) * N:s * N],
                                     od[:, ((s - 1) % 2) * 4096:((s - 1) % 2 + 1) * 4096].rearrange(
                                         "p (c t) -> p c t", t=N)).then_inc(stp[(s - 1) % 2], 16)
                sy.wait_ge(wv, 8 * nst)
                sy.dma_start(dstr[:, :, (nst - 1) * N:nst * N],
                             od[:, ((nst - 1) % 2) * 4096:((nst - 1) % 2 + 1) * 4096].rearrange(
                                 "p (c t) -> p c t", t=N)).then_inc(stp[(nst - 1) % 2], 16)
                sy.wait_ge(stp[0], 16 * (nst // 2))
                sy.wait_ge(stp[1], 16 * (nst // 2))

            @blk4.tensor
            def _(t):
                t.wait_ge(lw[2], 16)
                for s in range(nst):
                    for m in range(8):
                        k = 8 * s + m
                        if k >= 2:
                            t.wait_ge(wv, k - 1)
                        t.matmul(PW[k % 2][:, :], AoB[:, m * 128:(m + 1) * 128],
                                 obuf[:, s * N:(s + 1) * N], start=True, stop=True).then_inc(wm, 1)

            @blk4.vector
            def _(v):
                for s in range(nst):
                    ba = (s % 2) * 4096
                    v.wait_ge(lxs[s % 2], 16 * (s // 2 + 1))
                    if s >= 2:
                        v.wait_ge(stp[s % 2], 16 * (s // 2))
                    for m in range(8):
                        k = 8 * s + m
                        v.wait_ge(wm, k + 1)
                        v.tensor_tensor(od[:, ba + m * N:ba + (m + 1) * N], PW[k % 2][:, :],
                                        xs[:, ba + m * N:ba + (m + 1) * N], ALU.add).then_inc(wv, 1)
    return

def emit_D(E, src="s1", dst="u0"):
    nc = E.nc
    srcr = E.d[src].rearrange("(c p) t -> p c t", p=128)
    u0 = E.d[dst]
    from contextlib import ExitStack
    with ExitStack() as stk:
        gates = stk.enter_context(E.sbuf("d_gates", [32, 4096], BF16))
        IndB = stk.enter_context(E.sbuf("d_ind", [32, 4096], BF16))
        lxs = _dsem(E, "dx")
        lcw = _dsem(E, "dc", 2)
        gm1, ge1, gv1 = E.sem("dgm"), E.sem("dge"), E.sem("dgv")
        with ExitStack() as stk1:
            xs = stk1.enter_context(E.sbuf("d_xs", [128, 2 * 4096], F32))
            SelB = stk1.enter_context(E.sbuf("d_sel", [128, 8 * 32], F32))
            gbuf = stk1.enter_context(E.sbuf("d_g", [128, 2 * 32], F32))
            lbuf = stk1.enter_context(E.sbuf("d_l", [128, 2 * 32], F32))
            gw = stk1.enter_context(E.sbuf("d_gw", [128, 2 * 32], F32))
            mcol = stk1.enter_context(E.sbuf("d_m", [128, 2], F32))
            keep = stk1.enter_context(E.sbuf("d_keep", [128, 2 * 32], F32))
            gfb = stk1.enter_context(E.sbuf("d_gfb", [128, 2 * 32], BF16))
            pl0 = stk1.enter_context(E.psum("d_pl0", [128, 32], F32))
            pl1 = stk1.enter_context(E.psum("d_pl1", [128, 32], F32))
            idb = stk1.enter_context(E.sbuf("d_idb", [128, 128], BF16))
            pt0 = stk1.enter_context(E.psum("d_pt0", [32, 128], BF16))
            pt1 = stk1.enter_context(E.psum("d_pt1", [32, 128], BF16))
            with nc.Block() as blk:
                PL = [pl0, pl1]
                PT = [pt0, pt1]

                @blk.sync
                def _(sy):
                    sy.dma_start(SelB[:, :].rearrange("p (c n) -> p c n", n=32),
                                 E.d["SelT"].rearrange("(c p) n -> p c n", p=128)).then_inc(lcw[0], 16)
                    sy.dma_start(IndB[:, :], E.d["IndE"][:, :]).then_inc(lcw[1], 16)
                    sy.dma_start(idb[:, :], E.d["identb"][:, :]).then_inc(lcw[1], 16)
                    for s in range(NST):
                        if s >= 2:
                            sy.wait_ge(gm1, 8 * (s - 1))
                        sy.dma_start(xs[:, (s % 2) * 4096:(s % 2 + 1) * 4096].rearrange("p (c t) -> p c t", t=512),
                                     srcr[:, :, s * 512:(s + 1) * 512]).then_inc(lxs[s % 2], 16)

                @blk.tensor
                def _(t):
                    t.wait_ge(lcw[0], 16)
                    t.wait_ge(lcw[1], 32)
                    for s in range(NST):
                        for tt in range(4):
                            K = s * 4 + tt
                            t.wait_ge(lxs[s % 2], 16 * (s // 2 + 1))
                            if K >= 2:
                                t.wait_ge(ge1, 3 * (K - 2) + 1 if K >= 3 else 2)
                            for c in range(8):
                                mm = t.matmul(PL[K % 2][:, :], xs[:, (s % 2) * 4096 + c * 512 + tt * 128:
                                                                  (s % 2) * 4096 + c * 512 + (tt + 1) * 128],
                                              SelB[:, c * 32:(c + 1) * 32],
                                              start=(c == 0), stop=(c == 7))
                            mm.then_inc(gm1, 1)
                            t.wait_ge(gv1, 14 * K)
                            if K >= 1:
                                KK = K - 1
                                if KK >= 2:
                                    t.wait_ge(ge1, 3 * KK - 1)
                                t.transpose(PT[KK % 2][:, :], gfb[:, (KK % 2) * 32:(KK % 2 + 1) * 32],
                                            idb[:, :]).then_inc(gm1, 1)
                    t.wait_ge(gv1, 14 * 32)
                    KK = 31
                    t.wait_ge(ge1, 3 * KK - 1)
                    t.transpose(PT[KK % 2][:, :], gfb[:, (KK % 2) * 32:(KK % 2 + 1) * 32],
                                idb[:, :]).then_inc(gm1, 1)

                @blk.scalar
                def _(s_):
                    for K in range(32):
                        s_.wait_ge(gm1, 2 * K + 1)
                        if K >= 2:
                            s_.wait_ge(gv1, 14 * (K - 2) + 14)
                        s_.activation(gbuf[:, (K % 2) * 32:(K % 2 + 1) * 32], PL[K % 2][:, :],
                                      AF.Sigmoid).then_inc(ge1, 1)
                        s_.copy(lbuf[:, (K % 2) * 32:(K % 2 + 1) * 32], PL[K % 2][:, :]).then_inc(ge1, 1)
                        if K >= 1:
                            KK = K - 1
                            s_.copy(gates[:, KK * 128:(KK + 1) * 128], PT[KK % 2][:, :]).then_inc(ge1, 1)
                    s_.wait_ge(gm1, 64)
                    s_.copy(gates[:, 31 * 128:32 * 128], PT[31 % 2][:, :]).then_inc(ge1, 1)

                @blk.vector
                def _(v):
                    for K in range(32):
                        b = (K % 2) * 32
                        myv = 14 * K
                        v.wait_ge(ge1, 3 * K + 2)
                        v.tensor_copy(gw[:, b:b + 32], lbuf[:, b:b + 32]).then_inc(gv1, 1)
                        myv += 1
                        for it in range(4):
                            v.wait_ge(gv1, myv)
                            v.reduce_max(mcol[:, K % 2:K % 2 + 1], gw[:, b:b + 32],
                                         axis=AX.X).then_inc(gv1, 1)
                            myv += 1
                            if it < 3:
                                v.wait_ge(gv1, myv)
                                v.tensor_scalar(keep[:, b:b + 32], gw[:, b:b + 32],
                                                mcol[:, K % 2:K % 2 + 1], None, ALU.is_ge).then_inc(gv1, 1)
                                myv += 1
                                v.wait_ge(gv1, myv)
                                v.scalar_tensor_tensor(gw[:, b:b + 32], keep[:, b:b + 32], -1e30,
                                                       gw[:, b:b + 32], ALU.mult, ALU.add).then_inc(gv1, 1)
                                myv += 1
                        v.wait_ge(gv1, myv)
                        v.tensor_scalar(keep[:, b:b + 32], lbuf[:, b:b + 32],
                                        mcol[:, K % 2:K % 2 + 1], None, ALU.is_ge).then_inc(gv1, 1)
                        myv += 1
                        v.wait_ge(gv1, myv)
                        v.tensor_tensor(keep[:, b:b + 32], gbuf[:, b:b + 32], keep[:, b:b + 32],
                                        ALU.mult).then_inc(gv1, 1)
                        myv += 1
                        v.wait_ge(gv1, myv)
                        v.tensor_copy(gfb[:, b:b + 32], keep[:, b:b + 32]).then_inc(gv1, 1)
                        myv += 1
                        assert myv == 14 * (K + 1), (myv, K)

        NG, GE, NSP, SWD = 4, 8, 16, 256
        wk = _dsem(E, "dwk", 8, kind="sw")
        wv = _dsem(E, "dwv", 8, kind="sw")
        lxb = _dsem(E, "dxb", kind="sw")
        du = _dsem(E, "du", NSP, kind="sw")
        km, bm, vm = E.sem("dkm"), E.sem("dbm"), E.sem("dvm")
        sev, vv, vo = E.sem("dse"), E.sem("dvv"), E.sem("dvo")
        with ExitStack() as stk2:
            KeB = stk2.enter_context(E.sbuf("d_ke", [128, 8 * 4096], BF16))
            VeB = stk2.enter_context(E.sbuf("d_ve", [128, 8 * 4096], BF16))
            xb = stk2.enter_context(E.sbuf("d_xb", [128, 2 * 2048], BF16))
            scb = stk2.enter_context(E.sbuf("d_scb", [128, 2 * 1024], BF16))
            scg = stk2.enter_context(E.sbuf("d_scg", [128, 2 * 1024], BF16))
            odb = stk2.enter_context(E.sbuf("d_od", [128, 2 * 2048], F32))
            po = [stk2.enter_context(E.psum(f"d_po{j}", [128, 512], F32)) for j in range(4)]
            psc = [stk2.enter_context(E.psum(f"d_psc{j}", [128, 256], F32)) for j in range(3)]
            pgb = stk2.enter_context(E.psum("d_pgb", [128, 256], F32))
            with nc.Block() as blk2:
                Keys3 = E.d["Keys"]
                Vals3 = E.d["Values"]

                @blk2.gpsimd
                def _(g):
                    for gi in range(NG):
                        for ee in range(GE):
                            e = gi * GE + ee
                            if gi >= 1:
                                g.wait_ge(km, 32 * 16 * gi)
                                g.wait_ge(vm, 8 * 16 * gi)
                            g.dma_start(KeB[:, ee * 4096:(ee + 1) * 4096].rearrange("p (c f) -> p c f", f=512),
                                        Keys3[e].rearrange("(c p) f -> p c f", p=128)).then_inc(wk[ee], 16)
                            g.dma_start(VeB[:, ee * 4096:(ee + 1) * 4096].rearrange("p (f v) -> p f v", v=1024),
                                        Vals3[e].rearrange("(f p) v -> p f v", p=128)).then_inc(wv[ee], 16)
                        for s in range(NSP):
                            P = gi * NSP + s
                            if P >= 2:
                                g.wait_ge(km, 32 * (P - 1))
                            g.dma_start(xb[:, (P % 2) * 2048:(P % 2 + 1) * 2048].rearrange("p (c t) -> p c t", t=SWD),
                                        srcr[:, :, s * SWD:(s + 1) * SWD]).then_inc(lxb[P % 2], 16)
                            if P >= 1:
                                PP = P - 1
                                gp, sp = divmod(PP, NSP)
                                g.wait_ge(vo, 4 * (PP + 1))
                                if gp >= 1:
                                    g.wait_ge(du[sp], 64 * gp)
                                for j in range(4):
                                    tsub, vh = divmod(j, 2)
                                    d = u0[sp * SWD + tsub * 128: sp * SWD + (tsub + 1) * 128,
                                           vh * 512:(vh + 1) * 512]
                                    srcap = odb[:, (PP % 2) * 2048 + j * 512:(PP % 2) * 2048 + (j + 1) * 512]
                                    if gp == 0:
                                        g.dma_start(d, srcap).then_inc(du[sp], 16)
                                    else:
                                        g.dma_start(d, srcap, accum_op=ALU.add).then_inc(du[sp], 16)
                    PP = NG * NSP - 1
                    gp, sp = divmod(PP, NSP)
                    g.wait_ge(vo, 4 * (PP + 1))
                    g.wait_ge(du[sp], 64 * gp)
                    for j in range(4):
                        tsub, vh = divmod(j, 2)
                        d = u0[sp * SWD + tsub * 128: sp * SWD + (tsub + 1) * 128,
                               vh * 512:(vh + 1) * 512]
                        g.dma_start(d, odb[:, (PP % 2) * 2048 + j * 512:(PP % 2) * 2048 + (j + 1) * 512],
                                    accum_op=ALU.add).then_inc(du[sp], 16)
                    for sp in range(NSP):
                        g.wait_ge(du[sp], 64 * NG)

                @blk2.tensor
                def _(t):
                    for gi in range(NG):
                        for s in range(NSP):
                            P = gi * NSP + s
                            t.wait_ge(lxb[P % 2], 16 * (P // 2 + 1))
                            for step in range(GE + 1):
                                if step < GE:
                                    ee = step
                                    EIX = P * 8 + ee
                                    t.wait_ge(wk[ee], 16 * (gi + 1))
                                    for fc in range(4):
                                        KIX = EIX * 4 + fc
                                        if KIX >= 3:
                                            t.wait_ge(sev, KIX - 2)
                                        for c in range(8):
                                            mm = t.matmul(psc[KIX % 3][:, :],
                                                          KeB[:, ee * 4096 + c * 512 + fc * 128:
                                                              ee * 4096 + c * 512 + (fc + 1) * 128],
                                                          xb[:, (P % 2) * 2048 + c * SWD:(P % 2) * 2048 + (c + 1) * SWD],
                                                          start=(c == 0), stop=(c == 7))
                                        mm.then_inc(km, 1)
                                    if EIX >= 1:
                                        t.wait_ge(vv, 4 * EIX)
                                    t.matmul(pgb[:, :], IndB[:, (gi * GE + ee) * 128:(gi * GE + ee + 1) * 128],
                                             gates[:, s * SWD:(s + 1) * SWD], start=True,
                                             stop=True).then_inc(bm, 1)
                                if step >= 1:
                                    ee = step - 1
                                    EIX = P * 8 + ee
                                    t.wait_ge(wv[ee], 16 * (gi + 1))
                                    t.wait_ge(vv, 4 * (EIX + 1))
                                    if ee == 0:
                                        t.wait_ge(vo, 4 * P)
                                    for fc in range(4):
                                        for j in range(4):
                                            tsub, vh = divmod(j, 2)
                                            mm = t.matmul(po[j][:, :],
                                                     scg[:, (EIX % 2) * 1024 + fc * 256 + tsub * 128:
                                                         (EIX % 2) * 1024 + fc * 256 + (tsub + 1) * 128],
                                                     VeB[:, ee * 4096 + fc * 1024 + vh * 512:
                                                         ee * 4096 + fc * 1024 + (vh + 1) * 512],
                                                     start=(ee == 0 and fc == 0), stop=(ee == 7 and fc == 3))
                                    mm.then_inc(vm, 1)

                @blk2.scalar
                def _(s_):
                    for P in range(NG * NSP):
                        for ee in range(GE):
                            EIX = P * 8 + ee
                            for fc in range(4):
                                KIX = EIX * 4 + fc
                                s_.wait_ge(km, KIX + 1)
                                if EIX >= 2:
                                    s_.wait_ge(vv, 4 * (EIX - 1))
                                s_.activation(scb[:, (EIX % 2) * 1024 + fc * 256:(EIX % 2) * 1024 + (fc + 1) * 256],
                                              psc[KIX % 3][:, :], AF.Relu).then_inc(sev, 1)

                @blk2.vector
                def _(v):
                    for P in range(NG * NSP):
                        for ee in range(GE):
                            EIX = P * 8 + ee
                            v.wait_ge(bm, EIX + 1)
                            for fc in range(4):
                                KIX = EIX * 4 + fc
                                v.wait_ge(sev, KIX + 1)
                                if EIX >= 2:
                                    v.wait_ge(vm, EIX - 1)
                                v.tensor_tensor(scg[:, (EIX % 2) * 1024 + fc * 256:(EIX % 2) * 1024 + (fc + 1) * 256],
                                                scb[:, (EIX % 2) * 1024 + fc * 256:(EIX % 2) * 1024 + (fc + 1) * 256],
                                                pgb[:, :], ALU.mult).then_inc(vv, 1)
                        v.wait_ge(vm, 8 * (P + 1))
                        if P >= 2:
                            PP = P - 2
                            gp, sp = divmod(PP, NSP)
                            v.wait_ge(du[sp], 64 * gp + 64)
                        for j in range(4):
                            v.tensor_copy(odb[:, (P % 2) * 2048 + j * 512:(P % 2) * 2048 + (j + 1) * 512],
                                          po[j][:, :]).then_inc(vo, 1)
    return


def emit_final(E, usrc="u0", xsrc="s1", dst="out"):
    nc = E.nc
    u0 = E.d[usrc]
    xr = E.d[xsrc].rearrange("(c p) t -> p c t", p=128)
    out = E.d[dst]
    lc = E.sem("flc")
    lu, lx, stp = _dsem(E, "flu"), _dsem(E, "flx"), _dsem(E, "fst")
    fm, fe, fv = E.sem("ffm"), E.sem("ffe"), E.sem("ffv")
    V_PER, S_PER = 15, 4
    from contextlib import ExitStack
    with ExitStack() as stk:
        ub = stk.enter_context(E.sbuf("f_u", [128, 2 * 1024], F32))
        xc = stk.enter_context(E.sbuf("f_x", [128, 2 * 1024], F32))
        sm = stk.enter_context(E.sbuf("f_s", [128, 2 * 1024], F32))
        jk = stk.enter_context(E.sbuf("f_j", [128, 2048], F32))
        idf = stk.enter_context(E.sbuf("f_id", [128, 128], F32))
        cols = stk.enter_context(E.sbuf("f_c", [128, 8], F32))
        ob = stk.enter_context(E.sbuf("f_o", [128, 2 * 1024], F32))
        pt0 = stk.enter_context(E.psum("f_pt0", [128, 128], F32))
        pt1 = stk.enter_context(E.psum("f_pt1", [128, 128], F32))
        with nc.Block() as blk:
            PT = [pt0, pt1]

            @blk.sync
            def _(sy):
                sy.dma_start(idf[:, :], E.d["identf"][:, :]).then_inc(lc, 16)
                for tt in range(32):
                    if tt >= 2:
                        sy.wait_ge(fv, V_PER * (tt - 1))
                    sy.dma_start(ub[:, (tt % 2) * 1024:(tt % 2 + 1) * 1024],
                                 u0[tt * 128:(tt + 1) * 128, :]).then_inc(lu[tt % 2], 16)
                    sy.dma_start(xc[:, (tt % 2) * 1024:(tt % 2 + 1) * 1024].rearrange("p (c t) -> p c t", t=128),
                                 xr[:, :, tt * 128:(tt + 1) * 128]).then_inc(lx[tt % 2], 16)
                    if tt >= 1:
                        sy.wait_ge(fv, V_PER * tt)
                        sy.dma_start(out[(tt - 1) * 128:tt * 128, :],
                                     ob[:, ((tt - 1) % 2) * 1024:((tt - 1) % 2 + 1) * 1024]
                                     ).then_inc(stp[(tt - 1) % 2], 16)
                sy.wait_ge(fv, V_PER * 32)
                sy.dma_start(out[31 * 128:32 * 128, :],
                             ob[:, (31 % 2) * 1024:(31 % 2 + 1) * 1024]).then_inc(stp[31 % 2], 16)
                sy.wait_ge(stp[0], 16 * 16)
                sy.wait_ge(stp[1], 16 * 16)

            @blk.tensor
            def _(t):
                t.wait_ge(lc, 16)
                for tt in range(32):
                    t.wait_ge(lx[tt % 2], 16 * (tt // 2 + 1))
                    for c in range(8):
                        k = tt * 8 + c
                        if k >= 2:
                            kk = k - 2
                            t.wait_ge(fv, V_PER * (kk // 8) + (kk % 8) + 1)
                        t.transpose(PT[k % 2][:, :],
                                    xc[:, (tt % 2) * 1024 + c * 128:(tt % 2) * 1024 + (c + 1) * 128],
                                    idf[:, :]).then_inc(fm, 1)

            @blk.vector
            def _(v):
                for tt in range(32):
                    base = (tt % 2) * 1024
                    myv = V_PER * tt
                    v.wait_ge(lu[tt % 2], 16 * (tt // 2 + 1))
                    for c in range(8):
                        k = tt * 8 + c
                        v.wait_ge(fm, k + 1)
                        v.tensor_tensor(sm[:, base + c * 128:base + (c + 1) * 128],
                                        ub[:, base + c * 128:base + (c + 1) * 128],
                                        PT[k % 2][:, :], ALU.add).then_inc(fv, 1)
                        myv += 1
                    v.wait_ge(fe, S_PER * tt + 2)
                    v.tensor_scalar(cols[:, 0:1], cols[:, 0:1], 1.0 / 1024.0, None, ALU.mult).then_inc(fv, 1)
                    myv += 1
                    v.wait_ge(fv, myv)
                    v.tensor_scalar(cols[:, 1:2], cols[:, 1:2], 1.0 / 1024.0, None, ALU.mult).then_inc(fv, 1)
                    myv += 1
                    v.wait_ge(fv, myv)
                    v.scalar_tensor_tensor(cols[:, 2:3], cols[:, 0:1], -1.0, cols[:, 0:1],
                                           ALU.mult, ALU.mult).then_inc(fv, 1)
                    myv += 1
                    v.wait_ge(fv, myv)
                    v.tensor_tensor(cols[:, 3:4], cols[:, 1:2], cols[:, 2:3], ALU.add).then_inc(fv, 1)
                    myv += 1
                    v.wait_ge(fv, myv)
                    v.tensor_scalar(cols[:, 3:4], cols[:, 3:4], 1e-5, None, ALU.add).then_inc(fv, 1)
                    myv += 1
                    v.wait_ge(fe, S_PER * tt + 4)
                    v.scalar_tensor_tensor(cols[:, 5:6], cols[:, 0:1], -1.0, cols[:, 4:5],
                                           ALU.mult, ALU.mult).then_inc(fv, 1)
                    myv += 1
                    if tt >= 2:
                        v.wait_ge(stp[tt % 2], 16 * (tt // 2))
                    v.wait_ge(fv, myv)
                    v.tensor_scalar(ob[:, base:base + 1024], sm[:, base:base + 1024],
                                    cols[:, 4:5], cols[:, 5:6], ALU.mult, ALU.add).then_inc(fv, 1)
                    myv += 1
                    assert myv == V_PER * (tt + 1), (myv, tt)

            @blk.scalar
            def _(s_):
                for tt in range(32):
                    base = (tt % 2) * 1024
                    s_.wait_ge(fv, V_PER * tt + 8)
                    if tt >= 1:
                        s_.wait_ge(fe, S_PER * (tt - 1) + 2)
                    s_.activation(jk[:, 0:1024], sm[:, base:base + 1024], AF.Copy,
                                  accum_out=cols[:, 0:1]).then_inc(fe, 1)
                    s_.wait_ge(fe, S_PER * tt + 1)
                    s_.activation(jk[:, 1024:2048], sm[:, base:base + 1024], AF.Square,
                                  accum_out=cols[:, 1:2]).then_inc(fe, 1)
                    s_.wait_ge(fv, V_PER * tt + 13)
                    s_.activation(cols[:, 4:5], cols[:, 3:4], AF.Ln).then_inc(fe, 1)
                    s_.wait_ge(fe, S_PER * tt + 3)
                    s_.activation(cols[:, 4:5], cols[:, 4:5], AF.Exp, scale=-0.5).then_inc(fe, 1)
    return

def emit_A(E, src="s0", dst="t0"):
    nc = E.nc
    N = SW
    srcr = E.d[src].rearrange("(c p) t -> p c t", p=128)
    dstr = E.d[dst].rearrange("(c p) t -> p c t", p=128)
    qsr, ksr, vsr = E.d["qs"], E.d["ks"], E.d["vs"]
    ssqT = E.d["ssqT"]
    from contextlib import ExitStack
    with ExitStack() as stk:
        O_all = stk.enter_context(E.sbuf("a_oall", [128, 8 * 4096], BF16))
        WoB = stk.enter_context(E.sbuf("a_wo", [128, 8 * 1024], BF16))
        onb = stk.enter_context(E.sbuf("a_onb", [128, 128], BF16))
        lwo = _dsem(E, "awo", 1, kind="sw") + _dsem(E, "awoh", 2)

        lww = _dsem(E, "aww", 3, kind="sw") + _dsem(E, "awwh", 2)
        lxb = _dsem(E, "axb", kind="sw")
        asp = _dsem(E, "asp")
        am, ae, av = E.sem("aam"), E.sem("aae"), E.sem("aav")
        mz1 = E.sem("amz1")
        with ExitStack() as stk1:
            WqB = stk1.enter_context(E.sbuf("a_wq", [128, 8 * 256], BF16))
            WkB = stk1.enter_context(E.sbuf("a_wk", [128, 8 * 256], BF16))
            WvB = stk1.enter_context(E.sbuf("a_wv", [128, 8 * 1024], BF16))
            blkB = stk1.enter_context(E.sbuf("a_blk", [128, 256], BF16))
            blkT = stk1.enter_context(E.sbuf("a_blkt", [128, 256], BF16))
            xb = stk1.enter_context(E.sbuf("a_xb", [128, 2 * 4096], BF16))
            sqb = stk1.enter_context(E.sbuf("a_sqb", [128, 2 * 512], BF16))
            us = stk1.enter_context(E.sbuf("a_us", [8, 2 * 512], F32))
            Ls = stk1.enter_context(E.sbuf("a_ls", [8, 2 * 512], F32))
            rns = stk1.enter_context(E.sbuf("a_rns", [8, 2 * 512], F32))
            rnb = stk1.enter_context(E.sbuf("a_rnb", [128, 2 * 512], BF16))
            Tb = stk1.enter_context(E.sbuf("a_tb", [128, 2 * 512], BF16))
            qnb = stk1.enter_context(E.sbuf("a_qnb", [128, 2 * 512], BF16))
            pm = [stk1.enter_context(E.psum(f"a_pm{i}", [128, 512], F32)) for i in range(3)]
            pss = [stk1.enter_context(E.psum(f"a_pss{i}", [128, 512], F32)) for i in range(2)]
            pbc = [stk1.enter_context(E.psum(f"a_pbc{i}", [128, 512], F32)) for i in range(2)]
            with nc.Block() as blk:
                def wsl(j):
                    def f(c):
                        if j < 2:
                            return WqB[:, c * 256 + j * 128:c * 256 + (j + 1) * 128]
                        if j < 4:
                            return WkB[:, c * 256 + (j - 2) * 128:c * 256 + (j - 1) * 128]
                        return WvB[:, c * 1024 + (j - 4) * 128:c * 1024 + (j - 3) * 128]
                    return f

                def blk_lhs(j):
                    return blkB[:, 0:128] if j < 4 else blkB[:, 128:256]

                def blkT_lhs(j):
                    return blkT[:, 0:128] if j < 4 else blkT[:, 128:256]

                def nh(j):
                    return 8 if j < 4 else 2

                def spill_dst(j, s):
                    if j < 2:
                        return qsr[j * 128:(j + 1) * 128, s * N:(s + 1) * N]
                    if j < 4:
                        return ksr[(j - 2) * 128:(j - 1) * 128, s * N:(s + 1) * N]
                    return vsr[(j - 4) * 128:(j - 3) * 128, s * N:(s + 1) * N]

                @blk.gpsimd
                def _(g):
                    g.dma_start(WqB[:, :].rearrange("p (c n) -> p c n", n=256),
                                E.d["Wq"].rearrange("(c p) n -> p c n", p=128)).then_inc(lww[0], 16)
                    g.dma_start(WkB[:, :].rearrange("p (c n) -> p c n", n=256),
                                E.d["Wk"].rearrange("(c p) n -> p c n", p=128)).then_inc(lww[1], 16)
                    g.dma_start(WvB[:, :].rearrange("p (c n) -> p c n", n=1024),
                                E.d["Wv"].rearrange("(c p) n -> p c n", p=128)).then_inc(lww[2], 16)
                    g.dma_start(WoB[:, :].rearrange("p (v n) -> p v n", n=1024),
                                E.d["Wout"].rearrange("(v p) n -> p v n", p=128)).then_inc(lwo[0], 16)
                    for s in range(NST):
                        if s >= 2:
                            g.wait_ge(am, 36 * (s - 1))
                        g.dma_start(xb[:, (s % 2) * 4096:(s % 2 + 1) * 4096].rearrange("p (c t) -> p c t", t=N),
                                    srcr[:, :, s * N:(s + 1) * N]).then_inc(lxb[s % 2], 16)

                @blk.sync
                def _(sy):
                    sy.dma_start(blkB[:, :], E.d["blkP"][:, :]).then_inc(lww[3], 16)
                    sy.dma_start(blkT[:, :], E.d["blkTP"][:, :]).then_inc(lww[4], 16)
                    sy.dma_start(onb[:, :], E.d["onesb"][:, :]).then_inc(lwo[1], 16)
                    I_QN_S = {}
                    _c = 0
                    for M in range(98):
                        if M < 96:
                            _c += 1
                        if 1 <= M <= 96:
                            _c += 1
                        if 2 <= M <= 97:
                            _c += 1
                            I_QN_S[M - 2] = _c
                    for s in range(NST):
                        for j in range(12):
                            M = s * 12 + j
                            sy.wait_ge(av, I_QN_S[M])
                            sy.dma_start(spill_dst(j, s),
                                         qnb[:, (M % 2) * 512:(M % 2 + 1) * 512]).then_inc(asp[M % 2], 16)
                    sy.wait_ge(asp[0], 16 * 48)
                    sy.wait_ge(asp[1], 16 * 48)

                I_SQ, I_LN, I_RN, I_TB = {}, {}, {}, {}
                cnt = 0
                for M in range(97):
                    if M < 96:
                        cnt += 1; I_SQ[M] = cnt
                    if M >= 1:
                        cnt += 1; I_LN[M - 1] = cnt
                        cnt += 1; I_RN[M - 1] = cnt
                        cnt += 1; I_TB[M - 1] = cnt
                I_MX, I_RB, I_QN = {}, {}, {}
                cnt = 0
                for M in range(98):
                    if M < 96:
                        cnt += 1; I_MX[M] = cnt
                    if 1 <= M <= 96:
                        cnt += 1; I_RB[M - 1] = cnt
                    if 2 <= M <= 97:
                        cnt += 1; I_QN[M - 2] = cnt
                I_PJ, I_SS, I_BC = {}, {}, {}
                cnt = 0
                for M in range(97):
                    if M < 96:
                        cnt += 1; I_PJ[M] = cnt
                        cnt += 1; I_SS[M] = cnt
                    if M >= 1:
                        cnt += 1; I_BC[M - 1] = cnt

                @blk.tensor
                def _(t):
                    t.wait_ge(lww[0], 16)
                    t.wait_ge(lww[1], 16)
                    t.wait_ge(lww[2], 16)
                    t.wait_ge(lww[3], 16)
                    t.wait_ge(lww[4], 16)
                    for M in range(97):
                        s, j = divmod(M, 12)
                        if M < 96:
                            W = wsl(j)
                            t.wait_ge(lxb[s % 2], 16 * (s // 2 + 1))
                            if M >= 3:
                                t.wait_ge(ae, I_TB[M - 3])
                            for c in range(8):
                                mm = t.matmul(pm[M % 3][:, :], W(c),
                                              xb[:, (s % 2) * 4096 + c * N:(s % 2) * 4096 + (c + 1) * N],
                                              start=(c == 0), stop=(c == 7))
                            mm.then_inc(am, 1)
                            t.wait_ge(ae, I_SQ[M])
                            if M >= 2:
                                t.wait_ge(av, I_MX[M - 2])
                            t.matmul(pss[M % 2][:, :], blk_lhs(j),
                                     sqb[:, (M % 2) * 512:(M % 2 + 1) * 512],
                                     start=True, stop=True).then_inc(am, 1)
                        if M >= 1:
                            MM = M - 1
                            jj = MM % 12
                            t.wait_ge(av, I_RB[MM])
                            if MM >= 2:
                                t.wait_ge(av, I_QN[MM - 2])
                            t.wait_ge(mz1, 1)
                            t.matmul(pbc[MM % 2][:, :], blkT_lhs(jj),
                                     rnb[:, (MM % 2) * 512:(MM % 2 + 1) * 512],
                                     start=True, stop=True).then_inc(am, 1)

                @blk.scalar
                def _(s_):
                    for M in range(97):
                        if M < 96:
                            j = M % 12
                            s_.wait_ge(am, I_PJ[M])
                            if M >= 2:
                                s_.wait_ge(am, I_SS[M - 2])
                            s_.activation(sqb[:, (M % 2) * 512:(M % 2 + 1) * 512], pm[M % 3][:, :],
                                          AF.Square).then_inc(ae, 1)
                        if M >= 1:
                            MM = M - 1
                            jj = MM % 12
                            s_.wait_ge(av, I_MX[MM])
                            s_.activation(Ls[0:nh(jj), (MM % 2) * 512:(MM % 2 + 1) * 512],
                                          us[0:nh(jj), (MM % 2) * 512:(MM % 2 + 1) * 512],
                                          AF.Ln).then_inc(ae, 1)
                            s_.wait_ge(ae, I_LN[MM])
                            if MM >= 2:
                                s_.wait_ge(av, I_RB[MM - 2])
                            s_.activation(rns[0:nh(jj), (MM % 2) * 512:(MM % 2 + 1) * 512],
                                          Ls[0:nh(jj), (MM % 2) * 512:(MM % 2 + 1) * 512],
                                          AF.Exp, scale=-0.5).then_inc(ae, 1)
                            if MM >= 2:
                                s_.wait_ge(av, I_QN[MM - 2])
                            s_.copy(Tb[:, (MM % 2) * 512:(MM % 2 + 1) * 512],
                                    pm[MM % 3][:, :]).then_inc(ae, 1)

                @blk.vector
                def _(v):
                    v.memset(rnb[:, :], 0.0).then_inc(mz1, 1)
                    for M in range(98):
                        if M < 96:
                            j = M % 12
                            v.wait_ge(am, I_SS[M])
                            if M >= 2:
                                v.wait_ge(ae, I_LN[M - 2])
                            v.tensor_scalar(us[0:nh(j), (M % 2) * 512:(M % 2 + 1) * 512],
                                            pss[M % 2][0:nh(j), :], 1e-18, None,
                                            ALU.max).then_inc(av, 1)
                        if 1 <= M <= 96:
                            MM = M - 1
                            jj = MM % 12
                            v.wait_ge(ae, I_RN[MM])
                            v.tensor_copy(rnb[0:nh(jj), (MM % 2) * 512:(MM % 2 + 1) * 512],
                                          rns[0:nh(jj), (MM % 2) * 512:(MM % 2 + 1) * 512]).then_inc(av, 1)
                        if 2 <= M <= 97:
                            MM = M - 2
                            jj = MM % 12
                            v.wait_ge(ae, I_TB[MM])
                            v.wait_ge(am, I_BC[MM])
                            if MM >= 2:
                                v.wait_ge(asp[MM % 2], 16 * (MM // 2))
                            v.tensor_tensor(qnb[:, (MM % 2) * 512:(MM % 2 + 1) * 512],
                                            Tb[:, (MM % 2) * 512:(MM % 2 + 1) * 512],
                                            pbc[MM % 2][:, :], ALU.mult).then_inc(av, 1)

        hq, hk, hvt, hs = _dsem(E, "ahq"), _dsem(E, "ahk"), _dsem(E, "ahv"), _dsem(E, "ahs")
        fsim, fom = E.sem("afs"), E.sem("afm")
        fres, frev = E.sem("afre"), E.sem("afrv")
        fvsq, fgsq, fqs = E.sem("afvq"), E.sem("afgq"), E.sem("afqs")
        foev, fvo, fss, fvq = E.sem("afoe"), E.sem("afvo"), E.sem("afss"), E.sem("afvz")
        RELU_ENG = ['s', 'v', 's', 'v', 's', 's', 's', 'v',
                    's', 's', 's', 'v', 's', 's', 's', 'v']
        SQ_ENG = ['v', 'g', 'v', 's', 'v', 'g', 'v', 'v',
                  'v', 'g', 'v', 's', 'v', 'g', 'v', 'v']

        def _prefix(pat):
            d = {}
            for e in set(pat):
                c = [0]
                for x in pat:
                    c.append(c[-1] + (1 if x == e else 0))
                d[e] = c
            return d

        RP, QP = _prefix(RELU_ENG), _prefix(SQ_ENG)
        R_SEM = {'s': fres, 'v': frev}
        Q_SEM = {'s': fqs, 'v': fvsq, 'g': fgsq}

        def r_count(h8, p):
            e = RELU_ENG[p]
            return R_SEM[e], RP[e][16] * h8 + RP[e][p + 1]

        def q_count(h8, p):
            e = SQ_ENG[p]
            return Q_SEM[e], QP[e][16] * h8 + QP[e][p + 1]

        with ExitStack() as stk2:
            Qh = stk2.enter_context(E.sbuf("a_qh", [128, 2 * 4096], BF16))
            Kh = stk2.enter_context(E.sbuf("a_kh", [128, 2 * 4096], BF16))
            Vtm = stk2.enter_context(E.sbuf("a_vtm", [128, 2 * 4096], BF16))
            bkv = stk2.enter_context(E.sbuf("a_bkv", [128, 2], BF16))
            rb = stk2.enter_context(E.sbuf("a_rb", [128, 8 * 1024], BF16))
            sqq = stk2.enter_context(E.sbuf("a_sqq", [128, 8 * 1024], BF16))
            osqb = stk2.enter_context(E.sbuf("a_osq", [128, 2 * 512], BF16))
            ssqs = stk2.enter_context(E.sbuf("a_ssqs", [1, 2 * 4096], F32))
            psim = [stk2.enter_context(E.psum(f"a_psim{i}", [128, 1024], F32)) for i in range(3)]
            pO = [stk2.enter_context(E.psum("a_pO0", [128, 512], F32))]
            pq = [stk2.enter_context(E.psum("a_pq0", [1, 512], F32))]
            msA = E.sem("amsA")
            with nc.Block() as blk2:
                @blk2.sync
                def _(sy):
                    sy.wait_ge(msA, 4)
                    sy.dma_start(bkv[:, :], E.d["blkv"][:, :]).then_inc(lwo[2], 16)
                    for h in range(16):
                        if h >= 2:
                            sy.wait_ge(fsim, 128 * (h - 1))
                        sy.dma_start(Qh[0:16, (h % 2) * 4096:(h % 2 + 1) * 4096],
                                     qsr[h * 16:(h + 1) * 16, :]).then_inc(hq[h % 2], 16)
                        sy.dma_start(Kh[0:16, (h % 2) * 4096:(h % 2 + 1) * 4096],
                                     ksr[h * 16:(h + 1) * 16, :]).then_inc(hk[h % 2], 16)
                        if h >= 2:
                            sy.wait_ge(fom, 256 * (h - 1))
                        for jt in range(32):
                            sy.dma_start(Vtm[:, (h % 2) * 4096 + jt * 128:(h % 2) * 4096 + jt * 128 + 64],
                                         vsr[h * 64:(h + 1) * 64, jt * 128:(jt + 1) * 128],
                                         transpose=True).then_inc(hvt[h % 2], 16)
                        if h >= 1:
                            sy.wait_ge(fvq, 8 * h)
                            sy.dma_start(ssqT[h - 1:h, :],
                                         ssqs[:, ((h - 1) % 2) * 4096:((h - 1) % 2 + 1) * 4096]
                                         ).then_inc(hs[(h - 1) % 2], 16)
                    sy.wait_ge(fvq, 8 * 16)
                    sy.dma_start(ssqT[15:16, :], ssqs[:, (15 % 2) * 4096:(15 % 2 + 1) * 4096]
                                 ).then_inc(hs[15 % 2], 16)
                    sy.wait_ge(hs[0], 16 * 8)
                    sy.wait_ge(hs[1], 16 * 8)

                @blk2.tensor
                def _(t):
                    t.wait_ge(lwo[1], 16)
                    t.wait_ge(lwo[2], 16)
                    t.wait_ge(msA, 4)
                    pend_ssq = []

                    def emit_ssq(t):
                        h8p, hp, stp_ = pend_ssq.pop(0)
                        t.wait_ge(fvo, h8p + 1)
                        if h8p >= 1:
                            t.wait_ge(fvq, h8p)
                        t.matmul(pq[0][:, :], bkv[:, 0:1],
                                 osqb[:, (h8p % 2) * 512:(h8p % 2 + 1) * 512],
                                 start=True, stop=True).then_inc(fss, 1)

                    for h in range(16):
                        t.wait_ge(hq[h % 2], 16 * (h // 2 + 1))
                        t.wait_ge(hk[h % 2], 16 * (h // 2 + 1))
                        t.wait_ge(hvt[h % 2], 16 * 32 * (h // 2 + 1))
                        for st in range(8):
                            h8 = h * 8 + st
                            for p in range(21):
                                if p == 4 and pend_ssq:
                                    emit_ssq(t)
                                if p < 16:
                                    PIX = h8 * 16 + p
                                    if PIX >= 3:
                                        eng, cnt = r_count(*divmod(PIX - 3, 16))
                                        t.wait_ge(eng, cnt)
                                    for jj in range(2):
                                        j = 2 * p + jj
                                        mm = t.matmul(psim[p % 3][:, jj * 512:(jj + 1) * 512],
                                                 Kh[:, (h % 2) * 4096 + j * 128:(h % 2) * 4096 + (j + 1) * 128],
                                                 Qh[:, (h % 2) * 4096 + st * 512:(h % 2) * 4096 + (st + 1) * 512],
                                                 start=True, stop=True)
                                    mm.then_inc(fsim, 1)
                                if p >= 5:
                                    po_ = p - 5
                                    eng, cnt = q_count(h8, po_)
                                    t.wait_ge(eng, cnt)
                                    if po_ == 0 and h8 >= 1:
                                        t.wait_ge(foev, h8)
                                    for jj in range(2):
                                        jo = 2 * po_ + jj
                                        t.matmul(pO[0][:, :],
                                                 Vtm[:, (h % 2) * 4096 + jo * 128:(h % 2) * 4096 + (jo + 1) * 128],
                                                 sqq[:, (po_ % 8) * 1024 + jj * 512:(po_ % 8) * 1024 + (jj + 1) * 512],
                                                 start=(jo == 0), stop=(jo == 31)).then_inc(fom, 1)
                            pend_ssq.append((h8, h, st))
                        while pend_ssq:
                            emit_ssq(t)

                @blk2.scalar
                def _(s_):
                    for h in range(16):
                        for st in range(8):
                            h8 = h * 8 + st
                            for p in range(16):
                                PIX = h8 * 16 + p
                                if RELU_ENG[p] == 's':
                                    s_.wait_ge(fsim, PIX + 1)
                                    if PIX >= 8:
                                        eng, cnt = q_count(*divmod(PIX - 8, 16))
                                        s_.wait_ge(eng, cnt)
                                    s_.activation(rb[:, (p % 8) * 1024:(p % 8 + 1) * 1024],
                                                  psim[p % 3][:, :], AF.Relu).then_inc(fres, 1)
                                if SQ_ENG[p] == 's':
                                    eng, cnt = r_count(h8, p)
                                    s_.wait_ge(eng, cnt)
                                    if PIX >= 8:
                                        s_.wait_ge(fom, 2 * PIX - 14)
                                    s_.activation(sqq[:, (p % 8) * 1024:(p % 8 + 1) * 1024],
                                                  rb[:, (p % 8) * 1024:(p % 8 + 1) * 1024],
                                                  AF.Square).then_inc(fqs, 1)
                            s_.wait_ge(fom, 32 * (h8 + 1))
                            ch = h // 2
                            pbase = (h % 2) * 64
                            s_.copy(O_all[pbase:pbase + 64, ch * 4096 + st * 512:ch * 4096 + (st + 1) * 512],
                                    pO[0][0:64, :]).then_inc(foev, 1)
                            if h8 >= 1:
                                hp, stp_ = divmod(h8 - 1, 8)
                                s_.wait_ge(fss, h8)
                                if hp >= 2 and stp_ == 0:
                                    s_.wait_ge(hs[hp % 2], 16 * (hp // 2))
                                s_.copy(ssqs[:, (hp % 2) * 4096 + stp_ * 512:(hp % 2) * 4096 + (stp_ + 1) * 512],
                                        pq[0][:, :]).then_inc(fvq, 1)
                    s_.wait_ge(fss, 128)
                    s_.copy(ssqs[:, (15 % 2) * 4096 + 7 * 512:(15 % 2) * 4096 + 8 * 512],
                            pq[0][:, :]).then_inc(fvq, 1)

                @blk2.vector
                def _(v):
                    v.memset(Qh[:, :], 0.0).then_inc(msA, 1)
                    v.memset(Kh[:, :], 0.0).then_inc(msA, 1)
                    v.memset(Vtm[:, :], 0.0).then_inc(msA, 1)
                    v.memset(osqb[:, :], 0.0).then_inc(msA, 1)
                    for h in range(16):
                        for st in range(8):
                            h8 = h * 8 + st
                            for p in range(16):
                                PIX = h8 * 16 + p
                                if RELU_ENG[p] == 'v':
                                    v.wait_ge(fsim, PIX + 1)
                                    if PIX >= 8:
                                        eng, cnt = q_count(*divmod(PIX - 8, 16))
                                        v.wait_ge(eng, cnt)
                                    v.tensor_scalar(rb[:, (p % 8) * 1024:(p % 8 + 1) * 1024],
                                                    psim[p % 3][:, :], 0.0, None,
                                                    ALU.max).then_inc(frev, 1)
                                if SQ_ENG[p] == 'v':
                                    eng, cnt = r_count(h8, p)
                                    v.wait_ge(eng, cnt)
                                    if PIX >= 8:
                                        v.wait_ge(fom, 2 * PIX - 14)
                                    v.tensor_tensor(sqq[:, (p % 8) * 1024:(p % 8 + 1) * 1024],
                                                    rb[:, (p % 8) * 1024:(p % 8 + 1) * 1024],
                                                    rb[:, (p % 8) * 1024:(p % 8 + 1) * 1024],
                                                    ALU.mult).then_inc(fvsq, 1)


                @blk2.gpsimd
                def _(g):
                    g.wait_ge(msA, 4)
                    for h in range(16):
                        for st in range(8):
                            h8 = h * 8 + st
                            for p in range(16):
                                if SQ_ENG[p] != 'g':
                                    continue
                                PIX = h8 * 16 + p
                                eng, cnt = r_count(h8, p)
                                g.wait_ge(eng, cnt)
                                if PIX >= 8:
                                    g.wait_ge(fom, 2 * PIX - 14)
                                g.tensor_tensor(sqq[:, (p % 8) * 1024:(p % 8 + 1) * 1024],
                                                rb[:, (p % 8) * 1024:(p % 8 + 1) * 1024],
                                                rb[:, (p % 8) * 1024:(p % 8 + 1) * 1024],
                                                ALU.mult).then_inc(fgsq, 1)
                            g.wait_ge(foev, h8 + 1)
                            if h8 >= 2:
                                g.wait_ge(fss, h8 - 1)
                            ch = h // 2
                            pbase = (h % 2) * 64
                            g.tensor_tensor(osqb[0:64, (h8 % 2) * 512:(h8 % 2 + 1) * 512],
                                            O_all[pbase:pbase + 64, ch * 4096 + st * 512:ch * 4096 + (st + 1) * 512],
                                            O_all[pbase:pbase + 64, ch * 4096 + st * 512:ch * 4096 + (st + 1) * 512],
                                            ALU.mult).then_inc(fvo, 1)

        lfs, sfb = _dsem(E, "alf"), _dsem(E, "afb")
        a3e, a3v = E.sem("a3e"), E.sem("a3v")
        lxs = _dsem(E, "axs")
        lfb = _dsem(E, "albf")
        lb2 = E.sem("alb2")
        stp = _dsem(E, "ast")
        wfm, wvn, wm, wv = E.sem("awf"), E.sem("awn"), E.sem("awm"), E.sem("awv")
        fbT = E.d["fbT"]
        with ExitStack() as stk3a:
            fs2 = stk3a.enter_context(E.sbuf("a_fs2", [2, 2 * 4096], F32))
            fL = stk3a.enter_context(E.sbuf("a_fL", [2, 4096], F32))
            fn = stk3a.enter_context(E.sbuf("a_fn", [2, 4096], F32))
            fiv = stk3a.enter_context(E.sbuf("a_fiv", [2, 4096], F32))
            fbc = stk3a.enter_context(E.sbuf("a_fbc", [2, 2 * 4096], BF16))
            with nc.Block() as blk3a:
                @blk3a.sync
                def _(sy):
                    for q in range(8):
                        if q >= 2:
                            sy.wait_ge(a3v, 2 * (q - 1))
                        sy.dma_start(fs2[:, (q % 2) * 4096:(q % 2 + 1) * 4096],
                                     ssqT[2 * q:2 * q + 2, :]).then_inc(lfs[q % 2], 16)
                        if q >= 1:
                            sy.wait_ge(a3v, 2 * q)
                            sy.dma_start(fbT[:, (q - 1) * 4096:q * 4096],
                                         fbc[:, ((q - 1) % 2) * 4096:((q - 1) % 2 + 1) * 4096]
                                         ).then_inc(sfb[(q - 1) % 2], 16)
                    sy.wait_ge(a3v, 16)
                    sy.dma_start(fbT[:, 7 * 4096:8 * 4096],
                                 fbc[:, (7 % 2) * 4096:(7 % 2 + 1) * 4096]).then_inc(sfb[7 % 2], 16)
                    sy.wait_ge(sfb[0], 16 * 4)
                    sy.wait_ge(sfb[1], 16 * 4)

                @blk3a.vector
                def _(v):
                    for q in range(8):
                        v.wait_ge(lfs[q % 2], 16 * (q // 2 + 1))
                        if q >= 1:
                            v.wait_ge(a3e, 4 * q)
                        v.tensor_scalar(fs2[:, (q % 2) * 4096:(q % 2 + 1) * 4096],
                                        fs2[:, (q % 2) * 4096:(q % 2 + 1) * 4096], 1e-18, None,
                                        ALU.max).then_inc(a3v, 1)
                        v.wait_ge(a3e, 4 * (q + 1))
                        if q >= 2:
                            v.wait_ge(sfb[q % 2], 16 * (q // 2))
                        v.tensor_tensor(fbc[:, (q % 2) * 4096:(q % 2 + 1) * 4096],
                                        fn[:, :], fiv[:, :], ALU.mult).then_inc(a3v, 1)

                @blk3a.scalar
                def _(s_):
                    for q in range(8):
                        s_.wait_ge(a3v, 2 * q + 1)
                        s_.activation(fL[:, :], fs2[:, (q % 2) * 4096:(q % 2 + 1) * 4096],
                                      AF.Ln).then_inc(a3e, 1)
                        s_.wait_ge(a3e, 4 * q + 1)
                        s_.activation(fn[:, :], fL[:, :], AF.Exp, scale=0.5).then_inc(a3e, 1)
                        s_.wait_ge(a3e, 4 * q + 2)
                        s_.activation(fiv[:, :], fL[:, :], AF.Exp, scale=-0.5).then_inc(a3e, 1)
                        s_.wait_ge(a3e, 4 * q + 3)
                        s_.activation(fn[:, :], fn[:, :], AF.Tanh).then_inc(a3e, 1)

        with ExitStack() as stk3:
            fbs = stk3.enter_context(E.sbuf("a_fbs", [2, 2 * 4096], BF16))
            blk2B = stk3.enter_context(E.sbuf("a_blk2", [2, 128], BF16))
            Onb = stk3.enter_context(E.sbuf("a_onb2", [128, 2 * 4096], BF16))
            xs = stk3.enter_context(E.sbuf("a_xs", [128, 2 * 4096], F32))
            od = stk3.enter_context(E.sbuf("a_od", [128, 2 * 4096], F32))
            pf = [stk3.enter_context(E.psum(f"a_pf{i}", [128, 512], F32)) for i in range(2)]
            pw = [stk3.enter_context(E.psum(f"a_pw{i}", [128, 512], F32)) for i in range(2)]
            with nc.Block() as blk3:
                @blk3.sync
                def _(sy):
                    sy.dma_start(blk2B[:, :], E.d["blk2"][:, :]).then_inc(lb2, 16)
                    for s in range(NST):
                        if s >= 2:
                            sy.wait_ge(wv, 8 * (s - 1))
                        sy.dma_start(xs[:, (s % 2) * 4096:(s % 2 + 1) * 4096].rearrange("p (c t) -> p c t", t=N),
                                     srcr[:, :, s * N:(s + 1) * N]).then_inc(lxs[s % 2], 16)
                        if s >= 2:
                            sy.wait_ge(wfm, 8 * (s - 1))
                        sy.dma_start(fbs[:, (s % 2) * 4096:(s % 2 + 1) * 4096].rearrange("h (ch t) -> h ch t", t=512),
                                     fbT.rearrange("h (ch t) -> h ch t", t=4096)[:, :, s * 512:(s + 1) * 512]
                                     ).then_inc(lfb[s % 2], 16)
                        if s >= 1:
                            sy.wait_ge(wv, 8 * s)
                            sy.dma_start(dstr[:, :, (s - 1) * N:s * N],
                                         od[:, ((s - 1) % 2) * 4096:((s - 1) % 2 + 1) * 4096].rearrange(
                                             "p (c t) -> p c t", t=N)).then_inc(stp[(s - 1) % 2], 16)
                    sy.wait_ge(wv, 8 * NST)
                    sy.dma_start(dstr[:, :, (NST - 1) * N:NST * N],
                                 od[:, ((NST - 1) % 2) * 4096:((NST - 1) % 2 + 1) * 4096].rearrange(
                                     "p (c t) -> p c t", t=N)).then_inc(stp[(NST - 1) % 2], 16)
                    sy.wait_ge(stp[0], 16 * (NST // 2))
                    sy.wait_ge(stp[1], 16 * (NST // 2))

                @blk3.vector
                def _(v):
                    for s in range(NST):
                        ba = (s % 2) * 4096
                        v.wait_ge(lxs[s % 2], 16 * (s // 2 + 1))
                        if s >= 2:
                            v.wait_ge(wm, 8 * (s - 1))
                        for ch in range(8):
                            k = 8 * s + ch
                            v.wait_ge(wfm, k + 1)
                            v.tensor_tensor(Onb[:, ba + ch * 512:ba + (ch + 1) * 512],
                                            O_all[:, ch * 4096 + s * 512:ch * 4096 + (s + 1) * 512],
                                            pf[k % 2][:, :], ALU.mult).then_inc(wvn, 1)
                        if s >= 2:
                            v.wait_ge(stp[s % 2], 16 * (s // 2))
                        for dch in range(8):
                            k = 8 * s + dch
                            v.wait_ge(wm, k + 1)
                            v.tensor_tensor(od[:, ba + dch * 512:ba + (dch + 1) * 512],
                                            pw[k % 2][:, :],
                                            xs[:, ba + dch * 512:ba + (dch + 1) * 512],
                                            ALU.add).then_inc(wv, 1)

                @blk3.tensor
                def _(t):
                    t.wait_ge(lb2, 16)
                    t.wait_ge(lwo[0], 16)
                    for s in range(NST):
                        t.wait_ge(lfb[s % 2], 16 * (s // 2 + 1))
                        for ch in range(8):
                            k = 8 * s + ch
                            if k >= 2:
                                t.wait_ge(wvn, k - 1)
                            t.matmul(pf[k % 2][:, :], blk2B[:, :],
                                     fbs[:, (s % 2) * 4096 + ch * 512:(s % 2) * 4096 + (ch + 1) * 512],
                                     start=True, stop=True).then_inc(wfm, 1)
                        t.wait_ge(wvn, 8 * (s + 1))
                        for dch in range(8):
                            k = 8 * s + dch
                            if k >= 2:
                                t.wait_ge(wv, k - 1)
                            for vch in range(8):
                                mm = t.matmul(pw[k % 2][:, :],
                                              WoB[:, vch * 1024 + dch * 128:vch * 1024 + (dch + 1) * 128],
                                              Onb[:, (s % 2) * 4096 + vch * 512:(s % 2) * 4096 + (vch + 1) * 512],
                                              start=(vch == 0), stop=(vch == 7))
                            mm.then_inc(wm, 1)
    return


FUSED = True

_PIPELINE = [
    ("entry", lambda E: emit_entry(E, "x0", "s0"), {"x0"}, {"s0"}),
    ("A", lambda E: emit_A(E, "s0", "t0"), {"s0"}, {"t0"}),
    ("lnA", lambda E: emit_ln(E, "t0", "s1"), {"t0"}, {"s1"}),
    ("B", lambda E: emit_B(E, "s1", "t0"), {"s1"}, {"t0"}),
    ("lnB", lambda E: emit_ln(E, "t0", "s0"), {"t0"}, {"s0"}),
    ("C", lambda E: emit_C(E, "s0", "t0"), {"s0"}, {"t0"}),
    ("lnC", lambda E: emit_ln(E, "t0", "s1"), {"t0"}, {"s1"}),
    ("D", lambda E: emit_D(E, "s1", "u0"), {"s1"}, {"u0"}),
    ("final", lambda E: emit_final(E, "u0", "s1", "out"), {"u0", "s1"}, {"out"}),
]

_PROG_CACHE = {}

def _get_prog(key, phases, ext_in, ext_out):
    if key not in _PROG_CACHE:
        _PROG_CACHE[key] = build_program(phases, ext_in, ext_out)
    return _PROG_CACHE[key]


def kernel(**inputs):
    ncores = 8
    W = host_weights(inputs)
    C = host_consts()
    shared = dict(W)
    shared.update(C)
    x0s = [np.ascontiguousarray(inputs["L0_attn_tokens"][c]) for c in range(ncores)]
    if FUSED:
        nc, used_in, _ = _get_prog("fused", [p[1] for p in _PIPELINE],
                                   set(shared) | {"x0"}, {"out"})
        in_maps = []
        for c in range(ncores):
            im = {k: shared[k] for k in used_in if k in shared}
            im["x0"] = x0s[c]
            in_maps.append(im)
        res = run_bass_kernel_spmd(nc, in_maps, core_ids=list(range(ncores)))
        return np.stack([res.results[c]["out"] for c in range(ncores)], axis=0)
    state = [{"x0": x0s[c]} for c in range(ncores)]
    for name, fn, sin, sout in _PIPELINE:
        nc, used_in, used_out = _get_prog(name, [fn], set(shared) | sin, sout)
        in_maps = []
        for c in range(ncores):
            im = {k: shared[k] for k in used_in if k in shared}
            for k in sin:
                if k in used_in:
                    im[k] = state[c][k]
            in_maps.append(im)
        res = run_bass_kernel_spmd(nc, in_maps, core_ids=list(range(ncores)))
        for c in range(ncores):
            for k in used_out:
                state[c][k] = np.ascontiguousarray(res.results[c][k])
    return np.stack([state[c]["out"] for c in range(ncores)], axis=0)
